# version 30
# baseline (speedup 1.0000x reference)
import numpy as np
import concourse.bass as bass
import concourse.mybir as mybir
from concourse.bass_utils import run_bass_kernel_spmd

F32 = mybir.dt.float32
BF16 = mybir.dt.bfloat16
I32 = mybir.dt.int32
AF = mybir.ActivationFunctionType
OP = mybir.AluOpType
AX = mybir.AxisListType

D = 1024
KT = 8
LN_EPS = 1e-5
LN2C = 0.6931471805599453

ENGS = ("tensor", "vector", "scalar", "gpsimd", "sync")


class Chan:
    __slots__ = ("sem", "val")


class Eng:
    def __init__(self, K, eng, name):
        self._K = K
        self._e = eng
        self._n = name
        self.last = None

    def __getattr__(self, a):
        f = getattr(self._e, a)
        if not callable(f):
            return f

        def g(*args, **kw):
            ins = f(*args, **kw)
            if a == "wait_ge":
                return ins
            if ins is not None and hasattr(ins, "then_inc"):
                self.last = ins
            return ins

        return g

    def mark(self):
        K = self._K
        c = K.ch[self._n]
        assert self.last is not None, "mark with no pending instruction"
        self.last.then_inc(c.sem, 1)
        c.val += 1
        self.last = None
        return c.val

    def chain(self):
        v = self.mark()
        self._e.wait_ge(self._K.ch[self._n].sem, v)

    def wait(self, name, val):
        if val > 0:
            self._e.wait_ge(self._K.ch[name].sem, val)

    def dma(self, name, out, in_, **kw):
        K = self._K
        c = K.ch[name]
        ins = self._e.dma_start(out=out, in_=in_, **kw)
        ins.then_inc(c.sem, 16)
        c.val += 16
        self.last = None
        return c.val


class K:
    def __init__(self, nc, blk):
        self.nc = nc
        self.blk = blk
        self.ch = {}
        for n in ENGS:
            self.chan(n)
        self.pend = {n: {} for n in ENGS}

    def chan(self, name):
        if name in self.ch:
            return self.ch[name]
        c = Chan()
        c.sem = self.nc.alloc_semaphore("ch_" + name)
        c.val = 0
        self.ch[name] = c
        return c

    def dep(self, cons, prods=None, at=None):
        for cn in cons:
            d = self.pend[cn]
            if prods:
                for p in prods:
                    v = self.ch[p].val
                    if v > d.get(p, 0):
                        d[p] = v
            if at:
                for p, v in at.items():
                    if v > d.get(p, 0):
                        d[p] = v

    def body(self, name, fn):
        def run(eng):
            ctx = Eng(self, eng, name)
            pend = self.pend[name]
            own = self.ch[name]
            if own.val > pend.get(name, 0):
                pend[name] = own.val
            if pend:
                for p, v in sorted(pend.items()):
                    if v > 0:
                        eng.wait_ge(self.ch[p].sem, v)
                pend.clear()
            fn(ctx)
            if ctx.last is not None:
                ctx.mark()
        getattr(self.blk, name)(run)

    def T(self, fn): self.body("tensor", fn)
    def V(self, fn): self.body("vector", fn)
    def S(self, fn): self.body("scalar", fn)
    def G(self, fn): self.body("gpsimd", fn)
    def Y(self, fn): self.body("sync", fn)

    def snap(self, name):
        return self.ch[name].val


def _ap(t):
    return t if isinstance(t, bass.AP) else t[:]


def bc16(t):
    return _ap(t).bitcast(BF16)


def bci32(t):
    return _ap(t).bitcast(I32)


def tmax(*toks):
    at = {}
    for t in toks:
        if t is None:
            continue
        if isinstance(t, dict):
            for p, v in t.items():
                at[p] = max(at.get(p, 0), v)
        else:
            at[t[0]] = max(at.get(t[0], 0), t[1])
    return at


class Ring:
    def __init__(self, k, slots, chans, issue_eng="sync", free=None):
        self.k = k
        self.slots = slots
        self.chans = chans
        self.eng = issue_eng
        self.free = list(free) if free else [None] * len(slots)

    def load(self, i, in_, width=None, gates=None):
        k = self.k
        i = i % len(self.slots)
        slot = self.slots[i] if width is None else self.slots[i][:, 0:width]
        ch = self.chans[i]
        at = tmax(self.free[i], *(gates or []))
        if at:
            k.dep([self.eng], at=at)
        def dma(y, slot=slot, ch=ch, in_=in_):
            y.dma(ch, slot, in_)
        k.body(self.eng, dma)
        return slot, (ch, k.snap(ch))

    def release(self, i, token):
        self.free[i % len(self.slots)] = token


def build(T=2048, debug=False):
    nc = bass.Bass("TRN2", target_bir_lowering=False)
    NSZ = 512
    NT = T // NSZ
    TH = T // 2
    NHH = TH // 512
    FT = min(512, T)
    NFT = T // FT
    CH = min(1024, T)
    NCH = T // CH

    def din(name, shape, dt=F32):
        return nc.dram_tensor(name, shape, dt, kind="ExternalInput").ap()

    XT = din("XT", [D, T])
    WZ = din("WZ", [D, D]); WH = din("WH", [D, D])
    WFCT = din("WFCT", [D, 4096]); WPRT = din("WPRT", [4096, D])
    AFC = din("AFC", [16, D]); BFCT = din("BFCT", [16, 4096])
    APR = din("APR", [16, 4096]); BPRT = din("BPRT", [16, D])
    WKT = din("WKT", [D, D]); WVT = din("WVT", [D, D])
    WRT = din("WRT", [D, D]); WOT = din("WOT", [D, D])
    FW1 = din("FW1", [D, D]); FA1 = din("FA1", [D, D]); FA2 = din("FA2", [D, D])
    SM_IN = din("SM", [128, 120])
    IDNI = din("IDNI", [128, 128])

    spk = "ExternalOutput" if debug else "Internal"
    SP1 = nc.dram_tensor("SP1", [D, T], F32, kind=spk).ap()
    SP2 = nc.dram_tensor("SP2", [D, T], F32, kind=spk).ap()
    SP3 = nc.dram_tensor("SP3", [D, T], F32, kind=spk).ap()
    YT = nc.dram_tensor("YT", [D, T], F32, kind="ExternalOutput").ap()

    RT = 2048
    X0 = nc.alloc_sbuf_tensor("X0", [128, 8 * RT], F32)
    H0 = nc.alloc_sbuf_tensor("H0", [128, 8 * RT], F32)
    XB0 = nc.alloc_sbuf_tensor("XB0", [128, 8 * RT], BF16)
    E0 = nc.alloc_sbuf_tensor("E0", [128, 4 * RT], F32)
    SC = nc.alloc_sbuf_tensor("SC", [128, 2560], F32)
    AT0 = nc.alloc_sbuf_tensor("AT0", [128, 1024], F32)
    SM = nc.alloc_sbuf_tensor("SMT", [128, 120], F32)
    DRV = nc.alloc_sbuf_tensor("DRV", [128, 40], F32)
    QS = nc.alloc_sbuf_tensor("QS", [128, 12], F32)
    MX = nc.alloc_sbuf_tensor("MX", [128, 16], F32)
    ONESB = nc.alloc_sbuf_tensor("ONESB", [128, 128], BF16)
    IDN = nc.alloc_sbuf_tensor("IDN", [128, 128], BF16)
    ASL = nc.alloc_sbuf_tensor("ASL", [16, 256], BF16)
    ABACK = nc.alloc_sbuf_tensor("ABACK", [128, 8], F32)
    CARA = nc.alloc_sbuf_tensor("CARA", [128, 8], F32)
    CARB = nc.alloc_sbuf_tensor("CARB", [128, 8], F32)

    Xr = [X0[:, i * RT:i * RT + T] for i in range(8)]
    Xf = [X0[:, i * RT:(i + 1) * RT] for i in range(8)]
    Hr = [H0[:, i * RT:i * RT + T] for i in range(8)]
    Hf = [H0[:, i * RT:(i + 1) * RT] for i in range(8)]
    XBF = [XB0[:, i * RT:i * RT + T] for i in range(8)]
    XBFf = [XB0[:, i * RT:(i + 1) * RT] for i in range(8)]
    EX = [E0[:, i * RT:i * RT + T] for i in range(4)]
    EXf = [E0[:, i * RT:(i + 1) * RT] for i in range(4)]
    SCR = [SC[:, i * 512:(i + 1) * 512] for i in range(5)]

    PB = [nc.alloc_psum_tensor(f"PB{i}", [128, NSZ], F32) for i in range(8)]

    with nc.Block() as blk:
        k = K(nc, blk)

        for cn in ("xin0", "xin1", "smalls", "spx01", "spx23", "spx45",
                   "sp2w0", "sp2w1", "sp2w2", "sp2w3",
                   "sp2w4", "sp2w5", "sp2w6", "sp2w7", "yt",
                   "at0", "at1", "rr0", "rr1", "rr2", "rr3",
                   "ob0", "ob1", "ob2", "ob3",
                   "qp0", "qp1", "ar0", "ar1", "l40", "l41", "l42"):
            k.chan(cn)
        for wn in ("wz", "wh", "afcb", "aprb", "bfc", "bpr", "idn",
                   "wk", "wv", "wr", "wo", "fw1", "fa1", "fa2"):
            k.chan("gd_" + wn)

        banks = [None] * 8

        def bank_wait(ctx, idxs):
            for i in idxs:
                if banks[i] is not None:
                    ctx.wait(*banks[i])

        WRHS = bc16(_ap(XB0))[:, 0:512]

        def warm():
            at = {"vector": k.ch["vector"].val}
            k.dep(["tensor"], at=at)
            def wf(t):
                t.ldweights(ONESB[:])
                t.ldweights(ONESB[:])
            k.T(wf)

        def gdma(wn, out, in_, gate=None, **kw):
            c = k.ch["gd_" + wn]
            at = tmax(*(gate or []))
            if at:
                k.dep(["gpsimd"], at=at)
            def f(g):
                ins = g.dma_start(out=out, in_=in_, **kw)
                ins.then_inc(c.sem, 16)
                g.last = None
            k.G(f)
            c.val += 16
            return ("gd_" + wn, c.val)

        def w8_in(hbm, half):
            return hbm[half * 512:(half + 1) * 512, :].rearrange(
                "(a p) m -> p a m", p=128)

        def w8_out(region):
            return bc16(region).rearrange("p (a m) -> p a m", a=4)

        def ln_stats(src, tokens, MB, RB, stage="v"):
            slot_tok = [None, None]
            si = 0
            tiles = [(kk, n) for kk in range(KT) for n in range(NT)]
            if stage == "sg":
                for p0 in range(0, len(tiles), 2):
                    pair = tiles[p0:p0 + 2]
                    at = tmax(slot_tok[0], slot_tok[1],
                              *(tokens(kk, n) for kk, n in pair
                                if tokens))
                    k.dep(["scalar"], at=at or None)
                    def cst(s, pair=pair, p0=p0):
                        for j, (kk, n) in enumerate(pair):
                            sb = bc16(SCR[(p0 + j) % 2])
                            s.activation(sb[:, 0:512], src(kk, n), AF.Copy)
                        s.mark()
                    k.S(cst)
                    k.dep(["gpsimd"], at=at or None)
                    def sqg(g, pair=pair, p0=p0):
                        for j, (kk, n) in enumerate(pair):
                            sb = bc16(SCR[(p0 + j) % 2])
                            g.tensor_mul(sb[:, 512:1024], src(kk, n),
                                         src(kk, n))
                        g.mark()
                    k.G(sqg)
                    k.dep(["tensor"], ["scalar", "gpsimd"])
                    def mms2(t, pair=pair, p0=p0):
                        for j, (kk, n) in enumerate(pair):
                            sb = bc16(SCR[(p0 + j) % 2])
                            if kk == 0:
                                bank_wait(t, [n, NT + n])
                            t.matmul(PB[n][:], ONESB[:], sb[:, 0:512],
                                     start=(kk == 0), stop=(kk == KT - 1))
                            t.matmul(PB[NT + n][:], ONESB[:],
                                     sb[:, 512:1024],
                                     start=(kk == 0), stop=(kk == KT - 1))
                            slot_tok[(p0 + j) % 2] = ("tensor", t.mark())
                    k.T(mms2)
                return _ln_rows(MB, RB)
            for kk in range(KT):
                for n in range(NT):
                    sb = bc16(SCR[si % 2])
                    tok = tokens(kk, n) if tokens else None
                    at = tmax(slot_tok[si % 2], tok)
                    k.dep(["vector"], at=at or None)
                    def cst(v, kk=kk, n=n, sb=sb):
                        v.tensor_copy(sb[:, 0:512], src(kk, n))
                        v.tensor_mul(sb[:, 512:1024], src(kk, n), src(kk, n))
                        v.mark()
                    k.V(cst)
                    k.dep(["tensor"], ["vector"])
                    def mms(t, kk=kk, n=n, sb=sb):
                        if kk == 0:
                            bank_wait(t, [n, NT + n])
                        t.matmul(PB[n][:], ONESB[:], sb[:, 0:512],
                                 start=(kk == 0), stop=(kk == KT - 1))
                        t.matmul(PB[NT + n][:], ONESB[:], sb[:, 512:1024],
                                 start=(kk == 0), stop=(kk == KT - 1))
                        slot_tok[si % 2] = ("tensor", t.mark())
                    k.T(mms)
                    si += 1
            return _ln_rows(MB, RB)

        def _ln_rows(MB, RB):
            k.dep(["vector"], ["tensor"])
            def rows_v(v):
                for n in range(NT):
                    sl = slice(n * NSZ, (n + 1) * NSZ)
                    v.tensor_scalar(MB[:, sl], PB[n][:], 1.0 / D, None, OP.mult)
                    v.tensor_scalar(RB[:, sl], PB[NT + n][:], 1.0 / D, LN_EPS,
                                    OP.mult, OP.add)
                v.chain()
                for n in range(NT):
                    sl = slice(n * NSZ, (n + 1) * NSZ)
                    v.tensor_mul(SCR[0][:], MB[:, sl], MB[:, sl])
                    v.chain()
                    v.tensor_sub(SCR[1][:], RB[:, sl], SCR[0][:])
                    v.chain()
                    v.reciprocal(RB[:, sl], SCR[1][:])
                v.chain()
            k.V(rows_v)
            vtok = ("vector", k.snap("vector"))
            for i in range(2 * NT):
                banks[i] = vtok
            k.dep(["scalar"], ["vector"])
            def rows_s(s):
                for n in range(NT):
                    sl = slice(n * NSZ, (n + 1) * NSZ)
                    s.activation(RB[:, sl], RB[:, sl], AF.Sqrt)
                    s.chain()
            k.S(rows_s)
            return ("scalar", k.snap("scalar"))

        def ln_stream_stats(row_hbm, data_gate, MB, RB, ring,
                            stage="v",
                            stage_free=(None, None), tiles=None):
            slot_mm = [stage_free[0], stage_free[1]]
            li = 0
            si = 0
            for kk in range(KT):
                for h2 in range(NCH):
                    slot, tok = ring.load(
                        li, row_hbm(kk)[:, h2 * CH:(h2 + 1) * CH],
                        gates=None if data_gate is None
                        else [data_gate(kk, h2)])
                    last_mm = None
                    for j in range(CH // 512):
                        n = h2 * (CH // 512) + j
                        sb = bc16(_ap(tiles[si % 2]) if tiles
                                  else SCR[si % 2])
                        src = slot[:, j * 512:(j + 1) * 512]
                        at = tmax(slot_mm[si % 2], tok)
                        if stage == "sg":
                            k.dep(["scalar"], at=at or None)
                            def csts(s, sb=sb, src=src):
                                s.activation(sb[:, 0:512], src, AF.Copy)
                                s.mark()
                            k.S(csts)
                            k.dep(["gpsimd"], at=at or None)
                            def sqg(g, sb=sb, src=src):
                                g.tensor_mul(sb[:, 512:1024], src, src)
                                g.mark()
                            k.G(sqg)
                            k.dep(["tensor"], ["scalar", "gpsimd"])
                        else:
                            k.dep(["vector"], at=at or None)
                            def cst(v, sb=sb, src=src):
                                v.tensor_copy(sb[:, 0:512], src)
                                v.tensor_mul(sb[:, 512:1024], src, src)
                                v.mark()
                            k.V(cst)
                            k.dep(["tensor"], ["vector"])
                        def mms(t, kk=kk, n=n, sb=sb, si=si):
                            if kk == 0:
                                bank_wait(t, [n, NT + n])
                            t.matmul(PB[n][:], ONESB[:], sb[:, 0:512],
                                     start=(kk == 0), stop=(kk == KT - 1))
                            t.matmul(PB[NT + n][:], ONESB[:], sb[:, 512:1024],
                                     start=(kk == 0), stop=(kk == KT - 1))
                            slot_mm[si % 2] = ("tensor", t.mark())
                        k.T(mms)
                        last_mm = ("tensor", k.snap("tensor"))
                        si += 1
                    ring.release(li, last_mm)
                    li += 1
            return _ln_rows(MB, RB)

        def p0(y):
            inv = XT.rearrange("(a p) t -> p a t", p=128)
            outv = X0[:].rearrange("p (a t) -> p a t", a=8)[:, :, 0:T]
            y.dma("xin0", outv[:, 0:4, :], inv[:, 0:4, :])
            y.dma("xin1", outv[:, 4:8, :], inv[:, 4:8, :])
            y.dma("smalls", SM[:], SM_IN)
        k.Y(p0)
        def p0v(v):
            v.memset(ONESB[:], 1.0)
            v.memset(ABACK[:], 0.0)
            v.memset(CARA[:], 0.0)
            v.memset(CARB[:], 0.0)
            v.memset(DRV[:, 8:9], -LN2C)
            v.mark()
        k.V(p0v)
        k.dep(["vector", "scalar"], at={"smalls": k.snap("smalls")})
        def p0d(v):
            v.tensor_scalar(DRV[:, 0:8], SM[:, 0:8], -1.0, None, OP.mult)
            v.mark()
        k.V(p0d)
        def p0s(s):
            s.activation(DRV[:, 16:24], SM[:, 56:64], AF.Exp)
            s.chain()
            s.activation(DRV[:, 24:32], SM[:, 48:56], AF.Exp)
            s.mark()
        k.S(p0s)
        def p0d2(s):
            s.activation(DRV[:, 32:40], DRV[:, 24:32], AF.Exp, scale=-1.0)
            s.mark()
        k.S(p0d2)
        smalls_v = ("vector", k.snap("vector"))
        smalls_s = ("scalar", k.snap("scalar"))

        gdma("idn", IDN[:], IDNI)
        gdma("wz", w8_out(Hf[4]), w8_in(WZ, 0))
        gdma("wz", w8_out(Hf[5]), w8_in(WZ, 1))
        gdma("wh", w8_out(Hf[6]), w8_in(WH, 0))
        gdma("wh", w8_out(Hf[7]), w8_in(WH, 1))

        def ln1_tok(kk, n):
            return ("xin0", 16) if kk < 4 else ("xin1", 16)

        k.dep(["tensor"], at=tmax(smalls_v))
        ln1_rows = ln_stats(lambda kk, n: Xr[kk][:, n * NSZ:(n + 1) * NSZ],
                            ln1_tok, EX[0], EX[1])
        stats_T = ("tensor", k.snap("tensor"))

        xhat_tok = {}
        k.dep(["vector", "gpsimd"], at=tmax(ln1_rows))
        def xh_v(v):
            for kk in (0, 1, 2, 4, 5, 6):
                for n in range(NT):
                    sl = slice(n * NSZ, (n + 1) * NSZ)
                    v.tensor_sub(SCR[2][:], Xr[kk][:, sl], EX[0][:, sl])
                    v.chain()
                    v.tensor_mul(XBF[kk][:, sl], SCR[2][:], EX[1][:, sl])
                    v.chain()
                xhat_tok[kk] = ("vector", k.ch["vector"].val)
        k.V(xh_v)
        def xh_g(g):
            for kk in (3, 7):
                for n in range(NT):
                    sl = slice(n * NSZ, (n + 1) * NSZ)
                    g.tensor_sub(SCR[3][:], Xr[kk][:, sl], EX[0][:, sl])
                    g.chain()
                    g.tensor_mul(XBF[kk][:, sl], SCR[3][:], EX[1][:, sl])
                    g.chain()
                xhat_tok[kk] = ("gpsimd", k.ch["gpsimd"].val)
        k.G(xh_g)
        xhat_all = [("vector", k.snap("vector")), ("gpsimd", k.snap("gpsimd"))]

        for pair, ch in ((0, "spx01"), (2, "spx23"), (4, "spx45")):
            k.dep(["scalar"], at={"xin0" if pair < 4 else "xin1": 16})
            def spl(y, pair=pair, ch=ch):
                for kk in (pair, pair + 1):
                    y.dma(ch, SP1[kk * 128:(kk + 1) * 128, 0:T], Xr[kk])
            k.body("scalar", spl)

        def wz_sl(kk, m):
            return bc16(Hf[4 + kk // 4])[:, (kk % 4) * D + m * 128:
                                         (kk % 4) * D + (m + 1) * 128]

        def wh_sl(kk, m):
            return bc16(Hf[6 + kk // 4])[:, (kk % 4) * D + m * 128:
                                         (kk % 4) * D + (m + 1) * 128]

        qring = Ring(k, [AT0[:], EXf[3][:, 0:1024]], ["at0", "at1"])
        QCH = 1024

        class Pass1:

            def __init__(self, hbm, nrows, ping, gate=None):
                self.hbm = hbm
                self.ping = ping
                self.chunks = [(kk, c) for kk in range(nrows)
                               for c in range(hbm.shape[1] // QCH)]
                self.ci = 0
                self.prev = None
                def mz(v, ping=ping):
                    v.memset(MX[:, ping:ping + 1], 0.0)
                    v.mark()
                k.V(mz)

            def emit(self, n):
                end = min(self.ci + n, len(self.chunks))
                while self.ci < end:
                    ci = self.ci
                    kk, c = self.chunks[ci]
                    ping = self.ping
                    slot, tok = qring.load(
                        ci, self.hbm[kk * 128:(kk + 1) * 128,
                                     c * QCH:(c + 1) * QCH])
                    k.dep(["vector"], at=tmax(tok, self.prev))
                    def red(v, ci=ci, slot=slot, ping=ping):
                        v.tensor_reduce(MX[:, 12:13], slot, AX.X,
                                        OP.max, apply_absolute_value=True)
                        v.chain()
                        v.tensor_max(
                            MX[:, ping + (ci + 1) % 2:ping + 1 + (ci + 1) % 2],
                            MX[:, ping + ci % 2:ping + 1 + ci % 2],
                            MX[:, 12:13])
                        qring.release(ci, ("vector", v.mark()))
                    k.V(red)
                    self.prev = ("vector", k.snap("vector"))
                    self.ci += 1

            def fin(self):
                assert self.ci == len(self.chunks)
                return self.ping + (self.ci % 2), self.prev

        def zrow(p): return EXf[2][:, p * 512:(p + 1) * 512]
        def grow(p): return EXf[2][:, (2 + p) * 512:(3 + p) * 512]
        rtmp = SCR[4]

        C_B, V_B = Xf[4], Xf[5]
        HD = [Hr[0], Hr[1], Hr[2], Hr[3], Xr[0], Xr[1], Xr[2], Xr[3]]
        cv_gate = tmax(("spx45", 32), xhat_tok[4], xhat_tok[5], stats_T)

        p1fc = Pass1(WFCT, KT, 8)
        p1pr = Pass1(WPRT, 32, 10)
        p1fc.emit(4)

        last_mm = {}
        scan_done = {}
        evs_done = {}
        evv_done = {}
        ti = 0
        for m in range(8):
            for n in range(NT):
                sl = slice(n * NSZ, (n + 1) * NSZ)
                at = {}
                if m == 0 and n == 0:
                    at = tmax(*xhat_all, ("gd_wz", 32), ("gd_wh", 32))
                k.dep(["tensor"], at=at or None)
                def mmzh(t, m=m, n=n, sl=sl):
                    bank_wait(t, [n, NT + n])
                    for kk in range(KT):
                        t.matmul(PB[n][:], wz_sl(kk, m), XBF[kk][:, sl],
                                 start=(kk == 0), stop=(kk == KT - 1))
                    for kk in range(KT):
                        t.matmul(PB[NT + n][:], wh_sl(kk, m), XBF[kk][:, sl],
                                 start=(kk == 0), stop=(kk == KT - 1))
                    last_mm[(m, n)] = ("tensor", t.mark())
                k.T(mmzh)
                p = ti % 2
                at = tmax(last_mm[(m, n)],
                          cv_gate if (m, n) == (0, 0) else None,
                          smalls_v if (m, n) == (0, 0) else None,
                          smalls_s if (m, n) == (0, 0) else None,
                          scan_done.get(m - 1),
                          evv_done.get(ti - 2))
                k.dep(["scalar"], at=at)
                def ev_s(s, m=m, n=n, sl=sl, p=p):
                    s.activation(C_B[:, sl], PB[n][:], AF.Sigmoid,
                                 bias=DRV[:, m:m + 1], scale=-1.0)
                    s.activation(zrow(p)[:], PB[n][:], AF.Sigmoid,
                                 bias=SM[:, m:m + 1], scale=1.0)
                    s.activation(grow(p)[:], PB[NT + n][:], AF.Sigmoid,
                                 bias=SM[:, 8 + m:9 + m], scale=1.0)
                    evs_done[(m, n)] = ("scalar", s.mark())
                k.S(ev_s)
                k.dep(["vector"], at=tmax(evs_done[(m, n)], last_mm[(m, n)]))
                def ev_v(v, m=m, n=n, sl=sl, p=p, ti=ti):
                    v.tensor_scalar(rtmp[:], PB[NT + n][:], SM[:, 8 + m:9 + m],
                                    0.0, OP.add, OP.max)
                    v.chain()
                    v.scalar_tensor_tensor(grow(p)[:], grow(p)[:], 0.5,
                                           rtmp[:], OP.min, OP.add)
                    v.chain()
                    v.tensor_mul(V_B[:, sl], zrow(p)[:], grow(p)[:])
                    tok = ("vector", v.mark())
                    banks[n] = tok
                    banks[NT + n] = tok
                    evv_done[ti] = tok
                k.V(ev_v)
                ti += 1
            at = tmax(evs_done[(m, NT - 1)])
            if m >= 4:
                at = tmax(at, ("spx%d%d" % (2 * ((m - 4) // 2),
                                            2 * ((m - 4) // 2) + 1), 32),
                          xhat_tok[m - 4], stats_T)
            k.dep(["vector"], at=at)
            def scan(v, m=m):
                v.tensor_tensor_scan(HD[m][:, 0:T], C_B[:, 0:T], V_B[:, 0:T],
                                     0.0, OP.mult, OP.add)
                scan_done[m] = ("vector", v.mark())
            k.V(scan)
            p1fc.emit(4)
            p1pr.emit(4)
        p1fc.emit(64)
        p1pr.emit(64)
        mloop_T = ("tensor", k.snap("tensor"))

        def pass1_fin(fcol, qcol, bank, vtok):
            k.dep(["gpsimd"], at=tmax(vtok))
            def tree(g, fcol=fcol):
                g.tensor_reduce(MX[0:1, 13:14], MX[:, fcol:fcol + 1],
                                AX.C, OP.max)
                g.mark()
            k.G(tree)
            ONE1R = SC[0:1, 2048:2176]
            def onev(v):
                v.memset(ONE1R, 1.0)
                v.mark()
            k.V(onev)
            k.dep(["tensor"], ["gpsimd", "vector"])
            def bcm(t, bank=bank):
                bank_wait(t, [bank])
                t.matmul(PB[bank][:, 0:1], ONE1R, MX[0:1, 13:14],
                         start=True, stop=True)
                t.mark()
            k.T(bcm)
            k.dep(["vector"], ["tensor"])
            def qsc(v, qcol=qcol, bank=bank):
                v.tensor_scalar(QS[:, qcol + 1:qcol + 2], PB[bank][:, 0:1],
                                1.0 / 127.0, None, OP.mult)
                v.chain()
                v.reciprocal(QS[:, qcol + 2:qcol + 3], QS[:, qcol + 1:qcol + 2])
                v.chain()
                v.tensor_scalar(QS[:, qcol + 3:qcol + 4],
                                QS[:, qcol + 1:qcol + 2], -1.0, None, OP.mult)
                v.chain()
                v.tensor_scalar(QS[:, qcol + 4:qcol + 5],
                                QS[:, qcol + 2:qcol + 3], -2.0, None, OP.mult)
                banks[bank] = ("vector", v.mark())
            k.V(qsc)
            return ("vector", k.snap("vector"))

        f_fc, f_fc_tok = p1fc.fin()
        qfc_tok = pass1_fin(f_fc, 0, 7, f_fc_tok)

        lnh_rows = ln_stats(lambda kk, n: HD[kk][:, n * NSZ:(n + 1) * NSZ],
                            lambda kk, n: scan_done[kk], EX[0], EX[1])

        xring = Ring(k, [EXf[2][:, i * 512:(i + 1) * 512] for i in range(4)],
                     ["rr0", "rr1", "rr2", "rr3"], issue_eng="sync",
                     free=[("vector", k.snap("vector"))] * 4)
        comb_done = {}
        ci = 0
        for m in range(8):
            for n in range(NT):
                sl = slice(n * NSZ, (n + 1) * NSZ)
                if m < 6:
                    sch = "spx%d%d" % (2 * (m // 2), 2 * (m // 2) + 1)
                    slot, tok = xring.load(ci, SP1[m * 128:(m + 1) * 128, sl],
                                           gates=[(sch, 32)])
                else:
                    slot, tok = Xr[m][:, sl], None
                eng = "vector" if ci % 4 != 3 else "gpsimd"
                if eng == "vector":
                    k.dep([eng], at=tmax(lnh_rows, scan_done[m], tok))
                    def cmb(e, m=m, sl=sl, slot=slot):
                        tmp = SCR[2]
                        e.tensor_sub(tmp[:], HD[m][:, sl], EX[0][:, sl])
                        e.chain()
                        e.tensor_mul(tmp[:], tmp[:], EX[1][:, sl])
                        e.chain()
                        e.tensor_scalar(tmp[:], tmp[:], SM[:, 16 + m:17 + m],
                                        SM[:, 24 + m:25 + m], OP.mult, OP.add)
                        e.chain()
                        e.tensor_add(tmp[:], tmp[:], XBF[m][:, sl])
                        e.chain()
                        e.tensor_add(Hr[m][:, sl], tmp[:], slot)
                        comb_done[(m, n)] = (eng, e.mark())
                    k.V(cmb)
                else:
                    k.dep([eng], at=tmax(lnh_rows, scan_done[m], tok))
                    def cmb_g1(e, m=m, sl=sl):
                        tmp = SCR[3]
                        e.tensor_sub(tmp[:], HD[m][:, sl], EX[0][:, sl])
                        e.chain()
                        e.tensor_mul(tmp[:], tmp[:], EX[1][:, sl])
                        e.mark()
                    k.G(cmb_g1)
                    k.dep(["vector"], ["gpsimd"])
                    def cmb_s(e, m=m):
                        e.tensor_scalar(SCR[3][:], SCR[3][:],
                                        SM[:, 16 + m:17 + m],
                                        SM[:, 24 + m:25 + m],
                                        OP.mult, OP.add)
                        e.mark()
                    k.V(cmb_s)
                    k.dep(["gpsimd"], ["vector"])
                    def cmb_g2(e, m=m, n=n, sl=sl, slot=slot):
                        tmp = SCR[3]
                        e.tensor_add(tmp[:], tmp[:], XBF[m][:, sl])
                        e.chain()
                        e.tensor_add(Hr[m][:, sl], tmp[:], slot)
                        comb_done[(m, n)] = ("gpsimd", e.mark())
                    k.G(cmb_g2)
                if m < 6:
                    xring.release(ci, comb_done[(m, n)])
                    ci += 1
                warm()
        comb_m = {m: tmax(*(comb_done[(m, n)] for n in range(NT)))
                  for m in range(8)}

        f_pr, f_pr_tok = p1pr.fin()
        qpr_tok = pass1_fin(f_pr, 6, 7, f_pr_tok)

        sp2_at = {}
        for m in range(8):
            k.dep(["scalar"], at=comb_m[m])
            def spl(y, m=m):
                y.dma("sp2w%d" % m, SP2[m * 128:(m + 1) * 128, 0:T], Hr[m])
            k.body("scalar", spl)
            sp2_at[m] = ("sp2w%d" % m, 16)

        ln2_rows = ln_stats(lambda kk, n: Hr[kk][:, n * NSZ:(n + 1) * NSZ],
                            lambda kk, n: comb_done[(kk, n)], EX[0], EX[1])
        xh2_tok = {}
        for kk in (0, 1, 2, 4, 5, 6):
            k.dep(["vector"], at=tmax(ln2_rows) if kk == 0 else None)
            def xh2_v(v, kk=kk):
                for n in range(NT):
                    sl = slice(n * NSZ, (n + 1) * NSZ)
                    v.tensor_sub(SCR[2][:], Hr[kk][:, sl], EX[0][:, sl])
                    v.chain()
                    v.tensor_mul(XBF[kk][:, sl], SCR[2][:], EX[1][:, sl])
                    v.chain()
                xh2_tok[kk] = ("vector", k.ch["vector"].val)
            k.V(xh2_v)
            warm()
        k.dep(["gpsimd"], at=tmax(ln2_rows))
        def xh2_g(g):
            for kk in (3, 7):
                for n in range(NT):
                    sl = slice(n * NSZ, (n + 1) * NSZ)
                    g.tensor_sub(SCR[3][:], Hr[kk][:, sl], EX[0][:, sl])
                    g.chain()
                    g.tensor_mul(XBF[kk][:, sl], SCR[3][:], EX[1][:, sl])
                    g.chain()
                xh2_tok[kk] = ("gpsimd", k.ch["gpsimd"].val)
        k.G(xh2_g)
        xh2_all = [("vector", k.snap("vector")), ("gpsimd", k.snap("gpsimd"))]

        gdma("bfc", bc16(EXf[2])[0:16, 0:4096], BFCT,
             gate=[comb_m[5], ("vector", k.snap("vector")),
                   ("gpsimd", k.snap("gpsimd"))],
             max_dma_last_dim=8192)
        BSF = bc16(EXf[2])[0:16, 0:4096]
        gdma("bpr", bc16(SC)[0:16, 4096:5120], BPRT,
             gate=[("vector", k.snap("vector")),
                   ("tensor", k.snap("tensor"))])
        BSP = bc16(SC)[0:16, 4096:5120]

        k.dep(["scalar"], at=tmax(("gd_bfc", 16), qfc_tok))
        def bsf_s(s):
            s.activation(BSF, BSF, AF.Copy, scale=QS[0:16, 4:5])
            s.mark()
        k.S(bsf_s)
        bsf_tok = ("scalar", k.snap("scalar"))
        k.dep(["scalar"], at=tmax(("gd_bpr", 16), qpr_tok))
        def bsp_s(s):
            s.activation(BSP, BSP, AF.Copy, scale=QS[0:16, 10:11])
            s.mark()
        k.S(bsp_s)
        bsp_tok = ("scalar", k.snap("scalar"))

        def wfc_sl(kk, m):
            return bc16(Xf[kk])[:, m * 128:(m + 1) * 128]

        def wpr_sl(kk):
            return bc16(Hf[kk // 4])[:, (kk % 4) * D:(kk % 4 + 1) * D]

        def hm_sl(kk):
            return bc16(_ap(E0))[:, kk * FT:(kk + 1) * FT]

        x_free = {0: comb_m[4], 1: comb_m[5], 2: comb_m[6], 3: comb_m[7],
                  4: scan_done[7], 5: scan_done[7],
                  6: comb_m[6], 7: comb_m[7]}

        IQF = bci32(EXf[3][:, 1024:2048])
        IQP = bci32(AT0[:, 0:1024])
        abk = [0]

        aring = Ring(k, [ASL[0:16, 0:128], ASL[0:16, 128:256]],
                     ["ar0", "ar1"], issue_eng="gpsimd")
        ali = [0]

        def merge_chunk(kk, c, dst, bs_ap, a_hbm, qv_tok, first, evac,
                        bs_tok, merged):
            asl, atok = aring.load(ali[0],
                                   a_hbm[0:16, kk * 128:(kk + 1) * 128])
            for h in range(2):
                b = abk[0] % 4
                abk[0] += 1
                dgr = dst(kk, c)[:, h * 512:(h + 1) * 512]
                bgr = bs_ap[0:16, c * 1024 + h * 512:c * 1024 + (h + 1) * 512]
                at = tmax(qv_tok, atok,
                          *([bs_tok, ("gd_idn", 16)] if first[0] else []))
                first[0] = False
                if evac == "s":
                    k.dep(["tensor"], at=at)
                    def abm(t, b=b, dgr=dgr, bgr=bgr, asl=asl):
                        bank_wait(t, [b])
                        t.matmul(PB[b][:], asl, bgr, start=True, stop=False)
                        t.matmul(PB[b][:], IDN[:], dgr, start=False,
                                 stop=True)
                        t.mark()
                    k.T(abm)
                    k.dep(["scalar"], ["tensor"])
                    def mev(s, b=b, dgr=dgr):
                        s.activation(dgr, PB[b][:], AF.Copy)
                        banks[b] = ("scalar", s.mark())
                    k.S(mev)
                    merged[(kk, 2 * c + h)] = ("scalar", k.snap("scalar"))
                else:
                    k.dep(["tensor"], at=at)
                    def abm2(t, b=b, bgr=bgr, asl=asl):
                        bank_wait(t, [b])
                        t.matmul(PB[b][:], asl, bgr, start=True, stop=True)
                        t.mark()
                    k.T(abm2)
                    k.dep(["vector"], ["tensor"])
                    def mev2(v, b=b, dgr=dgr):
                        v.scalar_tensor_tensor(dgr, PB[b][:], 1.0,
                                               dgr, OP.mult, OP.add)
                        banks[b] = ("vector", v.mark())
                    k.V(mev2)
                    merged[(kk, 2 * c + h)] = ("vector", k.snap("vector"))
            aring.release(ali[0], ("tensor", k.snap("tensor")))
            ali[0] += 1

        def pass2(hbm, order, dst, gates, qtok, qcol, ring, ring_i0, iq,
                  deferred):
            for i, (kk, c) in enumerate(order):
                slot, tok = ring.load(ring_i0 + i,
                                      hbm[kk * 128:(kk + 1) * 128,
                                          c * QCH:(c + 1) * QCH])
                k.dep(["vector"], at=tmax(tok, qtok, *gates(kk)))
                def qv(e, slot=slot, kk=kk, c=c, iq=iq, qcol=qcol, dst=dst):
                    e.tensor_scalar(iq, slot, QS[:, qcol + 2:qcol + 3],
                                    None, OP.mult)
                    e.chain()
                    e.scalar_tensor_tensor(slot, slot,
                                           QS[:, qcol + 2:qcol + 3], iq,
                                           OP.mult, OP.subtract)
                    e.chain()
                    e.scalar_tensor_tensor(iq, slot, 0.5, iq,
                                           OP.is_ge, OP.add)
                    e.chain()
                    e.scalar_tensor_tensor(dst(kk, c), slot, -0.5, iq,
                                           OP.is_le, OP.subtract)
                    e.mark()
                k.V(qv)
                qv_tok = ("vector", k.snap("vector"))
                ring.release(ring_i0 + i, qv_tok)
                deferred.append((kk, c, qv_tok))
                warm()

        fcm_ready = {}
        fc_first = [True]
        fc_dst = lambda kk, c: bc16(Xf[kk])[:, c * 1024:(c + 1) * 1024]
        fcq = []
        pass2(WFCT, [(kk, c) for c in range(4) for kk in range(8)],
              fc_dst, lambda kk: [x_free[kk]], qfc_tok, 0, qring, 0, IQF, fcq)
        for (kk, c, qv_tok) in fcq:
            merge_chunk(kk, c, fc_dst, BSF, AFC, qv_tok, fc_first, "s",
                        bsf_tok, fcm_ready)

        ring2_free = tmax(ln2_rows, *xh2_all)
        ring2 = Ring(k, [SC[:, 0:1024], SC[:, 1024:2048]], ["qp0", "qp1"],
                     issue_eng="sync", free=[ring2_free, ring2_free])

        prm = {}
        pr_dst = lambda kk, c: wpr_sl(kk)
        prq = []
        pass2(WPRT, [(kk, 0) for kk in range(32)],
              pr_dst, lambda kk: [sp2_at[kk // 4], xh2_tok[kk // 4]],
              qpr_tok, 6, ring2, 0, IQP, prq)
        prq_done = ("vector", k.snap("vector"))

        oslots = [AT0[:, 0:FT], AT0[:, FT:2 * FT]]
        ob_free = [tmax(prq_done), tmax(prq_done)]

        sp3_done = {}
        ri1 = 32
        oi = 0
        for tt in range(NFT):
            tsl = slice(tt * FT, (tt + 1) * FT)
            efc_toks = []
            for g in range(8):
                bs = (g % 2) * 4
                at = tmax(*((fcm_ready[(kk, g)] for kk in range(8))
                            if tt == 0 else ()),
                          *(xh2_all if (tt == 0 and g == 0) else ()))
                k.dep(["tensor"], at=at or None)
                def mfc(t, g=g, bs=bs, tsl=tsl):
                    bank_wait(t, [bs + i for i in range(4)])
                    for gi in range(4):
                        m = g * 4 + gi
                        for kk in range(KT):
                            t.matmul(PB[bs + gi][:, 0:FT], wfc_sl(kk, m),
                                     XBF[kk][:, tsl], start=(kk == 0),
                                     stop=(kk == KT - 1))
                    t.mark()
                k.T(mfc)
                k.dep(["scalar"], ["tensor"])
                def efc(s, g=g, bs=bs):
                    for gi in range(4):
                        m = g * 4 + gi
                        s.activation(hm_sl(m), PB[bs + gi][:, 0:FT],
                                     AF.Gelu, bias=SM[:, 88 + m:89 + m],
                                     scale=QS[:, 3:4])
                    tok = ("scalar", s.mark())
                    for i in range(4):
                        banks[bs + i] = tok
                k.S(efc)
                efc_toks.append(("scalar", k.snap("scalar")))
            if tt == 0:
                pr_first = [True]
                for (kk, c, qv_tok) in prq:
                    merge_chunk(kk, c, pr_dst, BSP, APR, qv_tok,
                                pr_first, "v", bsp_tok, prm)
                prm_toks = list(prm.values())
            for g in range(2):
                bs = g * 4
                at = tmax(*(prm_toks if tt == 0 else []), *efc_toks)
                k.dep(["tensor"], at=at)
                def mpr(t, g=g, bs=bs):
                    bank_wait(t, [bs + i for i in range(4)])
                    for gi in range(4):
                        mo = g * 4 + gi
                        for kk in range(32):
                            t.matmul(PB[bs + gi][:, 0:FT],
                                     wpr_sl(kk)[:, mo * 128:(mo + 1) * 128],
                                     hm_sl(kk), start=(kk == 0),
                                     stop=(kk == 31))
                    t.mark()
                k.T(mpr)
                mpr_tok = ("tensor", k.snap("tensor"))
                for gi in range(4):
                    mo = g * 4 + gi
                    slot, tok = ring2.load(ri1,
                                           SP2[mo * 128:(mo + 1) * 128, tsl],
                                           width=FT, gates=[sp2_at[mo]])
                    osl = oslots[oi % 2]
                    k.dep(["vector"], at=tmax(mpr_tok, tok, ob_free[oi % 2]))
                    def epr(v, mo=mo, bs=bs, gi=gi, slot=slot, osl=osl):
                        v.tensor_scalar(osl, PB[bs + gi][:, 0:FT],
                                        QS[:, 9:10], SM[:, 32 + mo:33 + mo],
                                        OP.mult, OP.add)
                        v.chain()
                        v.tensor_add(osl, osl, slot)
                        banks[bs + gi] = ("vector", v.mark())
                    k.V(epr)
                    vtok = ("vector", k.snap("vector"))
                    ring2.release(ri1, vtok)
                    ri1 += 1
                    obch = "ob%d" % (oi % 2)
                    k.dep(["sync"], at=tmax(vtok))
                    def owr(y, mo=mo, tsl=tsl, osl=osl, obch=obch):
                        y.dma(obch, SP3[mo * 128:(mo + 1) * 128, tsl], osl)
                    k.Y(owr)
                    ob_free[oi % 2] = (obch, k.snap(obch))
                    sp3_done[(mo, tt)] = (obch, k.snap(obch))
                    oi += 1

        phaseB_T = ("tensor", k.snap("tensor"))
        obf = tmax(phaseB_T, ("ob0", k.snap("ob0")), ("ob1", k.snap("ob1")))

        gdma("wk", w8_out(Hf[0]), w8_in(WKT, 0), gate=[phaseB_T])
        gdma("wk", w8_out(Hf[1]), w8_in(WKT, 1))
        gdma("wv", w8_out(Hf[2]), w8_in(WVT, 0))
        gdma("wv", w8_out(Hf[3]), w8_in(WVT, 1))
        gdma("wr", w8_out(Hf[4]), w8_in(WRT, 0))
        gdma("wr", w8_out(Hf[5]), w8_in(WRT, 1))
        gdma("wo", w8_out(Hf[6]), w8_in(WOT, 0))
        gdma("wo", w8_out(Hf[7]), w8_in(WOT, 1))

        def wk_sl(kk, m):
            return bc16(Hf[kk // 4])[:, (kk % 4) * D + m * 128:
                                     (kk % 4) * D + (m + 1) * 128]

        def wv_sl(kk, m):
            return bc16(Hf[2 + kk // 4])[:, (kk % 4) * D + m * 128:
                                         (kk % 4) * D + (m + 1) * 128]

        def wr_sl(kk, m):
            return bc16(Hf[4 + kk // 4])[:, (kk % 4) * D + m * 128:
                                         (kk % 4) * D + (m + 1) * 128]

        def wo_sl(kk, m):
            return bc16(Hf[6 + kk // 4])[:, (kk % 4) * D + m * 128:
                                         (kk % 4) * D + (m + 1) * 128]

        qring.free = [obf, obf]
        qpf = tmax(("qp0", k.snap("qp0")), ("qp1", k.snap("qp1")),
                   ("vector", k.snap("vector")))
        ln3_rows = ln_stream_stats(
            lambda kk: SP3[kk * 128:(kk + 1) * 128, :],
            lambda kk, h2: tmax(*(sp3_done[(kk, tt)]
                                  for tt in range(h2 * (CH // FT),
                                                  (h2 + 1) * (CH // FT)))),
            EX[0], EX[1], qring,
            stage_free=(qpf, qpf))

        ALPHA = Xf[4][:, 0:TH]
        BETA = Xf[4][:, TH:2 * TH]
        EKBp = [Xf[5][:, 0:TH], Xf[5][:, TH:2 * TH]]
        EKUp = [Xf[7][:, 0:TH], Xf[7][:, TH:2 * TH]]
        EWB = SC[:, 0:TH]
        EKV = Xf[6][:, 0:TH]
        NUM = Xf[6][:, TH:2 * TH]
        DEN = SC[:, 1024:1024 + TH]
        TBF = bc16(SC)[:, 4096:4096 + TH]
        XHB = [EXf[2][:, 0:1024], EXf[2][:, 1024:2048]]

        def xk_sl(kk):
            return bc16(Xf[kk // 4])[:, (kk % 4) * TH:(kk % 4 + 1) * TH]

        def xv_sl(kk):
            return bc16(Xf[2 + kk // 4])[:, (kk % 4) * TH:(kk % 4 + 1) * TH]

        def xr_sl(kk):
            return XBFf[kk // 2][:, (kk % 2) * TH:(kk % 2 + 1) * TH]

        def rw_sl(th, kk):
            if th == 0:
                return XBFf[4 + kk // 2][:, (kk % 2) * TH:(kk % 2 + 1) * TH]
            return bc16(EXf[2 + kk // 4])[:, (kk % 4) * TH:(kk % 4 + 1) * TH]

        xhb_free = [None, None]
        kv_done = {}
        rh_done = {}
        rw_done = {}
        v1_done = {}
        v2_done = {}
        g1_done = {}
        for th in range(2):
            hsl = slice(th * TH, (th + 1) * TH)
            for kk in range(KT):
                slot, tok = qring.load(kk % 2,
                                       SP3[kk * 128:(kk + 1) * 128, hsl],
                                       width=TH)
                xb = XHB[kk % 2][:, 0:TH]
                k.dep(["gpsimd"], at=tmax(tok, ln3_rows, xhb_free[kk % 2]))
                def bl_g(g, kk=kk, hsl=hsl, slot=slot, xb=xb):
                    g.tensor_sub(xb, slot, EX[0][:, hsl])
                    g.chain()
                    g.tensor_mul(xb, xb, EX[1][:, hsl])
                    g.mark()
                k.G(bl_g)
                at = tmax(tok, ("gpsimd", k.snap("gpsimd")),
                          phaseB_T if th == 0 else rh_done[(0, 7)])
                k.dep(["vector"], at=at)
                def bl_v(v, kk=kk, th=th, hsl=hsl, slot=slot, xb=xb):
                    v.tensor_sub(slot[:, 1:TH], xb[:, 1:TH], xb[:, 0:TH - 1])
                    if th == 0:
                        v.tensor_copy(slot[:, 0:1], xb[:, 0:1])
                    else:
                        v.tensor_sub(slot[:, 0:1], xb[:, 0:1],
                                     ABACK[:, kk:kk + 1])
                    v.chain()
                    for dst, col in ((xk_sl(kk), 64), (xv_sl(kk), 72),
                                     (xr_sl(kk), 80)):
                        v.scalar_tensor_tensor(dst[:, 1:TH], slot[:, 1:TH],
                                               SM[:, col + kk:col + kk + 1],
                                               xb[:, 0:TH - 1],
                                               OP.mult, OP.add)
                        if th == 0:
                            v.tensor_scalar(dst[:, 0:1], slot[:, 0:1],
                                            SM[:, col + kk:col + kk + 1],
                                            None, OP.mult)
                        else:
                            v.scalar_tensor_tensor(
                                dst[:, 0:1], slot[:, 0:1],
                                SM[:, col + kk:col + kk + 1],
                                ABACK[:, kk:kk + 1], OP.mult, OP.add)
                    if th == 0:
                        v.chain()
                        v.tensor_copy(ABACK[:, kk:kk + 1], xb[:, TH - 1:TH])
                    v.mark()
                k.V(bl_v)
                btok = ("vector", k.snap("vector"))
                qring.release(kk % 2, btok)
                xhb_free[kk % 2] = btok
                warm()
            blends_done = ("vector", k.snap("vector"))

            for m in range(8):
                a = (m % 2) * 4
                b = 4 - a
                at = tmax(blends_done,
                          *((("gd_wk", 32), ("gd_wv", 32), ("gd_wr", 32))
                            if (th, m) == (0, 0) else ()))
                k.dep(["tensor"], at=at)
                def mkv(t, m=m, a=a):
                    bank_wait(t, [a + i for i in range(2 * NHH)])
                    for nh in range(NHH):
                        s2 = slice(nh * 512, (nh + 1) * 512)
                        for kk in range(KT):
                            t.matmul(PB[a + nh][:], wk_sl(kk, m),
                                     xk_sl(kk)[:, s2],
                                     start=(kk == 0), stop=(kk == KT - 1))
                        for kk in range(KT):
                            t.matmul(PB[a + NHH + nh][:], wv_sl(kk, m),
                                     xv_sl(kk)[:, s2],
                                     start=(kk == 0), stop=(kk == KT - 1))
                    kv_done[(th, m)] = ("tensor", t.mark())
                k.T(mkv)
                def mr(t, m=m, b=b):
                    bank_wait(t, [b + i for i in range(NHH)])
                    for nh in range(NHH):
                        s2 = slice(nh * 512, (nh + 1) * 512)
                        for kk in range(KT):
                            t.matmul(PB[b + nh][:], wr_sl(kk, m),
                                     xr_sl(kk)[:, s2],
                                     start=(kk == 0), stop=(kk == KT - 1))
                    rh_done[(th, m)] = ("tensor", t.mark())
                k.T(mr)
                p = m % 2
                mi = 2 * th + (m // 2) * 0
                def prev(dct, back):
                    mm2 = m - back
                    if mm2 >= 0:
                        return dct.get((th, mm2))
                    return dct.get((th - 1, 8 + mm2)) if th > 0 else None
                k.dep(["scalar"], at=tmax(kv_done[(th, m)], prev(v1_done, 2)))
                def ev1a(s, m=m, a=a, p=p):
                    for nh in range(NHH):
                        s2 = slice(nh * 512, (nh + 1) * 512)
                        s.activation(EKBp[p][:, s2], PB[a + nh][:], AF.Exp)
                    s.mark()
                k.S(ev1a)
                ek_tok = ("scalar", k.snap("scalar"))
                def ev1b(s, m=m, a=a, p=p):
                    for nh in range(NHH):
                        s2 = slice(nh * 512, (nh + 1) * 512)
                        s.activation(EKUp[p][:, s2], PB[a + nh][:], AF.Exp,
                                     bias=SM[:, 56 + m:57 + m])
                    tok = ("scalar", s.mark())
                    for i in range(NHH):
                        banks[a + i] = tok
                k.S(ev1b)
                warm()
                eku_tok = ("scalar", k.snap("scalar"))
                k.dep(["scalar"], at=tmax(rh_done[(th, m)], prev(v1_done, 1)))
                def ev2(s, m=m, b=b):
                    for nh in range(NHH):
                        s2 = slice(nh * 512, (nh + 1) * 512)
                        s.activation(TBF[:, s2], PB[b + nh][:], AF.Exp,
                                     scale=-1.0)
                    tok = ("scalar", s.mark())
                    for i in range(NHH):
                        banks[b + i] = tok
                k.S(ev2)
                tb_tok = ("scalar", k.snap("scalar"))
                k.dep(["vector"], at=tmax(ek_tok, tb_tok, prev(g1_done, 1)))
                def v1(v, m=m, th=th, a=a, p=p):
                    v.tensor_scalar(EWB[:], EX[0][:, 0:TH], 0.0,
                                    DRV[:, 32 + m:33 + m], OP.mult, OP.add)
                    for nh in range(NHH):
                        s2 = slice(nh * 512, (nh + 1) * 512)
                        v.tensor_mul(EKV[:, s2], EKBp[p][:, s2],
                                     PB[a + NHH + nh][:])
                        banks[a + NHH + nh] = ("vector", v.mark())
                    v.wait("vector", k.ch["vector"].val)
                    ia = 0.0 if th == 0 else CARA[:, m:m + 1]
                    v.tensor_tensor_scan(ALPHA, EWB, EKV, ia,
                                         OP.mult, OP.add)
                    ib = 0.0 if th == 0 else CARB[:, m:m + 1]
                    v.tensor_tensor_scan(BETA, EWB, EKBp[p], ib,
                                         OP.mult, OP.add)
                    v.chain()
                    if th == 0:
                        v.tensor_copy(CARA[:, m:m + 1], ALPHA[:, TH - 1:TH])
                        v.tensor_copy(CARB[:, m:m + 1], BETA[:, TH - 1:TH])
                    eu = DRV[:, 16 + m:17 + m]
                    v.scalar_tensor_tensor(NUM[:, 1:TH], EKV[:, 1:TH], eu,
                                           ALPHA[:, 0:TH - 1], OP.mult, OP.add)
                    if th == 0:
                        v.tensor_scalar(NUM[:, 0:1], EKV[:, 0:1], eu,
                                        None, OP.mult)
                    else:
                        v.scalar_tensor_tensor(NUM[:, 0:1], EKV[:, 0:1], eu,
                                               CARA[:, m:m + 1],
                                               OP.mult, OP.add)
                    v.wait("scalar", eku_tok[1])
                    v.tensor_add(DEN[:, 1:TH], EKUp[p][:, 1:TH],
                                 BETA[:, 0:TH - 1])
                    if th == 0:
                        v.tensor_copy(DEN[:, 0:1], EKUp[p][:, 0:1])
                    else:
                        v.tensor_add(DEN[:, 0:1], EKUp[p][:, 0:1],
                                     CARB[:, m:m + 1])
                    v.chain()
                    v.scalar_tensor_tensor(EKV[:], TBF, 1.0, DEN,
                                           OP.add, OP.mult)
                    v1_done[(th, m)] = ("vector", v.mark())
                k.V(v1)
                warm()
                k.dep(["scalar"], at=tmax(v1_done[(th, m)]))
                def sdiv(s):
                    s.activation(EWB[:], EKV[:], AF.Ln)
                    s.chain()
                    s.activation(DEN[:], EWB[:], AF.Exp, scale=-1.0)
                    s.mark()
                k.S(sdiv)
                rd_tok = ("scalar", k.snap("scalar"))
                at = tmax(rd_tok)
                if th == 1:
                    at = tmax(rd_tok, xhb_free[0], xhb_free[1],
                              ("at1", k.snap("at1")))
                k.dep(["vector"], at=at)
                def v2(v, th=th, m=m):
                    v.tensor_mul(rw_sl(th, m), NUM, DEN)
                    tok = ("vector", v.mark())
                    v2_done[(th, m)] = tok
                    g1_done[(th, m)] = tok
                    rw_done[(th, m)] = tok
                k.V(v2)
                warm()

        mloop_v = ("vector", k.snap("vector"))
        mloop_s = ("scalar", k.snap("scalar"))
        mloop_g = ("gpsimd", k.snap("gpsimd"))

        x2ring = Ring(k, [SCR[i][:, 0:512] for i in range(4)],
                      ["rr0", "rr1", "rr2", "rr3"], issue_eng="sync",
                      free=[mloop_s, mloop_s, mloop_g, mloop_g])
        wo_oslots = [Xf[4][:, i * 512:(i + 1) * 512] for i in range(4)]
        wo_ofree = [mloop_v, mloop_v, mloop_v, mloop_v]

        ri2 = 0
        sp1_done = {}
        for th in range(2):
            for mo in range(8):
                a = (mo % 2) * 4
                at = tmax(*((("gd_wo", 32),) if (th, mo) == (0, 0) else ()),
                          *((rw_done[(th, kk)] for kk in range(8))
                            if mo == 0 else ()))
                k.dep(["tensor"], at=at or None)
                def mwo(t, th=th, mo=mo, a=a):
                    bank_wait(t, [a + i for i in range(NHH)])
                    for nh in range(NHH):
                        s2 = slice(nh * 512, (nh + 1) * 512)
                        for kk in range(KT):
                            t.matmul(PB[a + nh][:], wo_sl(kk, mo),
                                     rw_sl(th, kk)[:, s2],
                                     start=(kk == 0), stop=(kk == KT - 1))
                    t.mark()
                k.T(mwo)
                wo_tok = ("tensor", k.snap("tensor"))
                for nh in range(NHH):
                    s2a = slice(th * TH + nh * 512, th * TH + (nh + 1) * 512)
                    tts = range((th * TH + nh * 512) // FT,
                                (th * TH + (nh + 1) * 512) // FT)
                    slot, tok = x2ring.load(
                        ri2, SP3[mo * 128:(mo + 1) * 128, s2a],
                        gates=[tmax(*(sp3_done[(mo, tt)] for tt in tts))])
                    osl = wo_oslots[ri2 % 4]
                    k.dep(["vector"], at=tmax(wo_tok, tok,
                                              wo_ofree[ri2 % 4]))
                    def awo(v, a=a, nh=nh, osl=osl, slot=slot):
                        v.tensor_add(osl, PB[a + nh][:], slot)
                        banks[a + nh] = ("vector", v.mark())
                    k.V(awo)
                    vtok = ("vector", k.snap("vector"))
                    x2ring.release(ri2, vtok)
                    obch = "ob%d" % (ri2 % 4)
                    k.dep(["sync"], at=tmax(vtok))
                    def owr(y, mo=mo, s2a=s2a, osl=osl, obch=obch):
                        y.dma(obch, SP1[mo * 128:(mo + 1) * 128, s2a], osl)
                    k.Y(owr)
                    wo_ofree[ri2 % 4] = (obch, k.snap(obch))
                    sp1_done[(mo, th, nh)] = (obch, k.snap(obch))
                    ri2 += 1
            if th == 0:
                wo_th0 = ("tensor", k.snap("tensor"))

        phaseC_T = ("tensor", k.snap("tensor"))

        gdma("fw1", w8_out(Xf[0]), w8_in(FW1, 0), gate=[phaseC_T])
        gdma("fw1", w8_out(Xf[1]), w8_in(FW1, 1))
        gdma("fa1", w8_out(Xf[2]), w8_in(FA1, 0))
        gdma("fa1", w8_out(Xf[3]), w8_in(FA1, 1))
        gdma("fa2", w8_out(Xf[5]), w8_in(FA2, 0),
             gate=[mloop_v, mloop_g, ("vector", k.snap("vector"))])
        gdma("fa2", w8_out(Xf[6]), w8_in(FA2, 1))

        def fw1_sl(kk, m):
            return bc16(Xf[kk // 4])[:, (kk % 4) * D + m * 128:
                                     (kk % 4) * D + (m + 1) * 128]

        def fa1_sl(kk, m):
            return bc16(Xf[2 + kk // 4])[:, (kk % 4) * D + m * 128:
                                         (kk % 4) * D + (m + 1) * 128]

        def fa2_sl(kk, m):
            return bc16(Xf[5 + kk // 4])[:, (kk % 4) * D + m * 128:
                                         (kk % 4) * D + (m + 1) * 128]

        qring.free = [phaseC_T, phaseC_T]

        def ln4_gate(kk, h2):
            toks = []
            for j in range(CH // 512):
                pos = h2 * CH + j * 512
                toks.append(sp1_done[(kk, pos // TH, (pos % TH) // 512)])
            return tmax(*toks)

        XBF32 = _ap(XB0).bitcast(F32)
        ln4ring = Ring(k, [XBF32[:, 4096:5120], XBF32[:, 5120:6144],
                           XBF32[:, 6144:7168]],
                       ["l40", "l41", "l42"], issue_eng="sync",
                       free=[wo_th0, wo_th0, wo_th0])
        ln4_rows = ln_stream_stats(
            lambda kk: SP1[kk * 128:(kk + 1) * 128, :],
            ln4_gate, EX[0], EX[1], ln4ring, stage="sg",
            stage_free=(wo_th0, wo_th0),
            tiles=(XBF32[:, 7168:7680], XBF32[:, 7680:8192]))

        xh4_tok = {}
        li4 = 0
        for h2 in range(NCH):
            for kk in range(KT):
                slot, tok = qring.load(li4, SP1[kk * 128:(kk + 1) * 128,
                                               h2 * CH:(h2 + 1) * CH])
                eng = "vector" if li4 % 4 != 3 else "gpsimd"
                k.dep([eng], at=tmax(tok, ln4_rows))
                def xh4(e, kk=kk, h2=h2, slot=slot):
                    hs = slice(h2 * CH, (h2 + 1) * CH)
                    e.tensor_sub(slot, slot, EX[0][:, hs])
                    e.chain()
                    e.tensor_mul(XBF[kk][:, hs], slot, EX[1][:, hs])
                    e.mark()
                k.body(eng, xh4)
                tok2 = (eng, k.snap(eng))
                qring.release(li4, tok2)
                xh4_tok[(kk, h2)] = tok2
                li4 += 1
                warm()

        def h1_sl(kk):
            return bc16(Hf[kk // 2])[:, (kk % 2) * T:(kk % 2) * T + T]

        def h2_sl(kk):
            return bc16(Hf[4 + kk // 2])[:, (kk % 2) * T:(kk % 2) * T + T]

        NJ = CH // 512
        em1_tok = {}
        for h2 in range(NCH):
            for m in range(8):
                bs = (m % 2) * 4 + (h2 % 2) * 2
                at = None
                if m == 0:
                    at = tmax(*(xh4_tok[(kk, h2)] for kk in range(KT)),
                              *((("gd_fw1", 32),) if h2 == 0 else ()))
                k.dep(["tensor"], at=at)
                def mm1(t, m=m, bs=bs, h2=h2):
                    bank_wait(t, [bs + j for j in range(NJ)])
                    for j in range(NJ):
                        n = h2 * NJ + j
                        sl = slice(n * NSZ, (n + 1) * NSZ)
                        for kk in range(KT):
                            t.matmul(PB[bs + j][:], fw1_sl(kk, m),
                                     XBF[kk][:, sl], start=(kk == 0),
                                     stop=(kk == KT - 1))
                    t.mark()
                k.T(mm1)
                mtok = ("tensor", k.snap("tensor"))
                k.dep(["vector"], at=tmax(mtok))
                def em1(v, m=m, bs=bs, h2=h2):
                    for j in range(NJ):
                        n = h2 * NJ + j
                        sl = slice(n * NSZ, (n + 1) * NSZ)
                        v.tensor_copy(h1_sl(m)[:, sl], PB[bs + j][:])
                    tok = ("vector", v.mark())
                    for j in range(NJ):
                        banks[bs + j] = tok
                    em1_tok[(h2, m)] = tok
                k.V(em1)

        em2_tok = {}
        for h2 in range(NCH):
            for m in range(8):
                bs = (m % 2) * 4 + (h2 % 2) * 2
                at = None
                if m == 0:
                    at = tmax(*(em1_tok[(h2, mm)] for mm in range(8)),
                              *((("gd_fa1", 32),) if h2 == 0 else ()))
                k.dep(["tensor"], at=at)
                def mm2(t, m=m, bs=bs, h2=h2):
                    bank_wait(t, [bs + j for j in range(NJ)])
                    for j in range(NJ):
                        n = h2 * NJ + j
                        sl = slice(n * NSZ, (n + 1) * NSZ)
                        for kk in range(KT):
                            t.matmul(PB[bs + j][:], fa1_sl(kk, m),
                                     h1_sl(kk)[:, sl], start=(kk == 0),
                                     stop=(kk == KT - 1))
                    t.mark()
                k.T(mm2)
                mtok = ("tensor", k.snap("tensor"))
                k.dep(["vector"], at=tmax(mtok))
                def em2(v, m=m, bs=bs, h2=h2):
                    for j in range(NJ):
                        n = h2 * NJ + j
                        sl = slice(n * NSZ, (n + 1) * NSZ)
                        v.tensor_copy(h2_sl(m)[:, sl], PB[bs + j][:])
                    tok = ("vector", v.mark())
                    for j in range(NJ):
                        banks[bs + j] = tok
                    em2_tok[(h2, m)] = tok
                k.V(em2)

        x3ring = Ring(k, [EXf[2][:, i * 512:(i + 1) * 512] for i in range(4)],
                      ["rr0", "rr1", "rr2", "rr3"], issue_eng="sync",
                      free=[phaseC_T] * 4)
        out_slots = [Xf[4][:, i * 512:(i + 1) * 512] for i in range(4)]
        out_free = [("ob%d" % i, k.snap("ob%d" % i)) for i in range(4)]
        ri3 = 0
        oi3 = 0
        for h2 in range(NCH):
            for m in range(8):
                bs = (m % 2) * 4 + (h2 % 2) * 2
                at = None
                if m == 0:
                    at = tmax(*(em2_tok[(h2, mm)] for mm in range(8)),
                              *((("gd_fa2", 32),) if h2 == 0 else ()))
                k.dep(["tensor"], at=at)
                def mm3(t, m=m, bs=bs, h2=h2):
                    bank_wait(t, [bs + j for j in range(NJ)])
                    for j in range(NJ):
                        n = h2 * NJ + j
                        sl = slice(n * NSZ, (n + 1) * NSZ)
                        for kk in range(KT):
                            t.matmul(PB[bs + j][:], fa2_sl(kk, m),
                                     h2_sl(kk)[:, sl], start=(kk == 0),
                                     stop=(kk == KT - 1))
                    t.mark()
                k.T(mm3)
                mtok = ("tensor", k.snap("tensor"))
                slots = []
                for j in range(NJ):
                    n = h2 * NJ + j
                    sl = slice(n * NSZ, (n + 1) * NSZ)
                    slots.append((ri3 + j, x3ring.load(
                        ri3 + j, SP1[m * 128:(m + 1) * 128, sl],
                        gates=[sp1_done[(m, n // NHH, n % NHH)]])))
                ri3 += NJ
                k.dep(["scalar"],
                      at=tmax(mtok, *(out_free[(oi3 + j) % 4]
                                      for j in range(NJ))))
                def em3(s, m=m, bs=bs, oi3=oi3):
                    for j in range(NJ):
                        s.activation(out_slots[(oi3 + j) % 4], PB[bs + j][:],
                                     AF.Tanh, bias=SM[:, 40 + m:41 + m],
                                     scale=1.0)
                    tok2 = ("scalar", s.mark())
                    for j in range(NJ):
                        banks[bs + j] = tok2
                k.S(em3)
                stok = ("scalar", k.snap("scalar"))
                for j in range(NJ):
                    n = h2 * NJ + j
                    sl = slice(n * NSZ, (n + 1) * NSZ)
                    ridx, (slot, tok) = slots[j]
                    osl = out_slots[(oi3 + j) % 4]
                    k.dep(["vector"], at=tmax(stok, tok))
                    def am3(v, osl=osl, slot=slot):
                        v.tensor_add(osl, osl, slot)
                        v.mark()
                    k.V(am3)
                    vtok = ("vector", k.snap("vector"))
                    x3ring.release(ridx, vtok)
                    obch = "ob%d" % ((oi3 + j) % 4)
                    k.dep(["sync"], at=tmax(vtok))
                    def owr(y, m=m, sl=sl, osl=osl, obch=obch):
                        y.dma(obch, YT[m * 128:(m + 1) * 128, sl], osl)
                    k.Y(owr)
                    out_free[(oi3 + j) % 4] = (obch, k.snap(obch))
                oi3 += NJ

        fin_vals = {name: c.val for name, c in k.ch.items()}
        def fin(y):
            for name, v in sorted(fin_vals.items()):
                if v > 0:
                    y.wait_ge(k.ch[name].sem, v)
        k.Y(fin)

    return nc


def _prep_inputs(inputs, b, T=2048):
    c = np.ascontiguousarray
    f = np.float32

    def col(v, n):
        return np.asarray(v, f).reshape(n, 128).T

    sm = np.zeros((128, 120), f)
    sm[:, 0:8] = col(inputs["L0_attn_bz"], 8)
    sm[:, 8:16] = col(inputs["L0_attn_bh"], 8)
    sm[:, 16:24] = col(inputs["L0_attn_gamma"], 8)
    sm[:, 24:32] = col(inputs["L0_attn_beta"], 8)
    sm[:, 32:40] = col(inputs["L0_ffn_b_proj"], 8)
    sm[:, 40:48] = col(inputs["L1_ffn_bias"], 8)
    sm[:, 48:56] = col(inputs["L1_attn_time_decay"], 8)
    sm[:, 56:64] = col(inputs["L1_attn_time_first"], 8)
    sm[:, 64:72] = col(np.asarray(inputs["L1_attn_time_mix_k"]).reshape(-1), 8)
    sm[:, 72:80] = col(np.asarray(inputs["L1_attn_time_mix_v"]).reshape(-1), 8)
    sm[:, 80:88] = col(np.asarray(inputs["L1_attn_time_mix_r"]).reshape(-1), 8)
    sm[:, 88:120] = col(inputs["L0_ffn_b_fc"], 32)

    m = dict(
        XT=c(np.asarray(inputs["L0_attn_x"], f)[b].T[:, 0:T]),
        WZ=c(np.asarray(inputs["L0_attn_Wz"], f)),
        WH=c(np.asarray(inputs["L0_attn_Wh"], f)),
        WFCT=c(np.asarray(inputs["L0_ffn_W_fc"], f).T),
        WPRT=c(np.asarray(inputs["L0_ffn_W_proj"], f).T),
        AFC=c(np.asarray(inputs["L0_ffn_A_fc"], f)),
        BFCT=c(np.asarray(inputs["L0_ffn_B_fc"], f).T),
        APR=c(np.asarray(inputs["L0_ffn_A_proj"], f)),
        BPRT=c(np.asarray(inputs["L0_ffn_B_proj"], f).T),
        WKT=c(np.asarray(inputs["L1_attn_Wk"], f).T),
        WVT=c(np.asarray(inputs["L1_attn_Wv"], f).T),
        WRT=c(np.asarray(inputs["L1_attn_Wr"], f).T),
        WOT=c(np.asarray(inputs["L1_attn_Wo"], f).T),
        FW1=c(np.asarray(inputs["L1_ffn_weight"], f)),
        FA1=c(np.asarray(inputs["L1_ffn_aux_weight1"], f)),
        FA2=c(np.asarray(inputs["L1_ffn_aux_weight2"], f)),
        SM=sm,
        IDNI=np.eye(128, dtype=f),
    )
    return m


def kernel(**inputs):
    nc = build(T=2048, debug=False)
    in_maps = [_prep_inputs(inputs, b) for b in range(8)]
    res = run_bass_kernel_spmd(nc, in_maps, core_ids=list(range(8)))
    out = np.stack([np.asarray(r["YT"], np.float32).T for r in res.results],
                   axis=0)
    return out


# version 33
# speedup vs baseline: 1.1022x; 1.1022x over previous
import numpy as np
import concourse.bass as bass
import concourse.mybir as mybir
from concourse.bass_utils import run_bass_kernel_spmd

F32 = mybir.dt.float32
BF16 = mybir.dt.bfloat16
I32 = mybir.dt.int32
AF = mybir.ActivationFunctionType
OP = mybir.AluOpType
AX = mybir.AxisListType

D = 1024
KT = 8
LN_EPS = 1e-5
LN2C = 0.6931471805599453

ENGS = ("tensor", "vector", "scalar", "gpsimd", "sync")


class Chan:
    __slots__ = ("sem", "val")


class Eng:
    def __init__(self, K, eng, name):
        self._K = K
        self._e = eng
        self._n = name
        self.last = None

    def __getattr__(self, a):
        f = getattr(self._e, a)
        if not callable(f):
            return f

        def g(*args, **kw):
            ins = f(*args, **kw)
            if a == "wait_ge":
                return ins
            if ins is not None and hasattr(ins, "then_inc"):
                self.last = ins
            return ins

        return g

    def mark(self):
        K = self._K
        c = K.ch[self._n]
        assert self.last is not None, "mark with no pending instruction"
        self.last.then_inc(c.sem, 1)
        c.val += 1
        self.last = None
        return c.val

    def chain(self):
        v = self.mark()
        self._e.wait_ge(self._K.ch[self._n].sem, v)

    def wait(self, name, val):
        if val > 0:
            self._e.wait_ge(self._K.ch[name].sem, val)

    def dma(self, name, out, in_, **kw):
        K = self._K
        c = K.ch[name]
        ins = self._e.dma_start(out=out, in_=in_, **kw)
        ins.then_inc(c.sem, 16)
        c.val += 16
        self.last = None
        return c.val


class K:
    def __init__(self, nc, blk):
        self.nc = nc
        self.blk = blk
        self.ch = {}
        for n in ENGS:
            self.chan(n)
        self.pend = {n: {} for n in ENGS}

    def chan(self, name):
        if name in self.ch:
            return self.ch[name]
        c = Chan()
        c.sem = self.nc.alloc_semaphore("ch_" + name)
        c.val = 0
        self.ch[name] = c
        return c

    def dep(self, cons, prods=None, at=None):
        for cn in cons:
            d = self.pend[cn]
            if prods:
                for p in prods:
                    v = self.ch[p].val
                    if v > d.get(p, 0):
                        d[p] = v
            if at:
                for p, v in at.items():
                    if v > d.get(p, 0):
                        d[p] = v

    def body(self, name, fn):
        def run(eng):
            ctx = Eng(self, eng, name)
            pend = self.pend[name]
            own = self.ch[name]
            if own.val > pend.get(name, 0):
                pend[name] = own.val
            if pend:
                for p, v in sorted(pend.items()):
                    if v > 0:
                        eng.wait_ge(self.ch[p].sem, v)
                pend.clear()
            fn(ctx)
            if ctx.last is not None:
                ctx.mark()
        getattr(self.blk, name)(run)

    def T(self, fn): self.body("tensor", fn)
    def V(self, fn): self.body("vector", fn)
    def S(self, fn): self.body("scalar", fn)
    def G(self, fn): self.body("gpsimd", fn)
    def Y(self, fn): self.body("sync", fn)

    def snap(self, name):
        return self.ch[name].val


def _ap(t):
    return t if isinstance(t, bass.AP) else t[:]


def bc16(t):
    return _ap(t).bitcast(BF16)


def bci32(t):
    return _ap(t).bitcast(I32)


def tmax(*toks):
    at = {}
    for t in toks:
        if t is None:
            continue
        if isinstance(t, dict):
            for p, v in t.items():
                at[p] = max(at.get(p, 0), v)
        else:
            at[t[0]] = max(at.get(t[0], 0), t[1])
    return at


class Ring:
    def __init__(self, k, slots, chans, issue_eng="sync", free=None):
        self.k = k
        self.slots = slots
        self.chans = chans
        self.eng = issue_eng
        self.free = list(free) if free else [None] * len(slots)

    def load(self, i, in_, width=None, gates=None):
        k = self.k
        i = i % len(self.slots)
        slot = self.slots[i] if width is None else self.slots[i][:, 0:width]
        ch = self.chans[i]
        at = tmax(self.free[i], *(gates or []))
        if at:
            k.dep([self.eng], at=at)
        def dma(y, slot=slot, ch=ch, in_=in_):
            y.dma(ch, slot, in_)
        k.body(self.eng, dma)
        return slot, (ch, k.snap(ch))

    def release(self, i, token):
        self.free[i % len(self.slots)] = token


def build(T=2048, debug=False):
    nc = bass.Bass("TRN2", target_bir_lowering=False)
    NSZ = 512
    NT = T // NSZ
    TH = T // 2
    NHH = TH // 512
    FT = min(512, T)
    NFT = T // FT
    CH = min(1024, T)
    NCH = T // CH

    def din(name, shape, dt=F32):
        return nc.dram_tensor(name, shape, dt, kind="ExternalInput").ap()

    XT = din("XT", [D, T])
    WZ = din("WZ", [D, D]); WH = din("WH", [D, D])
    WFCT = din("WFCT", [D, 4096]); WPRT = din("WPRT", [4096, D])
    AFC = din("AFC", [16, D]); BFCT = din("BFCT", [16, 4096])
    APR = din("APR", [16, 4096]); BPRT = din("BPRT", [16, D])
    WKT = din("WKT", [D, D]); WVT = din("WVT", [D, D])
    WRT = din("WRT", [D, D]); WOT = din("WOT", [D, D])
    FW1 = din("FW1", [D, D]); FA1 = din("FA1", [D, D]); FA2 = din("FA2", [D, D])
    SM_IN = din("SM", [128, 120])
    IDNI = din("IDNI", [128, 128])

    spk = "ExternalOutput" if debug else "Internal"
    SP1 = nc.dram_tensor("SP1", [D, T], F32, kind=spk).ap()
    SP2 = nc.dram_tensor("SP2", [D, T], F32, kind=spk).ap()
    SP3 = nc.dram_tensor("SP3", [D, T], F32, kind=spk).ap()
    YT = nc.dram_tensor("YT", [D, T], F32, kind="ExternalOutput").ap()

    RT = 2048
    X0 = nc.alloc_sbuf_tensor("X0", [128, 8 * RT], F32)
    H0 = nc.alloc_sbuf_tensor("H0", [128, 8 * RT], F32)
    XB0 = nc.alloc_sbuf_tensor("XB0", [128, 8 * RT], BF16)
    E0 = nc.alloc_sbuf_tensor("E0", [128, 4 * RT], F32)
    SC = nc.alloc_sbuf_tensor("SC", [128, 2560], F32)
    AT0 = nc.alloc_sbuf_tensor("AT0", [128, 1024], F32)
    SM = nc.alloc_sbuf_tensor("SMT", [128, 120], F32)
    DRV = nc.alloc_sbuf_tensor("DRV", [128, 40], F32)
    QS = nc.alloc_sbuf_tensor("QS", [128, 12], F32)
    MX = nc.alloc_sbuf_tensor("MX", [128, 16], F32)
    ONESB = nc.alloc_sbuf_tensor("ONESB", [128, 128], BF16)
    IDN = nc.alloc_sbuf_tensor("IDN", [128, 128], BF16)
    ASL = nc.alloc_sbuf_tensor("ASL", [16, 256], BF16)
    ABACK = nc.alloc_sbuf_tensor("ABACK", [128, 8], F32)
    CARA = nc.alloc_sbuf_tensor("CARA", [128, 8], F32)
    CARB = nc.alloc_sbuf_tensor("CARB", [128, 8], F32)

    Xr = [X0[:, i * RT:i * RT + T] for i in range(8)]
    Xf = [X0[:, i * RT:(i + 1) * RT] for i in range(8)]
    Hr = [H0[:, i * RT:i * RT + T] for i in range(8)]
    Hf = [H0[:, i * RT:(i + 1) * RT] for i in range(8)]
    XBF = [XB0[:, i * RT:i * RT + T] for i in range(8)]
    XBFf = [XB0[:, i * RT:(i + 1) * RT] for i in range(8)]
    EX = [E0[:, i * RT:i * RT + T] for i in range(4)]
    EXf = [E0[:, i * RT:(i + 1) * RT] for i in range(4)]
    SCR = [SC[:, i * 512:(i + 1) * 512] for i in range(5)]

    PB = [nc.alloc_psum_tensor(f"PB{i}", [128, NSZ], F32) for i in range(8)]

    with nc.Block() as blk:
        k = K(nc, blk)

        for cn in ("xin0", "xin1", "smalls", "spx01", "spx23", "spx45",
                   "sp2w0", "sp2w1", "sp2w2", "sp2w3",
                   "sp2w4", "sp2w5", "sp2w6", "sp2w7", "yt",
                   "at0", "at1", "rr0", "rr1", "rr2", "rr3",
                   "ob0", "ob1", "ob2", "ob3",
                   "qp0", "qp1", "ar0", "ar1", "l40", "l41", "l42"):
            k.chan(cn)
        for wn in ("wz", "wh", "afcb", "aprb", "bfc", "bpr", "idn",
                   "wk", "wv", "wr", "wo", "fw1", "fa1", "fa2"):
            k.chan("gd_" + wn)

        banks = [None] * 8

        def bank_wait(ctx, idxs):
            for i in idxs:
                if banks[i] is not None:
                    ctx.wait(*banks[i])

        def gdma(wn, out, in_, gate=None, **kw):
            c = k.ch["gd_" + wn]
            at = tmax(*(gate or []))
            if at:
                k.dep(["gpsimd"], at=at)
            def f(g):
                ins = g.dma_start(out=out, in_=in_, **kw)
                ins.then_inc(c.sem, 16)
                g.last = None
            k.G(f)
            c.val += 16
            return ("gd_" + wn, c.val)

        def w8_in(hbm, half):
            return hbm[half * 512:(half + 1) * 512, :].rearrange(
                "(a p) m -> p a m", p=128)

        def w8_out(region):
            return bc16(region).rearrange("p (a m) -> p a m", a=4)

        def ln_stats(src, tokens, MB, RB, stage="v"):
            slot_tok = [None, None]
            si = 0
            tiles = [(kk, n) for kk in range(KT) for n in range(NT)]
            if stage == "sg":
                for p0 in range(0, len(tiles), 2):
                    pair = tiles[p0:p0 + 2]
                    at = tmax(slot_tok[0], slot_tok[1],
                              *(tokens(kk, n) for kk, n in pair
                                if tokens))
                    k.dep(["scalar"], at=at or None)
                    def cst(s, pair=pair, p0=p0):
                        for j, (kk, n) in enumerate(pair):
                            sb = bc16(SCR[(p0 + j) % 2])
                            s.activation(sb[:, 0:512], src(kk, n), AF.Copy)
                        s.mark()
                    k.S(cst)
                    k.dep(["gpsimd"], at=at or None)
                    def sqg(g, pair=pair, p0=p0):
                        for j, (kk, n) in enumerate(pair):
                            sb = bc16(SCR[(p0 + j) % 2])
                            g.tensor_mul(sb[:, 512:1024], src(kk, n),
                                         src(kk, n))
                        g.mark()
                    k.G(sqg)
                    k.dep(["tensor"], ["scalar", "gpsimd"])
                    def mms2(t, pair=pair, p0=p0):
                        for j, (kk, n) in enumerate(pair):
                            sb = bc16(SCR[(p0 + j) % 2])
                            if kk == 0:
                                bank_wait(t, [n, NT + n])
                            t.matmul(PB[n][:], ONESB[:], sb[:, 0:512],
                                     start=(kk == 0), stop=(kk == KT - 1))
                            t.matmul(PB[NT + n][:], ONESB[:],
                                     sb[:, 512:1024],
                                     start=(kk == 0), stop=(kk == KT - 1))
                            slot_tok[(p0 + j) % 2] = ("tensor", t.mark())
                    k.T(mms2)
                return _ln_rows(MB, RB)
            for kk in range(KT):
                for n in range(NT):
                    sb = bc16(SCR[si % 2])
                    tok = tokens(kk, n) if tokens else None
                    at = tmax(slot_tok[si % 2], tok)
                    k.dep(["vector"], at=at or None)
                    def cst(v, kk=kk, n=n, sb=sb):
                        v.tensor_copy(sb[:, 0:512], src(kk, n))
                        v.tensor_mul(sb[:, 512:1024], src(kk, n), src(kk, n))
                        v.mark()
                    k.V(cst)
                    k.dep(["tensor"], ["vector"])
                    def mms(t, kk=kk, n=n, sb=sb):
                        if kk == 0:
                            bank_wait(t, [n, NT + n])
                        t.matmul(PB[n][:], ONESB[:], sb[:, 0:512],
                                 start=(kk == 0), stop=(kk == KT - 1))
                        t.matmul(PB[NT + n][:], ONESB[:], sb[:, 512:1024],
                                 start=(kk == 0), stop=(kk == KT - 1))
                        slot_tok[si % 2] = ("tensor", t.mark())
                    k.T(mms)
                    si += 1
            return _ln_rows(MB, RB)

        def _ln_rows(MB, RB):
            k.dep(["vector"], ["tensor"])
            def rows_v(v):
                for n in range(NT):
                    sl = slice(n * NSZ, (n + 1) * NSZ)
                    v.tensor_scalar(MB[:, sl], PB[n][:], 1.0 / D, None, OP.mult)
                    v.tensor_scalar(RB[:, sl], PB[NT + n][:], 1.0 / D, LN_EPS,
                                    OP.mult, OP.add)
                v.chain()
                for n in range(NT):
                    sl = slice(n * NSZ, (n + 1) * NSZ)
                    v.tensor_mul(SCR[0][:], MB[:, sl], MB[:, sl])
                    v.chain()
                    v.tensor_sub(SCR[1][:], RB[:, sl], SCR[0][:])
                    v.chain()
                    v.reciprocal(RB[:, sl], SCR[1][:])
                v.chain()
            k.V(rows_v)
            vtok = ("vector", k.snap("vector"))
            for i in range(2 * NT):
                banks[i] = vtok
            k.dep(["scalar"], ["vector"])
            def rows_s(s):
                for n in range(NT):
                    sl = slice(n * NSZ, (n + 1) * NSZ)
                    s.activation(RB[:, sl], RB[:, sl], AF.Sqrt)
                    s.chain()
            k.S(rows_s)
            return ("scalar", k.snap("scalar"))

        def ln_stream_stats(row_hbm, data_gate, MB, RB, ring,
                            stage="v",
                            stage_free=(None, None), tiles=None):
            slot_mm = [stage_free[0], stage_free[1]]
            li = 0
            si = 0
            for kk in range(KT):
                for h2 in range(NCH):
                    slot, tok = ring.load(
                        li, row_hbm(kk)[:, h2 * CH:(h2 + 1) * CH],
                        gates=None if data_gate is None
                        else [data_gate(kk, h2)])
                    last_mm = None
                    for j in range(CH // 512):
                        n = h2 * (CH // 512) + j
                        sb = bc16(_ap(tiles[si % 2]) if tiles
                                  else SCR[si % 2])
                        src = slot[:, j * 512:(j + 1) * 512]
                        at = tmax(slot_mm[si % 2], tok)
                        if stage == "sg":
                            k.dep(["scalar"], at=at or None)
                            def csts(s, sb=sb, src=src):
                                s.activation(sb[:, 0:512], src, AF.Copy)
                                s.mark()
                            k.S(csts)
                            k.dep(["gpsimd"], at=at or None)
                            def sqg(g, sb=sb, src=src):
                                g.tensor_mul(sb[:, 512:1024], src, src)
                                g.mark()
                            k.G(sqg)
                            k.dep(["tensor"], ["scalar", "gpsimd"])
                        else:
                            k.dep(["vector"], at=at or None)
                            def cst(v, sb=sb, src=src):
                                v.tensor_copy(sb[:, 0:512], src)
                                v.tensor_mul(sb[:, 512:1024], src, src)
                                v.mark()
                            k.V(cst)
                            k.dep(["tensor"], ["vector"])
                        def mms(t, kk=kk, n=n, sb=sb, si=si):
                            if kk == 0:
                                bank_wait(t, [n, NT + n])
                            t.matmul(PB[n][:], ONESB[:], sb[:, 0:512],
                                     start=(kk == 0), stop=(kk == KT - 1))
                            t.matmul(PB[NT + n][:], ONESB[:], sb[:, 512:1024],
                                     start=(kk == 0), stop=(kk == KT - 1))
                            slot_mm[si % 2] = ("tensor", t.mark())
                        k.T(mms)
                        last_mm = ("tensor", k.snap("tensor"))
                        si += 1
                    ring.release(li, last_mm)
                    li += 1
            return _ln_rows(MB, RB)

        def p0(y):
            inv = XT.rearrange("(a p) t -> p a t", p=128)
            outv = X0[:].rearrange("p (a t) -> p a t", a=8)[:, :, 0:T]
            y.dma("xin0", outv[:, 0:4, :], inv[:, 0:4, :])
            y.dma("xin1", outv[:, 4:8, :], inv[:, 4:8, :])
            y.dma("smalls", SM[:], SM_IN)
        k.Y(p0)
        def p0v(v):
            v.memset(ONESB[:], 1.0)
            v.memset(ABACK[:], 0.0)
            v.memset(CARA[:], 0.0)
            v.memset(CARB[:], 0.0)
            v.memset(DRV[:, 8:9], -LN2C)
            v.mark()
        k.V(p0v)
        k.dep(["vector", "scalar"], at={"smalls": k.snap("smalls")})
        def p0d(v):
            v.tensor_scalar(DRV[:, 0:8], SM[:, 0:8], -1.0, None, OP.mult)
            v.mark()
        k.V(p0d)
        def p0s(s):
            s.activation(DRV[:, 16:24], SM[:, 56:64], AF.Exp)
            s.chain()
            s.activation(DRV[:, 24:32], SM[:, 48:56], AF.Exp)
            s.mark()
        k.S(p0s)
        def p0d2(s):
            s.activation(DRV[:, 32:40], DRV[:, 24:32], AF.Exp, scale=-1.0)
            s.mark()
        k.S(p0d2)
        smalls_v = ("vector", k.snap("vector"))
        smalls_s = ("scalar", k.snap("scalar"))

        gdma("idn", IDN[:], IDNI)
        gdma("wz", w8_out(Hf[4]), w8_in(WZ, 0))
        gdma("wz", w8_out(Hf[5]), w8_in(WZ, 1))
        gdma("wh", w8_out(Hf[6]), w8_in(WH, 0))
        gdma("wh", w8_out(Hf[7]), w8_in(WH, 1))

        def ln1_tok(kk, n):
            return ("xin0", 16) if kk < 4 else ("xin1", 16)

        k.dep(["tensor"], at=tmax(smalls_v))
        ln1_rows = ln_stats(lambda kk, n: Xr[kk][:, n * NSZ:(n + 1) * NSZ],
                            ln1_tok, EX[0], EX[1])
        stats_T = ("tensor", k.snap("tensor"))

        xhat_tok = {}
        k.dep(["vector", "gpsimd"], at=tmax(ln1_rows))
        def xh_v(v):
            for kk in (0, 1, 2, 4, 5, 6):
                for n in range(NT):
                    sl = slice(n * NSZ, (n + 1) * NSZ)
                    v.tensor_sub(SCR[2][:], Xr[kk][:, sl], EX[0][:, sl])
                    v.chain()
                    v.tensor_mul(XBF[kk][:, sl], SCR[2][:], EX[1][:, sl])
                    v.chain()
                xhat_tok[kk] = ("vector", k.ch["vector"].val)
        k.V(xh_v)
        def xh_g(g):
            for kk in (3, 7):
                for n in range(NT):
                    sl = slice(n * NSZ, (n + 1) * NSZ)
                    g.tensor_sub(SCR[3][:], Xr[kk][:, sl], EX[0][:, sl])
                    g.chain()
                    g.tensor_mul(XBF[kk][:, sl], SCR[3][:], EX[1][:, sl])
                    g.chain()
                xhat_tok[kk] = ("gpsimd", k.ch["gpsimd"].val)
        k.G(xh_g)
        xhat_all = [("vector", k.snap("vector")), ("gpsimd", k.snap("gpsimd"))]

        for pair, ch in ((0, "spx01"), (2, "spx23"), (4, "spx45")):
            k.dep(["scalar"], at={"xin0" if pair < 4 else "xin1": 16})
            def spl(y, pair=pair, ch=ch):
                for kk in (pair, pair + 1):
                    y.dma(ch, SP1[kk * 128:(kk + 1) * 128, 0:T], Xr[kk])
            k.body("scalar", spl)

        def wz_sl(kk, m):
            return bc16(Hf[4 + kk // 4])[:, (kk % 4) * D + m * 128:
                                         (kk % 4) * D + (m + 1) * 128]

        def wh_sl(kk, m):
            return bc16(Hf[6 + kk // 4])[:, (kk % 4) * D + m * 128:
                                         (kk % 4) * D + (m + 1) * 128]

        qring = Ring(k, [AT0[:], EXf[3][:, 0:1024]], ["at0", "at1"])
        QCH = 1024

        class Pass1:

            def __init__(self, hbm, nrows, ping, gate=None):
                self.hbm = hbm
                self.ping = ping
                self.chunks = [(kk, c) for kk in range(nrows)
                               for c in range(hbm.shape[1] // QCH)]
                self.ci = 0
                self.prev = None
                def mz(v, ping=ping):
                    v.memset(MX[:, ping:ping + 1], 0.0)
                    v.mark()
                k.V(mz)

            def emit(self, n):
                end = min(self.ci + n, len(self.chunks))
                while self.ci < end:
                    ci = self.ci
                    kk, c = self.chunks[ci]
                    ping = self.ping
                    slot, tok = qring.load(
                        ci, self.hbm[kk * 128:(kk + 1) * 128,
                                     c * QCH:(c + 1) * QCH])
                    k.dep(["vector"], at=tmax(tok, self.prev))
                    def red(v, ci=ci, slot=slot, ping=ping):
                        v.tensor_reduce(MX[:, 12:13], slot, AX.X,
                                        OP.max, apply_absolute_value=True)
                        v.chain()
                        v.tensor_max(
                            MX[:, ping + (ci + 1) % 2:ping + 1 + (ci + 1) % 2],
                            MX[:, ping + ci % 2:ping + 1 + ci % 2],
                            MX[:, 12:13])
                        qring.release(ci, ("vector", v.mark()))
                    k.V(red)
                    self.prev = ("vector", k.snap("vector"))
                    self.ci += 1

            def fin(self):
                assert self.ci == len(self.chunks)
                return self.ping + (self.ci % 2), self.prev

        def zrow(p): return EXf[2][:, p * 512:(p + 1) * 512]
        def grow(p): return EXf[2][:, (2 + p) * 512:(3 + p) * 512]
        rtmp = SCR[4]

        C_B, V_B = Xf[4], Xf[5]
        HD = [Hr[0], Hr[1], Hr[2], Hr[3], Xr[0], Xr[1], Xr[2], Xr[3]]
        cv_gate = tmax(("spx45", 32), xhat_tok[4], xhat_tok[5], stats_T)

        p1fc = Pass1(WFCT, KT, 8)
        p1pr = Pass1(WPRT, 32, 10)
        p1fc.emit(4)

        last_mm = {}
        scan_done = {}
        evs_done = {}
        evv_done = {}
        ti = 0
        for m in range(8):
            for n in range(NT):
                sl = slice(n * NSZ, (n + 1) * NSZ)
                at = {}
                if m == 0 and n == 0:
                    at = tmax(*xhat_all, ("gd_wz", 32), ("gd_wh", 32))
                k.dep(["tensor"], at=at or None)
                def mmzh(t, m=m, n=n, sl=sl):
                    bank_wait(t, [n, NT + n])
                    for kk in range(KT):
                        t.matmul(PB[n][:], wz_sl(kk, m), XBF[kk][:, sl],
                                 start=(kk == 0), stop=(kk == KT - 1))
                    for kk in range(KT):
                        t.matmul(PB[NT + n][:], wh_sl(kk, m), XBF[kk][:, sl],
                                 start=(kk == 0), stop=(kk == KT - 1))
                    last_mm[(m, n)] = ("tensor", t.mark())
                k.T(mmzh)
                p = ti % 2
                at = tmax(last_mm[(m, n)],
                          cv_gate if (m, n) == (0, 0) else None,
                          smalls_v if (m, n) == (0, 0) else None,
                          smalls_s if (m, n) == (0, 0) else None,
                          scan_done.get(m - 1),
                          evv_done.get(ti - 2))
                k.dep(["scalar"], at=at)
                def ev_s(s, m=m, n=n, sl=sl, p=p):
                    s.activation(C_B[:, sl], PB[n][:], AF.Sigmoid,
                                 bias=DRV[:, m:m + 1], scale=-1.0)
                    s.activation(zrow(p)[:], PB[n][:], AF.Sigmoid,
                                 bias=SM[:, m:m + 1], scale=1.0)
                    s.activation(grow(p)[:], PB[NT + n][:], AF.Sigmoid,
                                 bias=SM[:, 8 + m:9 + m], scale=1.0)
                    evs_done[(m, n)] = ("scalar", s.mark())
                k.S(ev_s)
                k.dep(["vector"], at=tmax(evs_done[(m, n)], last_mm[(m, n)]))
                def ev_v(v, m=m, n=n, sl=sl, p=p, ti=ti):
                    v.tensor_scalar(rtmp[:], PB[NT + n][:], SM[:, 8 + m:9 + m],
                                    0.0, OP.add, OP.max)
                    v.chain()
                    v.scalar_tensor_tensor(grow(p)[:], grow(p)[:], 0.5,
                                           rtmp[:], OP.min, OP.add)
                    v.chain()
                    v.tensor_mul(V_B[:, sl], zrow(p)[:], grow(p)[:])
                    tok = ("vector", v.mark())
                    banks[n] = tok
                    banks[NT + n] = tok
                    evv_done[ti] = tok
                k.V(ev_v)
                ti += 1
            at = tmax(evs_done[(m, NT - 1)])
            if m >= 4:
                at = tmax(at, ("spx%d%d" % (2 * ((m - 4) // 2),
                                            2 * ((m - 4) // 2) + 1), 32),
                          xhat_tok[m - 4], stats_T)
            k.dep(["vector"], at=at)
            def scan(v, m=m):
                v.tensor_tensor_scan(HD[m][:, 0:T], C_B[:, 0:T], V_B[:, 0:T],
                                     0.0, OP.mult, OP.add)
                scan_done[m] = ("vector", v.mark())
            k.V(scan)
            p1fc.emit(4)
            p1pr.emit(4)
        p1fc.emit(64)
        p1pr.emit(64)
        mloop_T = ("tensor", k.snap("tensor"))

        def pass1_fin(fcol, qcol, bank, vtok):
            k.dep(["gpsimd"], at=tmax(vtok))
            def tree(g, fcol=fcol):
                g.tensor_reduce(MX[0:1, 13:14], MX[:, fcol:fcol + 1],
                                AX.C, OP.max)
                g.mark()
            k.G(tree)
            ONE1R = SC[0:1, 2048:2176]
            def onev(v):
                v.memset(ONE1R, 1.0)
                v.mark()
            k.V(onev)
            k.dep(["tensor"], ["gpsimd", "vector"])
            def bcm(t, bank=bank):
                bank_wait(t, [bank])
                t.matmul(PB[bank][:, 0:1], ONE1R, MX[0:1, 13:14],
                         start=True, stop=True)
                t.mark()
            k.T(bcm)
            k.dep(["vector"], ["tensor"])
            def qsc(v, qcol=qcol, bank=bank):
                v.tensor_scalar(QS[:, qcol + 1:qcol + 2], PB[bank][:, 0:1],
                                1.0 / 127.0, None, OP.mult)
                v.chain()
                v.reciprocal(QS[:, qcol + 2:qcol + 3], QS[:, qcol + 1:qcol + 2])
                v.chain()
                v.tensor_scalar(QS[:, qcol + 3:qcol + 4],
                                QS[:, qcol + 1:qcol + 2], -1.0, None, OP.mult)
                v.chain()
                v.tensor_scalar(QS[:, qcol + 4:qcol + 5],
                                QS[:, qcol + 2:qcol + 3], -2.0, None, OP.mult)
                banks[bank] = ("vector", v.mark())
            k.V(qsc)
            return ("vector", k.snap("vector"))

        f_fc, f_fc_tok = p1fc.fin()
        qfc_tok = pass1_fin(f_fc, 0, 7, f_fc_tok)

        lnh_rows = ln_stats(lambda kk, n: HD[kk][:, n * NSZ:(n + 1) * NSZ],
                            lambda kk, n: scan_done[kk], EX[0], EX[1])

        xring = Ring(k, [EXf[2][:, i * 512:(i + 1) * 512] for i in range(4)],
                     ["rr0", "rr1", "rr2", "rr3"], issue_eng="sync",
                     free=[("vector", k.snap("vector"))] * 4)
        comb_done = {}
        ci = 0
        for m in range(8):
            for n in range(NT):
                sl = slice(n * NSZ, (n + 1) * NSZ)
                if m < 6:
                    sch = "spx%d%d" % (2 * (m // 2), 2 * (m // 2) + 1)
                    slot, tok = xring.load(ci, SP1[m * 128:(m + 1) * 128, sl],
                                           gates=[(sch, 32)])
                else:
                    slot, tok = Xr[m][:, sl], None
                eng = "vector" if ci % 4 != 3 else "gpsimd"
                if eng == "vector":
                    k.dep([eng], at=tmax(lnh_rows, scan_done[m], tok))
                    def cmb(e, m=m, sl=sl, slot=slot):
                        tmp = SCR[2]
                        e.tensor_sub(tmp[:], HD[m][:, sl], EX[0][:, sl])
                        e.chain()
                        e.tensor_mul(tmp[:], tmp[:], EX[1][:, sl])
                        e.chain()
                        e.tensor_scalar(tmp[:], tmp[:], SM[:, 16 + m:17 + m],
                                        SM[:, 24 + m:25 + m], OP.mult, OP.add)
                        e.chain()
                        e.tensor_add(tmp[:], tmp[:], XBF[m][:, sl])
                        e.chain()
                        e.tensor_add(Hr[m][:, sl], tmp[:], slot)
                        comb_done[(m, n)] = (eng, e.mark())
                    k.V(cmb)
                else:
                    k.dep([eng], at=tmax(lnh_rows, scan_done[m], tok))
                    def cmb_g1(e, m=m, sl=sl):
                        tmp = SCR[3]
                        e.tensor_sub(tmp[:], HD[m][:, sl], EX[0][:, sl])
                        e.chain()
                        e.tensor_mul(tmp[:], tmp[:], EX[1][:, sl])
                        e.mark()
                    k.G(cmb_g1)
                    k.dep(["vector"], ["gpsimd"])
                    def cmb_s(e, m=m):
                        e.tensor_scalar(SCR[3][:], SCR[3][:],
                                        SM[:, 16 + m:17 + m],
                                        SM[:, 24 + m:25 + m],
                                        OP.mult, OP.add)
                        e.mark()
                    k.V(cmb_s)
                    k.dep(["gpsimd"], ["vector"])
                    def cmb_g2(e, m=m, n=n, sl=sl, slot=slot):
                        tmp = SCR[3]
                        e.tensor_add(tmp[:], tmp[:], XBF[m][:, sl])
                        e.chain()
                        e.tensor_add(Hr[m][:, sl], tmp[:], slot)
                        comb_done[(m, n)] = ("gpsimd", e.mark())
                    k.G(cmb_g2)
                if m < 6:
                    xring.release(ci, comb_done[(m, n)])
                    ci += 1
        comb_m = {m: tmax(*(comb_done[(m, n)] for n in range(NT)))
                  for m in range(8)}

        f_pr, f_pr_tok = p1pr.fin()
        qpr_tok = pass1_fin(f_pr, 6, 7, f_pr_tok)

        sp2_at = {}
        for m in range(8):
            k.dep(["scalar"], at=comb_m[m])
            def spl(y, m=m):
                y.dma("sp2w%d" % m, SP2[m * 128:(m + 1) * 128, 0:T], Hr[m])
            k.body("scalar", spl)
            sp2_at[m] = ("sp2w%d" % m, 16)

        ln2_rows = ln_stats(lambda kk, n: Hr[kk][:, n * NSZ:(n + 1) * NSZ],
                            lambda kk, n: comb_done[(kk, n)], EX[0], EX[1])
        xh2_tok = {}
        k.dep(["vector", "gpsimd"], at=tmax(ln2_rows))
        def xh2_v(v):
            for kk in (0, 1, 2, 4, 5, 6):
                for n in range(NT):
                    sl = slice(n * NSZ, (n + 1) * NSZ)
                    v.tensor_sub(SCR[2][:], Hr[kk][:, sl], EX[0][:, sl])
                    v.chain()
                    v.tensor_mul(XBF[kk][:, sl], SCR[2][:], EX[1][:, sl])
                    v.chain()
                xh2_tok[kk] = ("vector", k.ch["vector"].val)
        k.V(xh2_v)
        def xh2_g(g):
            for kk in (3, 7):
                for n in range(NT):
                    sl = slice(n * NSZ, (n + 1) * NSZ)
                    g.tensor_sub(SCR[3][:], Hr[kk][:, sl], EX[0][:, sl])
                    g.chain()
                    g.tensor_mul(XBF[kk][:, sl], SCR[3][:], EX[1][:, sl])
                    g.chain()
                xh2_tok[kk] = ("gpsimd", k.ch["gpsimd"].val)
        k.G(xh2_g)
        xh2_all = [("vector", k.snap("vector")), ("gpsimd", k.snap("gpsimd"))]

        gdma("bfc", bc16(EXf[2])[0:16, 0:4096], BFCT,
             gate=[comb_m[5], ("vector", k.snap("vector")),
                   ("gpsimd", k.snap("gpsimd"))],
             max_dma_last_dim=8192)
        BSF = bc16(EXf[2])[0:16, 0:4096]
        gdma("bpr", bc16(SC)[0:16, 4096:5120], BPRT,
             gate=[("vector", k.snap("vector")),
                   ("tensor", k.snap("tensor"))])
        BSP = bc16(SC)[0:16, 4096:5120]

        k.dep(["scalar"], at=tmax(("gd_bfc", 16), qfc_tok))
        def bsf_s(s):
            s.activation(BSF, BSF, AF.Copy, scale=QS[0:16, 4:5])
            s.mark()
        k.S(bsf_s)
        bsf_tok = ("scalar", k.snap("scalar"))
        k.dep(["scalar"], at=tmax(("gd_bpr", 16), qpr_tok))
        def bsp_s(s):
            s.activation(BSP, BSP, AF.Copy, scale=QS[0:16, 10:11])
            s.mark()
        k.S(bsp_s)
        bsp_tok = ("scalar", k.snap("scalar"))

        def wfc_sl(kk, m):
            return bc16(Xf[kk])[:, m * 128:(m + 1) * 128]

        def wpr_sl(kk):
            return bc16(Hf[kk // 4])[:, (kk % 4) * D:(kk % 4 + 1) * D]

        def hm_sl(kk):
            return bc16(_ap(E0))[:, kk * FT:(kk + 1) * FT]

        x_free = {0: comb_m[4], 1: comb_m[5], 2: comb_m[6], 3: comb_m[7],
                  4: scan_done[7], 5: scan_done[7],
                  6: comb_m[6], 7: comb_m[7]}

        IQF = bci32(EXf[3][:, 1024:2048])
        IQP = bci32(AT0[:, 0:1024])
        abk = [0]

        aring = Ring(k, [ASL[0:16, 0:128], ASL[0:16, 128:256]],
                     ["ar0", "ar1"], issue_eng="gpsimd")
        ali = [0]

        sgr = []

        def merge_chunk(kk, c, dst, bs_ap, a_hbm, qv_tok, first, evac,
                        bs_tok, merged):
            asl, atok = aring.load(ali[0],
                                   a_hbm[0:16, kk * 128:(kk + 1) * 128])
            for h in range(2):
                b = abk[0] % 4
                abk[0] += 1
                dgr = dst(kk, c)[:, h * 512:(h + 1) * 512]
                bgr = bs_ap[0:16, c * 1024 + h * 512:c * 1024 + (h + 1) * 512]
                at = tmax(qv_tok, atok,
                          *([bs_tok, ("gd_idn", 16)] if first[0] else []))
                first[0] = False
                if evac == "s":
                    k.dep(["tensor"], at=at)
                    def abm(t, b=b, dgr=dgr, bgr=bgr, asl=asl):
                        bank_wait(t, [b])
                        t.matmul(PB[b][:], asl, bgr, start=True, stop=False)
                        t.matmul(PB[b][:], IDN[:], dgr, start=False,
                                 stop=True)
                        t.mark()
                    k.T(abm)
                    sgr.append((b, dgr, kk, 2 * c + h))
                    if h == 1:
                        k.dep(["scalar"], ["tensor"])
                        def mev(s, pair=tuple(sgr)):
                            for b2, d2, _, _ in pair:
                                s.activation(d2, PB[b2][:], AF.Copy)
                            tok = ("scalar", s.mark())
                            for b2, _, _, _ in pair:
                                banks[b2] = tok
                        k.S(mev)
                        stok = ("scalar", k.snap("scalar"))
                        for _, _, kk2, g2 in sgr:
                            merged[(kk2, g2)] = stok
                        del sgr[:]
                else:
                    k.dep(["tensor"], at=at)
                    def abm2(t, b=b, bgr=bgr, asl=asl):
                        bank_wait(t, [b])
                        t.matmul(PB[b][:], asl, bgr, start=True, stop=True)
                        t.mark()
                    k.T(abm2)
                    k.dep(["vector"], ["tensor"])
                    def mev2(v, b=b, dgr=dgr):
                        v.scalar_tensor_tensor(dgr, PB[b][:], 1.0,
                                               dgr, OP.mult, OP.add)
                        banks[b] = ("vector", v.mark())
                    k.V(mev2)
                    merged[(kk, 2 * c + h)] = ("vector", k.snap("vector"))
            aring.release(ali[0], ("tensor", k.snap("tensor")))
            ali[0] += 1

        def pass2(hbm, order, dst, gates, qtok, qcol, ring, ring_i0, iq,
                  deferred):
            for i, (kk, c) in enumerate(order):
                slot, tok = ring.load(ring_i0 + i,
                                      hbm[kk * 128:(kk + 1) * 128,
                                          c * QCH:(c + 1) * QCH])
                k.dep(["vector"], at=tmax(tok, qtok, *gates(kk)))
                def qv(e, slot=slot, kk=kk, c=c, iq=iq, qcol=qcol, dst=dst):
                    e.tensor_scalar(iq, slot, QS[:, qcol + 2:qcol + 3],
                                    None, OP.mult)
                    e.chain()
                    e.scalar_tensor_tensor(slot, slot,
                                           QS[:, qcol + 2:qcol + 3], iq,
                                           OP.mult, OP.subtract)
                    e.chain()
                    e.scalar_tensor_tensor(iq, slot, 0.5, iq,
                                           OP.is_ge, OP.add)
                    e.chain()
                    e.scalar_tensor_tensor(dst(kk, c), slot, -0.5, iq,
                                           OP.is_le, OP.subtract)
                    e.mark()
                k.V(qv)
                qv_tok = ("vector", k.snap("vector"))
                ring.release(ring_i0 + i, qv_tok)
                deferred.append((kk, c, qv_tok))

        fcm_ready = {}
        fc_first = [True]
        fc_dst = lambda kk, c: bc16(Xf[kk])[:, c * 1024:(c + 1) * 1024]
        fcq = []
        pass2(WFCT, [(kk, c) for c in range(4) for kk in range(8)],
              fc_dst, lambda kk: [x_free[kk]], qfc_tok, 0, qring, 0, IQF, fcq)
        for (kk, c, qv_tok) in fcq:
            merge_chunk(kk, c, fc_dst, BSF, AFC, qv_tok, fc_first, "s",
                        bsf_tok, fcm_ready)

        ring2_free = tmax(ln2_rows, *xh2_all)
        ring2 = Ring(k, [SC[:, 0:1024], SC[:, 1024:2048]], ["qp0", "qp1"],
                     issue_eng="sync", free=[ring2_free, ring2_free])

        prm = {}
        pr_dst = lambda kk, c: wpr_sl(kk)
        prq = []
        pass2(WPRT, [(kk, 0) for kk in range(32)],
              pr_dst, lambda kk: [sp2_at[kk // 4], xh2_tok[kk // 4]],
              qpr_tok, 6, ring2, 0, IQP, prq)
        prq_done = ("vector", k.snap("vector"))

        oslots = [AT0[:, 0:FT], AT0[:, FT:2 * FT]]
        ob_free = [tmax(prq_done), tmax(prq_done)]

        sp3_done = {}
        ri1 = 32
        oi = 0
        for tt in range(NFT):
            tsl = slice(tt * FT, (tt + 1) * FT)
            efc_toks = []
            for g in range(8):
                bs = (g % 2) * 4
                at = tmax(*((fcm_ready[(kk, g)] for kk in range(8))
                            if tt == 0 else ()),
                          *(xh2_all if (tt == 0 and g == 0) else ()))
                k.dep(["tensor"], at=at or None)
                def mfc(t, g=g, bs=bs, tsl=tsl):
                    bank_wait(t, [bs + i for i in range(4)])
                    for gi in range(4):
                        m = g * 4 + gi
                        for kk in range(KT):
                            t.matmul(PB[bs + gi][:, 0:FT], wfc_sl(kk, m),
                                     XBF[kk][:, tsl], start=(kk == 0),
                                     stop=(kk == KT - 1))
                    t.mark()
                k.T(mfc)
                k.dep(["scalar"], ["tensor"])
                def efc(s, g=g, bs=bs):
                    for gi in range(4):
                        m = g * 4 + gi
                        s.activation(hm_sl(m), PB[bs + gi][:, 0:FT],
                                     AF.Gelu, bias=SM[:, 88 + m:89 + m],
                                     scale=QS[:, 3:4])
                    tok = ("scalar", s.mark())
                    for i in range(4):
                        banks[bs + i] = tok
                k.S(efc)
                efc_toks.append(("scalar", k.snap("scalar")))
            if tt == 0:
                pr_first = [True]
                for (kk, c, qv_tok) in prq:
                    merge_chunk(kk, c, pr_dst, BSP, APR, qv_tok,
                                pr_first, "v", bsp_tok, prm)
                prm_toks = list(prm.values())
            for g in range(2):
                bs = g * 4
                at = tmax(*(prm_toks if tt == 0 else []), *efc_toks)
                k.dep(["tensor"], at=at)
                def mpr(t, g=g, bs=bs):
                    bank_wait(t, [bs + i for i in range(4)])
                    for gi in range(4):
                        mo = g * 4 + gi
                        for kk in range(32):
                            t.matmul(PB[bs + gi][:, 0:FT],
                                     wpr_sl(kk)[:, mo * 128:(mo + 1) * 128],
                                     hm_sl(kk), start=(kk == 0),
                                     stop=(kk == 31))
                    t.mark()
                k.T(mpr)
                mpr_tok = ("tensor", k.snap("tensor"))
                for gi in range(4):
                    mo = g * 4 + gi
                    slot, tok = ring2.load(ri1,
                                           SP2[mo * 128:(mo + 1) * 128, tsl],
                                           width=FT, gates=[sp2_at[mo]])
                    osl = oslots[oi % 2]
                    k.dep(["vector"], at=tmax(mpr_tok, tok, ob_free[oi % 2]))
                    def epr(v, mo=mo, bs=bs, gi=gi, slot=slot, osl=osl):
                        v.tensor_scalar(osl, PB[bs + gi][:, 0:FT],
                                        QS[:, 9:10], SM[:, 32 + mo:33 + mo],
                                        OP.mult, OP.add)
                        v.chain()
                        v.tensor_add(osl, osl, slot)
                        banks[bs + gi] = ("vector", v.mark())
                    k.V(epr)
                    vtok = ("vector", k.snap("vector"))
                    ring2.release(ri1, vtok)
                    ri1 += 1
                    obch = "ob%d" % (oi % 2)
                    k.dep(["sync"], at=tmax(vtok))
                    def owr(y, mo=mo, tsl=tsl, osl=osl, obch=obch):
                        y.dma(obch, SP3[mo * 128:(mo + 1) * 128, tsl], osl)
                    k.Y(owr)
                    ob_free[oi % 2] = (obch, k.snap(obch))
                    sp3_done[(mo, tt)] = (obch, k.snap(obch))
                    oi += 1

        phaseB_T = ("tensor", k.snap("tensor"))
        obf = tmax(phaseB_T, ("ob0", k.snap("ob0")), ("ob1", k.snap("ob1")))

        gdma("wk", w8_out(Hf[0]), w8_in(WKT, 0), gate=[phaseB_T])
        gdma("wk", w8_out(Hf[1]), w8_in(WKT, 1))
        gdma("wv", w8_out(Hf[2]), w8_in(WVT, 0))
        gdma("wv", w8_out(Hf[3]), w8_in(WVT, 1))
        gdma("wr", w8_out(Hf[4]), w8_in(WRT, 0))
        gdma("wr", w8_out(Hf[5]), w8_in(WRT, 1))
        gdma("wo", w8_out(Hf[6]), w8_in(WOT, 0))
        gdma("wo", w8_out(Hf[7]), w8_in(WOT, 1))

        def wk_sl(kk, m):
            return bc16(Hf[kk // 4])[:, (kk % 4) * D + m * 128:
                                     (kk % 4) * D + (m + 1) * 128]

        def wv_sl(kk, m):
            return bc16(Hf[2 + kk // 4])[:, (kk % 4) * D + m * 128:
                                         (kk % 4) * D + (m + 1) * 128]

        def wr_sl(kk, m):
            return bc16(Hf[4 + kk // 4])[:, (kk % 4) * D + m * 128:
                                         (kk % 4) * D + (m + 1) * 128]

        def wo_sl(kk, m):
            return bc16(Hf[6 + kk // 4])[:, (kk % 4) * D + m * 128:
                                         (kk % 4) * D + (m + 1) * 128]

        qring.free = [obf, obf]
        qpf = tmax(("qp0", k.snap("qp0")), ("qp1", k.snap("qp1")),
                   ("vector", k.snap("vector")))
        ln3_rows = ln_stream_stats(
            lambda kk: SP3[kk * 128:(kk + 1) * 128, :],
            lambda kk, h2: tmax(*(sp3_done[(kk, tt)]
                                  for tt in range(h2 * (CH // FT),
                                                  (h2 + 1) * (CH // FT)))),
            EX[0], EX[1], qring,
            stage_free=(qpf, qpf))

        ALPHA = Xf[4][:, 0:TH]
        BETA = Xf[4][:, TH:2 * TH]
        EKBp = [Xf[5][:, 0:TH], Xf[5][:, TH:2 * TH]]
        EKUp = [Xf[7][:, 0:TH], Xf[7][:, TH:2 * TH]]
        EWB = SC[:, 0:TH]
        EKV = Xf[6][:, 0:TH]
        NUM = Xf[6][:, TH:2 * TH]
        DEN = SC[:, 1024:1024 + TH]
        TBF = bc16(SC)[:, 4096:4096 + TH]
        XHB = [EXf[2][:, 0:1024], EXf[2][:, 1024:2048]]

        def xk_sl(kk):
            return bc16(Xf[kk // 4])[:, (kk % 4) * TH:(kk % 4 + 1) * TH]

        def xv_sl(kk):
            return bc16(Xf[2 + kk // 4])[:, (kk % 4) * TH:(kk % 4 + 1) * TH]

        def xr_sl(kk):
            return XBFf[kk // 2][:, (kk % 2) * TH:(kk % 2 + 1) * TH]

        def rw_sl(th, kk):
            if th == 0:
                return XBFf[4 + kk // 2][:, (kk % 2) * TH:(kk % 2 + 1) * TH]
            return bc16(EXf[2 + kk // 4])[:, (kk % 4) * TH:(kk % 4 + 1) * TH]

        xhb_free = [None, None]
        kv_done = {}
        rh_done = {}
        rw_done = {}
        v1_done = {}
        v2_done = {}
        g1_done = {}
        for th in range(2):
            hsl = slice(th * TH, (th + 1) * TH)
            for kk in range(KT):
                slot, tok = qring.load(kk % 2,
                                       SP3[kk * 128:(kk + 1) * 128, hsl],
                                       width=TH)
                xb = XHB[kk % 2][:, 0:TH]
                k.dep(["gpsimd"], at=tmax(tok, ln3_rows, xhb_free[kk % 2]))
                def bl_g(g, kk=kk, hsl=hsl, slot=slot, xb=xb):
                    g.tensor_sub(xb, slot, EX[0][:, hsl])
                    g.chain()
                    g.tensor_mul(xb, xb, EX[1][:, hsl])
                    g.mark()
                k.G(bl_g)
                at = tmax(tok, ("gpsimd", k.snap("gpsimd")),
                          phaseB_T if th == 0 else rh_done[(0, 7)])
                k.dep(["vector"], at=at)
                def bl_v(v, kk=kk, th=th, hsl=hsl, slot=slot, xb=xb):
                    v.tensor_sub(slot[:, 1:TH], xb[:, 1:TH], xb[:, 0:TH - 1])
                    if th == 0:
                        v.tensor_copy(slot[:, 0:1], xb[:, 0:1])
                    else:
                        v.tensor_sub(slot[:, 0:1], xb[:, 0:1],
                                     ABACK[:, kk:kk + 1])
                    v.chain()
                    for dst, col in ((xk_sl(kk), 64), (xv_sl(kk), 72),
                                     (xr_sl(kk), 80)):
                        v.scalar_tensor_tensor(dst[:, 1:TH], slot[:, 1:TH],
                                               SM[:, col + kk:col + kk + 1],
                                               xb[:, 0:TH - 1],
                                               OP.mult, OP.add)
                        if th == 0:
                            v.tensor_scalar(dst[:, 0:1], slot[:, 0:1],
                                            SM[:, col + kk:col + kk + 1],
                                            None, OP.mult)
                        else:
                            v.scalar_tensor_tensor(
                                dst[:, 0:1], slot[:, 0:1],
                                SM[:, col + kk:col + kk + 1],
                                ABACK[:, kk:kk + 1], OP.mult, OP.add)
                    if th == 0:
                        v.chain()
                        v.tensor_copy(ABACK[:, kk:kk + 1], xb[:, TH - 1:TH])
                    v.mark()
                k.V(bl_v)
                btok = ("vector", k.snap("vector"))
                qring.release(kk % 2, btok)
                xhb_free[kk % 2] = btok
            blends_done = ("vector", k.snap("vector"))

            for m in range(8):
                a = (m % 2) * 4
                b = 4 - a
                at = tmax(blends_done,
                          *((("gd_wk", 32), ("gd_wv", 32), ("gd_wr", 32))
                            if (th, m) == (0, 0) else ()))
                k.dep(["tensor"], at=at)
                def mkv(t, m=m, a=a):
                    bank_wait(t, [a + i for i in range(2 * NHH)])
                    for nh in range(NHH):
                        s2 = slice(nh * 512, (nh + 1) * 512)
                        for kk in range(KT):
                            t.matmul(PB[a + nh][:], wk_sl(kk, m),
                                     xk_sl(kk)[:, s2],
                                     start=(kk == 0), stop=(kk == KT - 1))
                        for kk in range(KT):
                            t.matmul(PB[a + NHH + nh][:], wv_sl(kk, m),
                                     xv_sl(kk)[:, s2],
                                     start=(kk == 0), stop=(kk == KT - 1))
                    kv_done[(th, m)] = ("tensor", t.mark())
                k.T(mkv)
                def mr(t, m=m, b=b):
                    bank_wait(t, [b + i for i in range(NHH)])
                    for nh in range(NHH):
                        s2 = slice(nh * 512, (nh + 1) * 512)
                        for kk in range(KT):
                            t.matmul(PB[b + nh][:], wr_sl(kk, m),
                                     xr_sl(kk)[:, s2],
                                     start=(kk == 0), stop=(kk == KT - 1))
                    rh_done[(th, m)] = ("tensor", t.mark())
                k.T(mr)
                p = m % 2
                mi = 2 * th + (m // 2) * 0
                def prev(dct, back):
                    mm2 = m - back
                    if mm2 >= 0:
                        return dct.get((th, mm2))
                    return dct.get((th - 1, 8 + mm2)) if th > 0 else None
                k.dep(["scalar"], at=tmax(kv_done[(th, m)], prev(v1_done, 2)))
                def ev1a(s, m=m, a=a, p=p):
                    for nh in range(NHH):
                        s2 = slice(nh * 512, (nh + 1) * 512)
                        s.activation(EKBp[p][:, s2], PB[a + nh][:], AF.Exp)
                    s.mark()
                k.S(ev1a)
                ek_tok = ("scalar", k.snap("scalar"))
                def ev1b(s, m=m, a=a, p=p):
                    for nh in range(NHH):
                        s2 = slice(nh * 512, (nh + 1) * 512)
                        s.activation(EKUp[p][:, s2], PB[a + nh][:], AF.Exp,
                                     bias=SM[:, 56 + m:57 + m])
                    tok = ("scalar", s.mark())
                    for i in range(NHH):
                        banks[a + i] = tok
                k.S(ev1b)
                eku_tok = ("scalar", k.snap("scalar"))
                k.dep(["scalar"], at=tmax(rh_done[(th, m)], prev(v1_done, 1)))
                def ev2(s, m=m, b=b):
                    for nh in range(NHH):
                        s2 = slice(nh * 512, (nh + 1) * 512)
                        s.activation(TBF[:, s2], PB[b + nh][:], AF.Exp,
                                     scale=-1.0)
                    tok = ("scalar", s.mark())
                    for i in range(NHH):
                        banks[b + i] = tok
                k.S(ev2)
                tb_tok = ("scalar", k.snap("scalar"))
                k.dep(["vector"], at=tmax(ek_tok, tb_tok, prev(g1_done, 1)))
                def v1(v, m=m, th=th, a=a, p=p):
                    v.tensor_scalar(EWB[:], EX[0][:, 0:TH], 0.0,
                                    DRV[:, 32 + m:33 + m], OP.mult, OP.add)
                    for nh in range(NHH):
                        s2 = slice(nh * 512, (nh + 1) * 512)
                        v.tensor_mul(EKV[:, s2], EKBp[p][:, s2],
                                     PB[a + NHH + nh][:])
                        banks[a + NHH + nh] = ("vector", v.mark())
                    v.wait("vector", k.ch["vector"].val)
                    ia = 0.0 if th == 0 else CARA[:, m:m + 1]
                    v.tensor_tensor_scan(ALPHA, EWB, EKV, ia,
                                         OP.mult, OP.add)
                    ib = 0.0 if th == 0 else CARB[:, m:m + 1]
                    v.tensor_tensor_scan(BETA, EWB, EKBp[p], ib,
                                         OP.mult, OP.add)
                    v.chain()
                    if th == 0:
                        v.tensor_copy(CARA[:, m:m + 1], ALPHA[:, TH - 1:TH])
                        v.tensor_copy(CARB[:, m:m + 1], BETA[:, TH - 1:TH])
                    eu = DRV[:, 16 + m:17 + m]
                    v.scalar_tensor_tensor(NUM[:, 1:TH], EKV[:, 1:TH], eu,
                                           ALPHA[:, 0:TH - 1], OP.mult, OP.add)
                    if th == 0:
                        v.tensor_scalar(NUM[:, 0:1], EKV[:, 0:1], eu,
                                        None, OP.mult)
                    else:
                        v.scalar_tensor_tensor(NUM[:, 0:1], EKV[:, 0:1], eu,
                                               CARA[:, m:m + 1],
                                               OP.mult, OP.add)
                    v.wait("scalar", eku_tok[1])
                    v.tensor_add(DEN[:, 1:TH], EKUp[p][:, 1:TH],
                                 BETA[:, 0:TH - 1])
                    if th == 0:
                        v.tensor_copy(DEN[:, 0:1], EKUp[p][:, 0:1])
                    else:
                        v.tensor_add(DEN[:, 0:1], EKUp[p][:, 0:1],
                                     CARB[:, m:m + 1])
                    v.chain()
                    v.scalar_tensor_tensor(EKV[:], TBF, 1.0, DEN,
                                           OP.add, OP.mult)
                    v1_done[(th, m)] = ("vector", v.mark())
                k.V(v1)
                k.dep(["scalar"], at=tmax(v1_done[(th, m)]))
                def sdiv(s):
                    s.activation(EWB[:], EKV[:], AF.Ln)
                    s.chain()
                    s.activation(DEN[:], EWB[:], AF.Exp, scale=-1.0)
                    s.mark()
                k.S(sdiv)
                rd_tok = ("scalar", k.snap("scalar"))
                at = tmax(rd_tok)
                if th == 1:
                    at = tmax(rd_tok, xhb_free[0], xhb_free[1],
                              ("at1", k.snap("at1")))
                k.dep(["vector"], at=at)
                def v2(v, th=th, m=m):
                    v.tensor_mul(rw_sl(th, m), NUM, DEN)
                    tok = ("vector", v.mark())
                    v2_done[(th, m)] = tok
                    g1_done[(th, m)] = tok
                    rw_done[(th, m)] = tok
                k.V(v2)

        mloop_v = ("vector", k.snap("vector"))
        mloop_s = ("scalar", k.snap("scalar"))
        mloop_g = ("gpsimd", k.snap("gpsimd"))

        x2ring = Ring(k, [SCR[i][:, 0:512] for i in range(4)],
                      ["rr0", "rr1", "rr2", "rr3"], issue_eng="sync",
                      free=[mloop_s, mloop_s, mloop_g, mloop_g])
        wo_oslots = [Xf[4][:, i * 512:(i + 1) * 512] for i in range(4)]
        wo_ofree = [mloop_v, mloop_v, mloop_v, mloop_v]

        ri2 = 0
        sp1_done = {}
        for th in range(2):
            for mo in range(8):
                a = (mo % 2) * 4
                at = tmax(*((("gd_wo", 32),) if (th, mo) == (0, 0) else ()),
                          *((rw_done[(th, kk)] for kk in range(8))
                            if mo == 0 else ()))
                k.dep(["tensor"], at=at or None)
                def mwo(t, th=th, mo=mo, a=a):
                    bank_wait(t, [a + i for i in range(NHH)])
                    for nh in range(NHH):
                        s2 = slice(nh * 512, (nh + 1) * 512)
                        for kk in range(KT):
                            t.matmul(PB[a + nh][:], wo_sl(kk, mo),
                                     rw_sl(th, kk)[:, s2],
                                     start=(kk == 0), stop=(kk == KT - 1))
                    t.mark()
                k.T(mwo)
                wo_tok = ("tensor", k.snap("tensor"))
                for nh in range(NHH):
                    s2a = slice(th * TH + nh * 512, th * TH + (nh + 1) * 512)
                    tts = range((th * TH + nh * 512) // FT,
                                (th * TH + (nh + 1) * 512) // FT)
                    slot, tok = x2ring.load(
                        ri2, SP3[mo * 128:(mo + 1) * 128, s2a],
                        gates=[tmax(*(sp3_done[(mo, tt)] for tt in tts))])
                    osl = wo_oslots[ri2 % 4]
                    k.dep(["vector"], at=tmax(wo_tok, tok,
                                              wo_ofree[ri2 % 4]))
                    def awo(v, a=a, nh=nh, osl=osl, slot=slot):
                        v.tensor_add(osl, PB[a + nh][:], slot)
                        banks[a + nh] = ("vector", v.mark())
                    k.V(awo)
                    vtok = ("vector", k.snap("vector"))
                    x2ring.release(ri2, vtok)
                    obch = "ob%d" % (ri2 % 4)
                    k.dep(["sync"], at=tmax(vtok))
                    def owr(y, mo=mo, s2a=s2a, osl=osl, obch=obch):
                        y.dma(obch, SP1[mo * 128:(mo + 1) * 128, s2a], osl)
                    k.Y(owr)
                    wo_ofree[ri2 % 4] = (obch, k.snap(obch))
                    sp1_done[(mo, th, nh)] = (obch, k.snap(obch))
                    ri2 += 1
            if th == 0:
                wo_th0 = ("tensor", k.snap("tensor"))

        phaseC_T = ("tensor", k.snap("tensor"))

        gdma("fw1", w8_out(Xf[0]), w8_in(FW1, 0), gate=[phaseC_T])
        gdma("fw1", w8_out(Xf[1]), w8_in(FW1, 1))
        gdma("fa1", w8_out(Xf[2]), w8_in(FA1, 0))
        gdma("fa1", w8_out(Xf[3]), w8_in(FA1, 1))
        gdma("fa2", w8_out(Xf[5]), w8_in(FA2, 0),
             gate=[mloop_v, mloop_g, ("vector", k.snap("vector"))])
        gdma("fa2", w8_out(Xf[6]), w8_in(FA2, 1))

        def fw1_sl(kk, m):
            return bc16(Xf[kk // 4])[:, (kk % 4) * D + m * 128:
                                     (kk % 4) * D + (m + 1) * 128]

        def fa1_sl(kk, m):
            return bc16(Xf[2 + kk // 4])[:, (kk % 4) * D + m * 128:
                                         (kk % 4) * D + (m + 1) * 128]

        def fa2_sl(kk, m):
            return bc16(Xf[5 + kk // 4])[:, (kk % 4) * D + m * 128:
                                         (kk % 4) * D + (m + 1) * 128]

        qring.free = [phaseC_T, phaseC_T]

        def ln4_gate(kk, h2):
            toks = []
            for j in range(CH // 512):
                pos = h2 * CH + j * 512
                toks.append(sp1_done[(kk, pos // TH, (pos % TH) // 512)])
            return tmax(*toks)

        XBF32 = _ap(XB0).bitcast(F32)
        ln4ring = Ring(k, [XBF32[:, 4096:5120], XBF32[:, 5120:6144],
                           XBF32[:, 6144:7168]],
                       ["l40", "l41", "l42"], issue_eng="sync",
                       free=[wo_th0, wo_th0, wo_th0])
        ln4_rows = ln_stream_stats(
            lambda kk: SP1[kk * 128:(kk + 1) * 128, :],
            ln4_gate, EX[0], EX[1], ln4ring, stage="sg",
            stage_free=(wo_th0, wo_th0),
            tiles=(XBF32[:, 7168:7680], XBF32[:, 7680:8192]))

        xh4_tok = {}
        li4 = 0
        for h2 in range(NCH):
            for kk in range(KT):
                slot, tok = qring.load(li4, SP1[kk * 128:(kk + 1) * 128,
                                               h2 * CH:(h2 + 1) * CH])
                eng = "vector" if li4 % 4 != 3 else "gpsimd"
                k.dep([eng], at=tmax(tok, ln4_rows))
                def xh4(e, kk=kk, h2=h2, slot=slot):
                    hs = slice(h2 * CH, (h2 + 1) * CH)
                    e.tensor_sub(slot, slot, EX[0][:, hs])
                    e.chain()
                    e.tensor_mul(XBF[kk][:, hs], slot, EX[1][:, hs])
                    e.mark()
                k.body(eng, xh4)
                tok2 = (eng, k.snap(eng))
                qring.release(li4, tok2)
                xh4_tok[(kk, h2)] = tok2
                li4 += 1

        def h1_sl(kk):
            return bc16(Hf[kk // 2])[:, (kk % 2) * T:(kk % 2) * T + T]

        def h2_sl(kk):
            return bc16(Hf[4 + kk // 2])[:, (kk % 2) * T:(kk % 2) * T + T]

        NJ = CH // 512
        em1_tok = {}
        for h2 in range(NCH):
            for m in range(8):
                bs = (m % 2) * 4 + (h2 % 2) * 2
                at = None
                if m == 0:
                    at = tmax(*(xh4_tok[(kk, h2)] for kk in range(KT)),
                              *((("gd_fw1", 32),) if h2 == 0 else ()))
                k.dep(["tensor"], at=at)
                def mm1(t, m=m, bs=bs, h2=h2):
                    bank_wait(t, [bs + j for j in range(NJ)])
                    for j in range(NJ):
                        n = h2 * NJ + j
                        sl = slice(n * NSZ, (n + 1) * NSZ)
                        for kk in range(KT):
                            t.matmul(PB[bs + j][:], fw1_sl(kk, m),
                                     XBF[kk][:, sl], start=(kk == 0),
                                     stop=(kk == KT - 1))
                    t.mark()
                k.T(mm1)
                mtok = ("tensor", k.snap("tensor"))
                k.dep(["vector"], at=tmax(mtok))
                def em1(v, m=m, bs=bs, h2=h2):
                    for j in range(NJ):
                        n = h2 * NJ + j
                        sl = slice(n * NSZ, (n + 1) * NSZ)
                        v.tensor_copy(h1_sl(m)[:, sl], PB[bs + j][:])
                    tok = ("vector", v.mark())
                    for j in range(NJ):
                        banks[bs + j] = tok
                    em1_tok[(h2, m)] = tok
                k.V(em1)

        em2_tok = {}
        for h2 in range(NCH):
            for m in range(8):
                bs = (m % 2) * 4 + (h2 % 2) * 2
                at = None
                if m == 0:
                    at = tmax(*(em1_tok[(h2, mm)] for mm in range(8)),
                              *((("gd_fa1", 32),) if h2 == 0 else ()))
                k.dep(["tensor"], at=at)
                def mm2(t, m=m, bs=bs, h2=h2):
                    bank_wait(t, [bs + j for j in range(NJ)])
                    for j in range(NJ):
                        n = h2 * NJ + j
                        sl = slice(n * NSZ, (n + 1) * NSZ)
                        for kk in range(KT):
                            t.matmul(PB[bs + j][:], fa1_sl(kk, m),
                                     h1_sl(kk)[:, sl], start=(kk == 0),
                                     stop=(kk == KT - 1))
                    t.mark()
                k.T(mm2)
                mtok = ("tensor", k.snap("tensor"))
                k.dep(["vector"], at=tmax(mtok))
                def em2(v, m=m, bs=bs, h2=h2):
                    for j in range(NJ):
                        n = h2 * NJ + j
                        sl = slice(n * NSZ, (n + 1) * NSZ)
                        v.tensor_copy(h2_sl(m)[:, sl], PB[bs + j][:])
                    tok = ("vector", v.mark())
                    for j in range(NJ):
                        banks[bs + j] = tok
                    em2_tok[(h2, m)] = tok
                k.V(em2)

        x3ring = Ring(k, [EXf[2][:, i * 512:(i + 1) * 512] for i in range(4)],
                      ["rr0", "rr1", "rr2", "rr3"], issue_eng="sync",
                      free=[phaseC_T] * 4)
        out_slots = [Xf[4][:, i * 512:(i + 1) * 512] for i in range(4)]
        out_free = [("ob%d" % i, k.snap("ob%d" % i)) for i in range(4)]
        ri3 = 0
        oi3 = 0
        for h2 in range(NCH):
            for m in range(8):
                bs = (m % 2) * 4 + (h2 % 2) * 2
                at = None
                if m == 0:
                    at = tmax(*(em2_tok[(h2, mm)] for mm in range(8)),
                              *((("gd_fa2", 32),) if h2 == 0 else ()))
                k.dep(["tensor"], at=at)
                def mm3(t, m=m, bs=bs, h2=h2):
                    bank_wait(t, [bs + j for j in range(NJ)])
                    for j in range(NJ):
                        n = h2 * NJ + j
                        sl = slice(n * NSZ, (n + 1) * NSZ)
                        for kk in range(KT):
                            t.matmul(PB[bs + j][:], fa2_sl(kk, m),
                                     h2_sl(kk)[:, sl], start=(kk == 0),
                                     stop=(kk == KT - 1))
                    t.mark()
                k.T(mm3)
                mtok = ("tensor", k.snap("tensor"))
                slots = []
                for j in range(NJ):
                    n = h2 * NJ + j
                    sl = slice(n * NSZ, (n + 1) * NSZ)
                    slots.append((ri3 + j, x3ring.load(
                        ri3 + j, SP1[m * 128:(m + 1) * 128, sl],
                        gates=[sp1_done[(m, n // NHH, n % NHH)]])))
                ri3 += NJ
                k.dep(["scalar"],
                      at=tmax(mtok, *(out_free[(oi3 + j) % 4]
                                      for j in range(NJ))))
                def em3(s, m=m, bs=bs, oi3=oi3):
                    for j in range(NJ):
                        s.activation(out_slots[(oi3 + j) % 4], PB[bs + j][:],
                                     AF.Tanh, bias=SM[:, 40 + m:41 + m],
                                     scale=1.0)
                    tok2 = ("scalar", s.mark())
                    for j in range(NJ):
                        banks[bs + j] = tok2
                k.S(em3)
                stok = ("scalar", k.snap("scalar"))
                for j in range(NJ):
                    n = h2 * NJ + j
                    sl = slice(n * NSZ, (n + 1) * NSZ)
                    ridx, (slot, tok) = slots[j]
                    osl = out_slots[(oi3 + j) % 4]
                    k.dep(["vector"], at=tmax(stok, tok))
                    def am3(v, osl=osl, slot=slot):
                        v.tensor_add(osl, osl, slot)
                        v.mark()
                    k.V(am3)
                    vtok = ("vector", k.snap("vector"))
                    x3ring.release(ridx, vtok)
                    obch = "ob%d" % ((oi3 + j) % 4)
                    k.dep(["sync"], at=tmax(vtok))
                    def owr(y, m=m, sl=sl, osl=osl, obch=obch):
                        y.dma(obch, YT[m * 128:(m + 1) * 128, sl], osl)
                    k.Y(owr)
                    out_free[(oi3 + j) % 4] = (obch, k.snap(obch))
                oi3 += NJ

        fin_vals = {name: c.val for name, c in k.ch.items()}
        def fin(y):
            for name, v in sorted(fin_vals.items()):
                if v > 0:
                    y.wait_ge(k.ch[name].sem, v)
        k.Y(fin)

    return nc


def _prep_inputs(inputs, b, T=2048):
    c = np.ascontiguousarray
    f = np.float32

    def col(v, n):
        return np.asarray(v, f).reshape(n, 128).T

    sm = np.zeros((128, 120), f)
    sm[:, 0:8] = col(inputs["L0_attn_bz"], 8)
    sm[:, 8:16] = col(inputs["L0_attn_bh"], 8)
    sm[:, 16:24] = col(inputs["L0_attn_gamma"], 8)
    sm[:, 24:32] = col(inputs["L0_attn_beta"], 8)
    sm[:, 32:40] = col(inputs["L0_ffn_b_proj"], 8)
    sm[:, 40:48] = col(inputs["L1_ffn_bias"], 8)
    sm[:, 48:56] = col(inputs["L1_attn_time_decay"], 8)
    sm[:, 56:64] = col(inputs["L1_attn_time_first"], 8)
    sm[:, 64:72] = col(np.asarray(inputs["L1_attn_time_mix_k"]).reshape(-1), 8)
    sm[:, 72:80] = col(np.asarray(inputs["L1_attn_time_mix_v"]).reshape(-1), 8)
    sm[:, 80:88] = col(np.asarray(inputs["L1_attn_time_mix_r"]).reshape(-1), 8)
    sm[:, 88:120] = col(inputs["L0_ffn_b_fc"], 32)

    m = dict(
        XT=c(np.asarray(inputs["L0_attn_x"], f)[b].T[:, 0:T]),
        WZ=c(np.asarray(inputs["L0_attn_Wz"], f)),
        WH=c(np.asarray(inputs["L0_attn_Wh"], f)),
        WFCT=c(np.asarray(inputs["L0_ffn_W_fc"], f).T),
        WPRT=c(np.asarray(inputs["L0_ffn_W_proj"], f).T),
        AFC=c(np.asarray(inputs["L0_ffn_A_fc"], f)),
        BFCT=c(np.asarray(inputs["L0_ffn_B_fc"], f).T),
        APR=c(np.asarray(inputs["L0_ffn_A_proj"], f)),
        BPRT=c(np.asarray(inputs["L0_ffn_B_proj"], f).T),
        WKT=c(np.asarray(inputs["L1_attn_Wk"], f).T),
        WVT=c(np.asarray(inputs["L1_attn_Wv"], f).T),
        WRT=c(np.asarray(inputs["L1_attn_Wr"], f).T),
        WOT=c(np.asarray(inputs["L1_attn_Wo"], f).T),
        FW1=c(np.asarray(inputs["L1_ffn_weight"], f)),
        FA1=c(np.asarray(inputs["L1_ffn_aux_weight1"], f)),
        FA2=c(np.asarray(inputs["L1_ffn_aux_weight2"], f)),
        SM=sm,
        IDNI=np.eye(128, dtype=f),
    )
    return m


def kernel(**inputs):
    nc = build(T=2048, debug=False)
    in_maps = [_prep_inputs(inputs, b) for b in range(8)]
    res = run_bass_kernel_spmd(nc, in_maps, core_ids=list(range(8)))
    out = np.stack([np.asarray(r["YT"], np.float32).T for r in res.results],
                   axis=0)
    return out


# version 34 (final)
# speedup vs baseline: 1.1418x; 1.0359x over previous
import numpy as np
import concourse.bass as bass
import concourse.mybir as mybir
from concourse.bass_utils import run_bass_kernel_spmd

F32 = mybir.dt.float32
BF16 = mybir.dt.bfloat16
I32 = mybir.dt.int32
AF = mybir.ActivationFunctionType
OP = mybir.AluOpType
AX = mybir.AxisListType

D = 1024
KT = 8
LN_EPS = 1e-5
LN2C = 0.6931471805599453

ENGS = ("tensor", "vector", "scalar", "gpsimd", "sync")


class Chan:
    __slots__ = ("sem", "val")


class Eng:
    def __init__(self, K, eng, name):
        self._K = K
        self._e = eng
        self._n = name
        self.last = None

    def __getattr__(self, a):
        f = getattr(self._e, a)
        if not callable(f):
            return f

        def g(*args, **kw):
            ins = f(*args, **kw)
            if a == "wait_ge":
                return ins
            if ins is not None and hasattr(ins, "then_inc"):
                self.last = ins
            return ins

        return g

    def mark(self):
        K = self._K
        c = K.ch[self._n]
        assert self.last is not None, "mark with no pending instruction"
        self.last.then_inc(c.sem, 1)
        c.val += 1
        self.last = None
        return c.val

    def chain(self):
        v = self.mark()
        self._e.wait_ge(self._K.ch[self._n].sem, v)

    def wait(self, name, val):
        if val > 0:
            self._e.wait_ge(self._K.ch[name].sem, val)

    def dma(self, name, out, in_, **kw):
        K = self._K
        c = K.ch[name]
        ins = self._e.dma_start(out=out, in_=in_, **kw)
        ins.then_inc(c.sem, 16)
        c.val += 16
        self.last = None
        return c.val


class K:
    def __init__(self, nc, blk):
        self.nc = nc
        self.blk = blk
        self.ch = {}
        for n in ENGS:
            self.chan(n)
        self.pend = {n: {} for n in ENGS}

    def chan(self, name):
        if name in self.ch:
            return self.ch[name]
        c = Chan()
        c.sem = self.nc.alloc_semaphore("ch_" + name)
        c.val = 0
        self.ch[name] = c
        return c

    def dep(self, cons, prods=None, at=None):
        for cn in cons:
            d = self.pend[cn]
            if prods:
                for p in prods:
                    v = self.ch[p].val
                    if v > d.get(p, 0):
                        d[p] = v
            if at:
                for p, v in at.items():
                    if v > d.get(p, 0):
                        d[p] = v

    def body(self, name, fn):
        def run(eng):
            ctx = Eng(self, eng, name)
            pend = self.pend[name]
            own = self.ch[name]
            if own.val > pend.get(name, 0):
                pend[name] = own.val
            if pend:
                for p, v in sorted(pend.items()):
                    if v > 0:
                        eng.wait_ge(self.ch[p].sem, v)
                pend.clear()
            fn(ctx)
            if ctx.last is not None:
                ctx.mark()
        getattr(self.blk, name)(run)

    def T(self, fn): self.body("tensor", fn)
    def V(self, fn): self.body("vector", fn)
    def S(self, fn): self.body("scalar", fn)
    def G(self, fn): self.body("gpsimd", fn)
    def Y(self, fn): self.body("sync", fn)

    def snap(self, name):
        return self.ch[name].val


def _ap(t):
    return t if isinstance(t, bass.AP) else t[:]


def bc16(t):
    return _ap(t).bitcast(BF16)


def bci32(t):
    return _ap(t).bitcast(I32)


def tmax(*toks):
    at = {}
    for t in toks:
        if t is None:
            continue
        if isinstance(t, dict):
            for p, v in t.items():
                at[p] = max(at.get(p, 0), v)
        else:
            at[t[0]] = max(at.get(t[0], 0), t[1])
    return at


class Ring:
    def __init__(self, k, slots, chans, issue_eng="sync", free=None):
        self.k = k
        self.slots = slots
        self.chans = chans
        self.eng = issue_eng
        self.free = list(free) if free else [None] * len(slots)

    def load(self, i, in_, width=None, gates=None):
        k = self.k
        i = i % len(self.slots)
        slot = self.slots[i] if width is None else self.slots[i][:, 0:width]
        ch = self.chans[i]
        at = tmax(self.free[i], *(gates or []))
        if at:
            k.dep([self.eng], at=at)
        def dma(y, slot=slot, ch=ch, in_=in_):
            y.dma(ch, slot, in_)
        k.body(self.eng, dma)
        return slot, (ch, k.snap(ch))

    def release(self, i, token):
        self.free[i % len(self.slots)] = token


def build(T=2048, debug=False):
    nc = bass.Bass("TRN2", target_bir_lowering=False)
    NSZ = 512
    NT = T // NSZ
    TH = T // 2
    NHH = TH // 512
    FT = min(512, T)
    NFT = T // FT
    CH = min(1024, T)
    NCH = T // CH

    def din(name, shape, dt=F32):
        return nc.dram_tensor(name, shape, dt, kind="ExternalInput").ap()

    XT = din("XT", [D, T])
    WZ = din("WZ", [D, D]); WH = din("WH", [D, D])
    WFCT = din("WFCT", [D, 4096]); WPRT = din("WPRT", [4096, D])
    AFC = din("AFC", [16, D]); BFCT = din("BFCT", [16, 4096])
    APR = din("APR", [16, 4096]); BPRT = din("BPRT", [16, D])
    WKT = din("WKT", [D, D]); WVT = din("WVT", [D, D])
    WRT = din("WRT", [D, D]); WOT = din("WOT", [D, D])
    FW1 = din("FW1", [D, D]); FA1 = din("FA1", [D, D]); FA2 = din("FA2", [D, D])
    SM_IN = din("SM", [128, 120])
    IDNI = din("IDNI", [128, 128])

    spk = "ExternalOutput" if debug else "Internal"
    SP1 = nc.dram_tensor("SP1", [D, T], F32, kind=spk).ap()
    SP2 = nc.dram_tensor("SP2", [D, T], F32, kind=spk).ap()
    SP3 = nc.dram_tensor("SP3", [D, T], F32, kind=spk).ap()
    YT = nc.dram_tensor("YT", [D, T], F32, kind="ExternalOutput").ap()

    RT = 2048
    X0 = nc.alloc_sbuf_tensor("X0", [128, 8 * RT], F32)
    H0 = nc.alloc_sbuf_tensor("H0", [128, 8 * RT], F32)
    XB0 = nc.alloc_sbuf_tensor("XB0", [128, 8 * RT], BF16)
    E0 = nc.alloc_sbuf_tensor("E0", [128, 4 * RT], F32)
    SC = nc.alloc_sbuf_tensor("SC", [128, 2560], F32)
    AT0 = nc.alloc_sbuf_tensor("AT0", [128, 1024], F32)
    SM = nc.alloc_sbuf_tensor("SMT", [128, 120], F32)
    DRV = nc.alloc_sbuf_tensor("DRV", [128, 40], F32)
    QS = nc.alloc_sbuf_tensor("QS", [128, 12], F32)
    MX = nc.alloc_sbuf_tensor("MX", [128, 16], F32)
    ONESB = nc.alloc_sbuf_tensor("ONESB", [128, 128], BF16)
    IDN = nc.alloc_sbuf_tensor("IDN", [128, 128], BF16)
    ASL = nc.alloc_sbuf_tensor("ASL", [16, 256], BF16)
    ABACK = nc.alloc_sbuf_tensor("ABACK", [128, 8], F32)
    CARA = nc.alloc_sbuf_tensor("CARA", [128, 8], F32)
    CARB = nc.alloc_sbuf_tensor("CARB", [128, 8], F32)

    Xr = [X0[:, i * RT:i * RT + T] for i in range(8)]
    Xf = [X0[:, i * RT:(i + 1) * RT] for i in range(8)]
    Hr = [H0[:, i * RT:i * RT + T] for i in range(8)]
    Hf = [H0[:, i * RT:(i + 1) * RT] for i in range(8)]
    XBF = [XB0[:, i * RT:i * RT + T] for i in range(8)]
    XBFf = [XB0[:, i * RT:(i + 1) * RT] for i in range(8)]
    EX = [E0[:, i * RT:i * RT + T] for i in range(4)]
    EXf = [E0[:, i * RT:(i + 1) * RT] for i in range(4)]
    SCR = [SC[:, i * 512:(i + 1) * 512] for i in range(5)]

    PB = [nc.alloc_psum_tensor(f"PB{i}", [128, NSZ], F32) for i in range(8)]

    with nc.Block() as blk:
        k = K(nc, blk)

        for cn in ("xin0", "xin1", "smalls", "spx01", "spx23", "spx45",
                   "sp2w0", "sp2w1", "sp2w2", "sp2w3",
                   "sp2w4", "sp2w5", "sp2w6", "sp2w7", "yt",
                   "at0", "at1", "rr0", "rr1", "rr2", "rr3",
                   "ob0", "ob1", "ob2", "ob3",
                   "qp0", "qp1", "ar0", "ar1", "l40", "l41", "l42"):
            k.chan(cn)
        for wn in ("wz", "wh", "afcb", "aprb", "bfc", "bpr", "idn",
                   "wk", "wv", "wr", "wo", "fw1", "fa1", "fa2"):
            k.chan("gd_" + wn)

        banks = [None] * 8

        def bank_wait(ctx, idxs):
            for i in idxs:
                if banks[i] is not None:
                    ctx.wait(*banks[i])

        def gdma(wn, out, in_, gate=None, **kw):
            c = k.ch["gd_" + wn]
            at = tmax(*(gate or []))
            if at:
                k.dep(["gpsimd"], at=at)
            def f(g):
                ins = g.dma_start(out=out, in_=in_, **kw)
                ins.then_inc(c.sem, 16)
                g.last = None
            k.G(f)
            c.val += 16
            return ("gd_" + wn, c.val)

        def w8_in(hbm, half):
            return hbm[half * 512:(half + 1) * 512, :].rearrange(
                "(a p) m -> p a m", p=128)

        def w8_out(region):
            return bc16(region).rearrange("p (a m) -> p a m", a=4)

        def ln_stats(src, tokens, MB, RB, stage="v"):
            slot_tok = [None, None]
            si = 0
            tiles = [(kk, n) for kk in range(KT) for n in range(NT)]
            if stage == "sg":
                for p0 in range(0, len(tiles), 2):
                    pair = tiles[p0:p0 + 2]
                    at = tmax(slot_tok[0], slot_tok[1],
                              *(tokens(kk, n) for kk, n in pair
                                if tokens))
                    k.dep(["scalar"], at=at or None)
                    def cst(s, pair=pair, p0=p0):
                        for j, (kk, n) in enumerate(pair):
                            sb = bc16(SCR[(p0 + j) % 2])
                            s.activation(sb[:, 0:512], src(kk, n), AF.Copy)
                        s.mark()
                    k.S(cst)
                    k.dep(["gpsimd"], at=at or None)
                    def sqg(g, pair=pair, p0=p0):
                        for j, (kk, n) in enumerate(pair):
                            sb = bc16(SCR[(p0 + j) % 2])
                            g.tensor_mul(sb[:, 512:1024], src(kk, n),
                                         src(kk, n))
                        g.mark()
                    k.G(sqg)
                    k.dep(["tensor"], ["scalar", "gpsimd"])
                    def mms2(t, pair=pair, p0=p0):
                        for j, (kk, n) in enumerate(pair):
                            sb = bc16(SCR[(p0 + j) % 2])
                            if kk == 0:
                                bank_wait(t, [n, NT + n])
                            t.matmul(PB[n][:], ONESB[:], sb[:, 0:512],
                                     start=(kk == 0), stop=(kk == KT - 1))
                            t.matmul(PB[NT + n][:], ONESB[:],
                                     sb[:, 512:1024],
                                     start=(kk == 0), stop=(kk == KT - 1))
                            slot_tok[(p0 + j) % 2] = ("tensor", t.mark())
                    k.T(mms2)
                return _ln_rows(MB, RB)
            for kk in range(KT):
                for n in range(NT):
                    sb = bc16(SCR[si % 2])
                    tok = tokens(kk, n) if tokens else None
                    at = tmax(slot_tok[si % 2], tok)
                    k.dep(["vector"], at=at or None)
                    def cst(v, kk=kk, n=n, sb=sb):
                        v.tensor_copy(sb[:, 0:512], src(kk, n))
                        v.tensor_mul(sb[:, 512:1024], src(kk, n), src(kk, n))
                        v.mark()
                    k.V(cst)
                    k.dep(["tensor"], ["vector"])
                    def mms(t, kk=kk, n=n, sb=sb):
                        if kk == 0:
                            bank_wait(t, [n, NT + n])
                        t.matmul(PB[n][:], ONESB[:], sb[:, 0:512],
                                 start=(kk == 0), stop=(kk == KT - 1))
                        t.matmul(PB[NT + n][:], ONESB[:], sb[:, 512:1024],
                                 start=(kk == 0), stop=(kk == KT - 1))
                        slot_tok[si % 2] = ("tensor", t.mark())
                    k.T(mms)
                    si += 1
            return _ln_rows(MB, RB)

        def _ln_rows(MB, RB):
            k.dep(["vector"], ["tensor"])
            def rows_v(v):
                for n in range(NT):
                    sl = slice(n * NSZ, (n + 1) * NSZ)
                    v.tensor_scalar(MB[:, sl], PB[n][:], 1.0 / D, None, OP.mult)
                    v.tensor_scalar(RB[:, sl], PB[NT + n][:], 1.0 / D, LN_EPS,
                                    OP.mult, OP.add)
                v.chain()
                for n in range(NT):
                    sl = slice(n * NSZ, (n + 1) * NSZ)
                    v.tensor_mul(SCR[0][:], MB[:, sl], MB[:, sl])
                    v.chain()
                    v.tensor_sub(SCR[1][:], RB[:, sl], SCR[0][:])
                    v.chain()
                    v.reciprocal(RB[:, sl], SCR[1][:])
                v.chain()
            k.V(rows_v)
            vtok = ("vector", k.snap("vector"))
            for i in range(2 * NT):
                banks[i] = vtok
            k.dep(["scalar"], ["vector"])
            def rows_s(s):
                for n in range(NT):
                    sl = slice(n * NSZ, (n + 1) * NSZ)
                    s.activation(RB[:, sl], RB[:, sl], AF.Sqrt)
                    s.chain()
            k.S(rows_s)
            return ("scalar", k.snap("scalar"))

        def ln_stream_stats(row_hbm, data_gate, MB, RB, ring,
                            stage="v",
                            stage_free=(None, None), tiles=None):
            slot_mm = [stage_free[0], stage_free[1]]
            li = 0
            si = 0
            for kk in range(KT):
                for h2 in range(NCH):
                    slot, tok = ring.load(
                        li, row_hbm(kk)[:, h2 * CH:(h2 + 1) * CH],
                        gates=None if data_gate is None
                        else [data_gate(kk, h2)])
                    last_mm = None
                    for j in range(CH // 512):
                        n = h2 * (CH // 512) + j
                        sb = bc16(_ap(tiles[si % 2]) if tiles
                                  else SCR[si % 2])
                        src = slot[:, j * 512:(j + 1) * 512]
                        at = tmax(slot_mm[si % 2], tok)
                        if stage == "sg":
                            k.dep(["scalar"], at=at or None)
                            def csts(s, sb=sb, src=src):
                                s.activation(sb[:, 0:512], src, AF.Copy)
                                s.mark()
                            k.S(csts)
                            k.dep(["gpsimd"], at=at or None)
                            def sqg(g, sb=sb, src=src):
                                g.tensor_mul(sb[:, 512:1024], src, src)
                                g.mark()
                            k.G(sqg)
                            k.dep(["tensor"], ["scalar", "gpsimd"])
                        else:
                            k.dep(["vector"], at=at or None)
                            def cst(v, sb=sb, src=src):
                                v.tensor_copy(sb[:, 0:512], src)
                                v.tensor_mul(sb[:, 512:1024], src, src)
                                v.mark()
                            k.V(cst)
                            k.dep(["tensor"], ["vector"])
                        def mms(t, kk=kk, n=n, sb=sb, si=si):
                            if kk == 0:
                                bank_wait(t, [n, NT + n])
                            t.matmul(PB[n][:], ONESB[:], sb[:, 0:512],
                                     start=(kk == 0), stop=(kk == KT - 1))
                            t.matmul(PB[NT + n][:], ONESB[:], sb[:, 512:1024],
                                     start=(kk == 0), stop=(kk == KT - 1))
                            slot_mm[si % 2] = ("tensor", t.mark())
                        k.T(mms)
                        last_mm = ("tensor", k.snap("tensor"))
                        si += 1
                    ring.release(li, last_mm)
                    li += 1
            return _ln_rows(MB, RB)

        def p0(y):
            inv = XT.rearrange("(a p) t -> p a t", p=128)
            outv = X0[:].rearrange("p (a t) -> p a t", a=8)[:, :, 0:T]
            y.dma("xin0", outv[:, 0:4, :], inv[:, 0:4, :])
            y.dma("xin1", outv[:, 4:8, :], inv[:, 4:8, :])
            y.dma("smalls", SM[:], SM_IN)
        k.Y(p0)
        def p0v(v):
            v.memset(ONESB[:], 1.0)
            v.memset(ABACK[:], 0.0)
            v.memset(CARA[:], 0.0)
            v.memset(CARB[:], 0.0)
            v.memset(DRV[:, 8:9], -LN2C)
            v.mark()
        k.V(p0v)
        k.dep(["vector", "scalar"], at={"smalls": k.snap("smalls")})
        def p0d(v):
            v.tensor_scalar(DRV[:, 0:8], SM[:, 0:8], -1.0, None, OP.mult)
            v.mark()
        k.V(p0d)
        def p0s(s):
            s.activation(DRV[:, 16:24], SM[:, 56:64], AF.Exp)
            s.chain()
            s.activation(DRV[:, 24:32], SM[:, 48:56], AF.Exp)
            s.mark()
        k.S(p0s)
        def p0d2(s):
            s.activation(DRV[:, 32:40], DRV[:, 24:32], AF.Exp, scale=-1.0)
            s.mark()
        k.S(p0d2)
        smalls_v = ("vector", k.snap("vector"))
        smalls_s = ("scalar", k.snap("scalar"))

        gdma("idn", IDN[:], IDNI)
        gdma("wz", w8_out(Hf[4]), w8_in(WZ, 0))
        gdma("wz", w8_out(Hf[5]), w8_in(WZ, 1))
        gdma("wh", w8_out(Hf[6]), w8_in(WH, 0))
        gdma("wh", w8_out(Hf[7]), w8_in(WH, 1))

        def ln1_tok(kk, n):
            return ("xin0", 16) if kk < 4 else ("xin1", 16)

        k.dep(["tensor"], at=tmax(smalls_v))
        ln1_rows = ln_stats(lambda kk, n: Xr[kk][:, n * NSZ:(n + 1) * NSZ],
                            ln1_tok, EX[0], EX[1])
        stats_T = ("tensor", k.snap("tensor"))

        xhat_tok = {}
        k.dep(["vector", "gpsimd"], at=tmax(ln1_rows))
        def xh_v(v):
            for kk in (0, 1, 2, 4, 5, 6):
                for n in range(NT):
                    sl = slice(n * NSZ, (n + 1) * NSZ)
                    v.tensor_sub(SCR[2][:], Xr[kk][:, sl], EX[0][:, sl])
                    v.chain()
                    v.tensor_mul(XBF[kk][:, sl], SCR[2][:], EX[1][:, sl])
                    v.chain()
                xhat_tok[kk] = ("vector", k.ch["vector"].val)
        k.V(xh_v)
        def xh_g(g):
            for kk in (3, 7):
                for n in range(NT):
                    sl = slice(n * NSZ, (n + 1) * NSZ)
                    g.tensor_sub(SCR[3][:], Xr[kk][:, sl], EX[0][:, sl])
                    g.chain()
                    g.tensor_mul(XBF[kk][:, sl], SCR[3][:], EX[1][:, sl])
                    g.chain()
                xhat_tok[kk] = ("gpsimd", k.ch["gpsimd"].val)
        k.G(xh_g)
        xhat_all = [("vector", k.snap("vector")), ("gpsimd", k.snap("gpsimd"))]

        for pair, ch in ((0, "spx01"), (2, "spx23"), (4, "spx45")):
            k.dep(["scalar"], at={"xin0" if pair < 4 else "xin1": 16})
            def spl(y, pair=pair, ch=ch):
                for kk in (pair, pair + 1):
                    y.dma(ch, SP1[kk * 128:(kk + 1) * 128, 0:T], Xr[kk])
            k.body("scalar", spl)

        def wz_sl(kk, m):
            return bc16(Hf[4 + kk // 4])[:, (kk % 4) * D + m * 128:
                                         (kk % 4) * D + (m + 1) * 128]

        def wh_sl(kk, m):
            return bc16(Hf[6 + kk // 4])[:, (kk % 4) * D + m * 128:
                                         (kk % 4) * D + (m + 1) * 128]

        qring = Ring(k, [AT0[:], EXf[3][:, 0:1024]], ["at0", "at1"])
        QCH = 1024

        class Pass1:

            def __init__(self, hbm, nrows, ping, gate=None):
                self.hbm = hbm
                self.ping = ping
                self.chunks = [(kk, c) for kk in range(nrows)
                               for c in range(hbm.shape[1] // QCH)]
                self.ci = 0
                self.prev = None
                def mz(v, ping=ping):
                    v.memset(MX[:, ping:ping + 1], 0.0)
                    v.mark()
                k.V(mz)

            def emit(self, n):
                end = min(self.ci + n, len(self.chunks))
                while self.ci < end:
                    ci = self.ci
                    kk, c = self.chunks[ci]
                    ping = self.ping
                    slot, tok = qring.load(
                        ci, self.hbm[kk * 128:(kk + 1) * 128,
                                     c * QCH:(c + 1) * QCH])
                    k.dep(["vector"], at=tmax(tok, self.prev))
                    def red(v, ci=ci, slot=slot, ping=ping):
                        v.tensor_reduce(MX[:, 12:13], slot, AX.X,
                                        OP.max, apply_absolute_value=True)
                        v.chain()
                        v.tensor_max(
                            MX[:, ping + (ci + 1) % 2:ping + 1 + (ci + 1) % 2],
                            MX[:, ping + ci % 2:ping + 1 + ci % 2],
                            MX[:, 12:13])
                        qring.release(ci, ("vector", v.mark()))
                    k.V(red)
                    self.prev = ("vector", k.snap("vector"))
                    self.ci += 1

            def fin(self):
                assert self.ci == len(self.chunks)
                return self.ping + (self.ci % 2), self.prev

        def zrow(p): return EXf[2][:, p * 512:(p + 1) * 512]
        def grow(p): return EXf[2][:, (2 + p) * 512:(3 + p) * 512]
        rtmp = SCR[4]

        C_B, V_B = Xf[4], Xf[5]
        HD = [Hr[0], Hr[1], Hr[2], Hr[3], Xr[0], Xr[1], Xr[2], Xr[3]]
        cv_gate = tmax(("spx45", 32), xhat_tok[4], xhat_tok[5], stats_T)

        p1fc = Pass1(WFCT, KT, 8)
        p1pr = Pass1(WPRT, 32, 10)
        p1fc.emit(4)

        last_mm = {}
        scan_done = {}
        evs_done = {}
        evv_done = {}
        ti = 0
        for m in range(8):
            for n in range(NT):
                sl = slice(n * NSZ, (n + 1) * NSZ)
                at = {}
                if m == 0 and n == 0:
                    at = tmax(*xhat_all, ("gd_wz", 32), ("gd_wh", 32))
                k.dep(["tensor"], at=at or None)
                def mmzh(t, m=m, n=n, sl=sl):
                    bank_wait(t, [n, NT + n])
                    for kk in range(KT):
                        t.matmul(PB[n][:], wz_sl(kk, m), XBF[kk][:, sl],
                                 start=(kk == 0), stop=(kk == KT - 1))
                    for kk in range(KT):
                        t.matmul(PB[NT + n][:], wh_sl(kk, m), XBF[kk][:, sl],
                                 start=(kk == 0), stop=(kk == KT - 1))
                    last_mm[(m, n)] = ("tensor", t.mark())
                k.T(mmzh)
                p = ti % 2
                at = tmax(last_mm[(m, n)],
                          cv_gate if (m, n) == (0, 0) else None,
                          smalls_v if (m, n) == (0, 0) else None,
                          smalls_s if (m, n) == (0, 0) else None,
                          scan_done.get(m - 1),
                          evv_done.get(ti - 2))
                k.dep(["scalar"], at=at)
                def ev_s(s, m=m, n=n, sl=sl, p=p):
                    s.activation(C_B[:, sl], PB[n][:], AF.Sigmoid,
                                 bias=DRV[:, m:m + 1], scale=-1.0)
                    s.activation(zrow(p)[:], PB[n][:], AF.Sigmoid,
                                 bias=SM[:, m:m + 1], scale=1.0)
                    s.activation(grow(p)[:], PB[NT + n][:], AF.Sigmoid,
                                 bias=SM[:, 8 + m:9 + m], scale=1.0)
                    evs_done[(m, n)] = ("scalar", s.mark())
                k.S(ev_s)
                k.dep(["vector"], at=tmax(evs_done[(m, n)], last_mm[(m, n)]))
                def ev_v(v, m=m, n=n, sl=sl, p=p, ti=ti):
                    v.tensor_scalar(rtmp[:], PB[NT + n][:], SM[:, 8 + m:9 + m],
                                    0.0, OP.add, OP.max)
                    v.chain()
                    v.scalar_tensor_tensor(grow(p)[:], grow(p)[:], 0.5,
                                           rtmp[:], OP.min, OP.add)
                    v.chain()
                    v.tensor_mul(V_B[:, sl], zrow(p)[:], grow(p)[:])
                    tok = ("vector", v.mark())
                    banks[n] = tok
                    banks[NT + n] = tok
                    evv_done[ti] = tok
                k.V(ev_v)
                ti += 1
            at = tmax(evs_done[(m, NT - 1)])
            if m >= 4:
                at = tmax(at, ("spx%d%d" % (2 * ((m - 4) // 2),
                                            2 * ((m - 4) // 2) + 1), 32),
                          xhat_tok[m - 4], stats_T)
            k.dep(["vector"], at=at)
            def scan(v, m=m):
                v.tensor_tensor_scan(HD[m][:, 0:T], C_B[:, 0:T], V_B[:, 0:T],
                                     0.0, OP.mult, OP.add)
                scan_done[m] = ("vector", v.mark())
            k.V(scan)
            p1fc.emit(4)
            p1pr.emit(4)
        p1fc.emit(64)
        p1pr.emit(64)
        mloop_T = ("tensor", k.snap("tensor"))

        def pass1_fin(fcol, qcol, bank, vtok):
            k.dep(["gpsimd"], at=tmax(vtok))
            def tree(g, fcol=fcol):
                g.tensor_reduce(MX[0:1, 13:14], MX[:, fcol:fcol + 1],
                                AX.C, OP.max)
                g.mark()
            k.G(tree)
            ONE1R = SC[0:1, 2048:2176]
            def onev(v):
                v.memset(ONE1R, 1.0)
                v.mark()
            k.V(onev)
            k.dep(["tensor"], ["gpsimd", "vector"])
            def bcm(t, bank=bank):
                bank_wait(t, [bank])
                t.matmul(PB[bank][:, 0:1], ONE1R, MX[0:1, 13:14],
                         start=True, stop=True)
                t.mark()
            k.T(bcm)
            k.dep(["vector"], ["tensor"])
            def qsc(v, qcol=qcol, bank=bank):
                v.tensor_scalar(QS[:, qcol + 1:qcol + 2], PB[bank][:, 0:1],
                                1.0 / 127.0, None, OP.mult)
                v.chain()
                v.reciprocal(QS[:, qcol + 2:qcol + 3], QS[:, qcol + 1:qcol + 2])
                v.chain()
                v.tensor_scalar(QS[:, qcol + 3:qcol + 4],
                                QS[:, qcol + 1:qcol + 2], -1.0, None, OP.mult)
                v.chain()
                v.tensor_scalar(QS[:, qcol + 4:qcol + 5],
                                QS[:, qcol + 2:qcol + 3], -2.0, None, OP.mult)
                banks[bank] = ("vector", v.mark())
            k.V(qsc)
            return ("vector", k.snap("vector"))

        f_fc, f_fc_tok = p1fc.fin()
        qfc_tok = pass1_fin(f_fc, 0, 7, f_fc_tok)

        lnh_rows = ln_stats(lambda kk, n: HD[kk][:, n * NSZ:(n + 1) * NSZ],
                            lambda kk, n: scan_done[kk], EX[0], EX[1])

        xring = Ring(k, [EXf[2][:, i * 512:(i + 1) * 512] for i in range(4)],
                     ["rr0", "rr1", "rr2", "rr3"], issue_eng="sync",
                     free=[("vector", k.snap("vector"))] * 4)
        comb_done = {}
        ci = 0
        for m in range(8):
            for n in range(NT):
                sl = slice(n * NSZ, (n + 1) * NSZ)
                if m < 6:
                    sch = "spx%d%d" % (2 * (m // 2), 2 * (m // 2) + 1)
                    slot, tok = xring.load(ci, SP1[m * 128:(m + 1) * 128, sl],
                                           gates=[(sch, 32)])
                else:
                    slot, tok = Xr[m][:, sl], None
                eng = "vector" if ci % 4 != 3 else "gpsimd"
                if eng == "vector":
                    k.dep([eng], at=tmax(lnh_rows, scan_done[m], tok))
                    def cmb(e, m=m, sl=sl, slot=slot):
                        tmp = SCR[2]
                        e.tensor_sub(tmp[:], HD[m][:, sl], EX[0][:, sl])
                        e.chain()
                        e.tensor_mul(tmp[:], tmp[:], EX[1][:, sl])
                        e.chain()
                        e.tensor_scalar(tmp[:], tmp[:], SM[:, 16 + m:17 + m],
                                        SM[:, 24 + m:25 + m], OP.mult, OP.add)
                        e.chain()
                        e.tensor_add(tmp[:], tmp[:], XBF[m][:, sl])
                        e.chain()
                        e.tensor_add(Hr[m][:, sl], tmp[:], slot)
                        comb_done[(m, n)] = (eng, e.mark())
                    k.V(cmb)
                else:
                    k.dep([eng], at=tmax(lnh_rows, scan_done[m], tok))
                    def cmb_g1(e, m=m, sl=sl):
                        tmp = SCR[3]
                        e.tensor_sub(tmp[:], HD[m][:, sl], EX[0][:, sl])
                        e.chain()
                        e.tensor_mul(tmp[:], tmp[:], EX[1][:, sl])
                        e.mark()
                    k.G(cmb_g1)
                    k.dep(["vector"], ["gpsimd"])
                    def cmb_s(e, m=m):
                        e.tensor_scalar(SCR[3][:], SCR[3][:],
                                        SM[:, 16 + m:17 + m],
                                        SM[:, 24 + m:25 + m],
                                        OP.mult, OP.add)
                        e.mark()
                    k.V(cmb_s)
                    k.dep(["gpsimd"], ["vector"])
                    def cmb_g2(e, m=m, n=n, sl=sl, slot=slot):
                        tmp = SCR[3]
                        e.tensor_add(tmp[:], tmp[:], XBF[m][:, sl])
                        e.chain()
                        e.tensor_add(Hr[m][:, sl], tmp[:], slot)
                        comb_done[(m, n)] = ("gpsimd", e.mark())
                    k.G(cmb_g2)
                if m < 6:
                    xring.release(ci, comb_done[(m, n)])
                    ci += 1
        comb_m = {m: tmax(*(comb_done[(m, n)] for n in range(NT)))
                  for m in range(8)}

        f_pr, f_pr_tok = p1pr.fin()
        qpr_tok = pass1_fin(f_pr, 6, 7, f_pr_tok)

        sp2_at = {}
        for m in range(8):
            k.dep(["scalar"], at=comb_m[m])
            def spl(y, m=m):
                y.dma("sp2w%d" % m, SP2[m * 128:(m + 1) * 128, 0:T], Hr[m])
            k.body("scalar", spl)
            sp2_at[m] = ("sp2w%d" % m, 16)

        ln2_rows = ln_stats(lambda kk, n: Hr[kk][:, n * NSZ:(n + 1) * NSZ],
                            lambda kk, n: comb_done[(kk, n)], EX[0], EX[1])
        xh2_tok = {}
        k.dep(["vector", "gpsimd"], at=tmax(ln2_rows))
        def xh2_v(v):
            for kk in (0, 1, 2, 4, 5, 6):
                for n in range(NT):
                    sl = slice(n * NSZ, (n + 1) * NSZ)
                    v.tensor_sub(SCR[2][:], Hr[kk][:, sl], EX[0][:, sl])
                    v.chain()
                    v.tensor_mul(XBF[kk][:, sl], SCR[2][:], EX[1][:, sl])
                    v.chain()
                xh2_tok[kk] = ("vector", k.ch["vector"].val)
        k.V(xh2_v)
        def xh2_g(g):
            for kk in (3, 7):
                for n in range(NT):
                    sl = slice(n * NSZ, (n + 1) * NSZ)
                    g.tensor_sub(SCR[3][:], Hr[kk][:, sl], EX[0][:, sl])
                    g.chain()
                    g.tensor_mul(XBF[kk][:, sl], SCR[3][:], EX[1][:, sl])
                    g.chain()
                xh2_tok[kk] = ("gpsimd", k.ch["gpsimd"].val)
        k.G(xh2_g)
        xh2_all = [("vector", k.snap("vector")), ("gpsimd", k.snap("gpsimd"))]

        gdma("bfc", bc16(EXf[2])[0:16, 0:4096], BFCT,
             gate=[comb_m[5], ("vector", k.snap("vector")),
                   ("gpsimd", k.snap("gpsimd"))],
             max_dma_last_dim=8192)
        BSF = bc16(EXf[2])[0:16, 0:4096]
        gdma("bpr", bc16(SC)[0:16, 4096:5120], BPRT,
             gate=[("vector", k.snap("vector")),
                   ("tensor", k.snap("tensor"))])
        BSP = bc16(SC)[0:16, 4096:5120]

        k.dep(["scalar"], at=tmax(("gd_bfc", 16), qfc_tok))
        def bsf_s(s):
            s.activation(BSF, BSF, AF.Copy, scale=QS[0:16, 4:5])
            s.mark()
        k.S(bsf_s)
        bsf_tok = ("scalar", k.snap("scalar"))
        k.dep(["scalar"], at=tmax(("gd_bpr", 16), qpr_tok))
        def bsp_s(s):
            s.activation(BSP, BSP, AF.Copy, scale=QS[0:16, 10:11])
            s.mark()
        k.S(bsp_s)
        bsp_tok = ("scalar", k.snap("scalar"))

        def wfc_sl(kk, m):
            return bc16(Xf[kk])[:, m * 128:(m + 1) * 128]

        def wpr_sl(kk):
            return bc16(Hf[kk // 4])[:, (kk % 4) * D:(kk % 4 + 1) * D]

        def hm_sl(kk):
            return bc16(_ap(E0))[:, kk * FT:(kk + 1) * FT]

        x_free = {0: comb_m[4], 1: comb_m[5], 2: comb_m[6], 3: comb_m[7],
                  4: scan_done[7], 5: scan_done[7],
                  6: comb_m[6], 7: comb_m[7]}

        IQF = bci32(EXf[3][:, 1024:2048])
        IQP = bci32(AT0[:, 0:1024])
        abk = [0]

        aring = Ring(k, [ASL[0:16, 0:128], ASL[0:16, 128:256]],
                     ["ar0", "ar1"], issue_eng="gpsimd")
        ali = [0]

        def merge_chunk(kk, c, dst, bs_ap, a_hbm, qv_tok, first, evac,
                        bs_tok, merged):
            asl, atok = aring.load(ali[0],
                                   a_hbm[0:16, kk * 128:(kk + 1) * 128])
            for h in range(2):
                b = abk[0] % 4
                abk[0] += 1
                dgr = dst(kk, c)[:, h * 512:(h + 1) * 512]
                bgr = bs_ap[0:16, c * 1024 + h * 512:c * 1024 + (h + 1) * 512]
                at = tmax(qv_tok, atok,
                          *([bs_tok, ("gd_idn", 16)] if first[0] else []))
                first[0] = False
                if evac == "s":
                    k.dep(["tensor"], at=at)
                    def abm(t, b=b, dgr=dgr, bgr=bgr, asl=asl):
                        bank_wait(t, [b])
                        t.matmul(PB[b][:], asl, bgr, start=True, stop=False)
                        t.matmul(PB[b][:], IDN[:], dgr, start=False,
                                 stop=True)
                        t.mark()
                    k.T(abm)
                    k.dep(["scalar"], ["tensor"])
                    def mev(s, b=b, dgr=dgr):
                        s.activation(dgr, PB[b][:], AF.Copy)
                        banks[b] = ("scalar", s.mark())
                    k.S(mev)
                    merged[(kk, 2 * c + h)] = ("scalar", k.snap("scalar"))
                else:
                    k.dep(["tensor"], at=at)
                    def abm2(t, b=b, bgr=bgr, asl=asl):
                        bank_wait(t, [b])
                        t.matmul(PB[b][:], asl, bgr, start=True, stop=True)
                        t.mark()
                    k.T(abm2)
                    k.dep(["vector"], ["tensor"])
                    def mev2(v, b=b, dgr=dgr):
                        v.scalar_tensor_tensor(dgr, PB[b][:], 1.0,
                                               dgr, OP.mult, OP.add)
                        banks[b] = ("vector", v.mark())
                    k.V(mev2)
                    merged[(kk, 2 * c + h)] = ("vector", k.snap("vector"))
            aring.release(ali[0], ("tensor", k.snap("tensor")))
            ali[0] += 1

        def pass2(hbm, order, dst, gates, qtok, qcol, ring, ring_i0, iq,
                  deferred):
            for i, (kk, c) in enumerate(order):
                slot, tok = ring.load(ring_i0 + i,
                                      hbm[kk * 128:(kk + 1) * 128,
                                          c * QCH:(c + 1) * QCH])
                k.dep(["vector"], at=tmax(tok, qtok, *gates(kk)))
                def qv(e, slot=slot, kk=kk, c=c, iq=iq, qcol=qcol, dst=dst):
                    e.tensor_scalar(iq, slot, QS[:, qcol + 2:qcol + 3],
                                    None, OP.mult)
                    e.chain()
                    e.scalar_tensor_tensor(slot, slot,
                                           QS[:, qcol + 2:qcol + 3], iq,
                                           OP.mult, OP.subtract)
                    e.chain()
                    e.scalar_tensor_tensor(iq, slot, 0.5, iq,
                                           OP.is_ge, OP.add)
                    e.chain()
                    e.scalar_tensor_tensor(dst(kk, c), slot, -0.5, iq,
                                           OP.is_le, OP.subtract)
                    e.mark()
                k.V(qv)
                qv_tok = ("vector", k.snap("vector"))
                ring.release(ring_i0 + i, qv_tok)
                deferred.append((kk, c, qv_tok))

        fcm_ready = {}
        fc_first = [True]
        fc_dst = lambda kk, c: bc16(Xf[kk])[:, c * 1024:(c + 1) * 1024]
        fcq = []
        pass2(WFCT, [(kk, c) for c in range(4) for kk in range(8)],
              fc_dst, lambda kk: [x_free[kk]], qfc_tok, 0, qring, 0, IQF, fcq)
        for (kk, c, qv_tok) in fcq:
            merge_chunk(kk, c, fc_dst, BSF, AFC, qv_tok, fc_first, "s",
                        bsf_tok, fcm_ready)

        ring2_free = tmax(ln2_rows, *xh2_all)
        ring2 = Ring(k, [SC[:, 0:1024], SC[:, 1024:2048]], ["qp0", "qp1"],
                     issue_eng="sync", free=[ring2_free, ring2_free])

        prm = {}
        pr_dst = lambda kk, c: wpr_sl(kk)
        prq = []
        pass2(WPRT, [(kk, 0) for kk in range(32)],
              pr_dst, lambda kk: [sp2_at[kk // 4], xh2_tok[kk // 4]],
              qpr_tok, 6, ring2, 0, IQP, prq)
        prq_done = ("vector", k.snap("vector"))

        oslots = [AT0[:, 0:FT], AT0[:, FT:2 * FT]]
        ob_free = [tmax(prq_done), tmax(prq_done)]

        sp3_done = {}
        ri1 = 32
        oi = 0
        for tt in range(NFT):
            tsl = slice(tt * FT, (tt + 1) * FT)
            efc_toks = []
            for g in range(8):
                bs = (g % 2) * 4
                at = tmax(*((fcm_ready[(kk, g)] for kk in range(8))
                            if tt == 0 else ()),
                          *(xh2_all if (tt == 0 and g == 0) else ()))
                k.dep(["tensor"], at=at or None)
                def mfc(t, g=g, bs=bs, tsl=tsl):
                    bank_wait(t, [bs + i for i in range(4)])
                    for gi in range(4):
                        m = g * 4 + gi
                        for kk in range(KT):
                            t.matmul(PB[bs + gi][:, 0:FT], wfc_sl(kk, m),
                                     XBF[kk][:, tsl], start=(kk == 0),
                                     stop=(kk == KT - 1))
                    t.mark()
                k.T(mfc)
                k.dep(["scalar"], ["tensor"])
                def efc(s, g=g, bs=bs):
                    for gi in range(4):
                        m = g * 4 + gi
                        s.activation(hm_sl(m), PB[bs + gi][:, 0:FT],
                                     AF.Gelu, bias=SM[:, 88 + m:89 + m],
                                     scale=QS[:, 3:4])
                    tok = ("scalar", s.mark())
                    for i in range(4):
                        banks[bs + i] = tok
                k.S(efc)
                efc_toks.append(("scalar", k.snap("scalar")))
            if tt == 0:
                pr_first = [True]
                for (kk, c, qv_tok) in prq:
                    merge_chunk(kk, c, pr_dst, BSP, APR, qv_tok,
                                pr_first, "v", bsp_tok, prm)
                prm_toks = list(prm.values())
            for g in range(2):
                bs = g * 4
                at = tmax(*(prm_toks if tt == 0 else []), *efc_toks)
                k.dep(["tensor"], at=at)
                def mpr(t, g=g, bs=bs):
                    bank_wait(t, [bs + i for i in range(4)])
                    for gi in range(4):
                        mo = g * 4 + gi
                        for kk in range(32):
                            t.matmul(PB[bs + gi][:, 0:FT],
                                     wpr_sl(kk)[:, mo * 128:(mo + 1) * 128],
                                     hm_sl(kk), start=(kk == 0),
                                     stop=(kk == 31))
                    t.mark()
                k.T(mpr)
                mpr_tok = ("tensor", k.snap("tensor"))
                for gi in range(4):
                    mo = g * 4 + gi
                    slot, tok = ring2.load(ri1,
                                           SP2[mo * 128:(mo + 1) * 128, tsl],
                                           width=FT, gates=[sp2_at[mo]])
                    osl = oslots[oi % 2]
                    k.dep(["vector"], at=tmax(mpr_tok, tok, ob_free[oi % 2]))
                    def epr(v, mo=mo, bs=bs, gi=gi, slot=slot, osl=osl):
                        v.tensor_scalar(osl, PB[bs + gi][:, 0:FT],
                                        QS[:, 9:10], SM[:, 32 + mo:33 + mo],
                                        OP.mult, OP.add)
                        v.chain()
                        v.tensor_add(osl, osl, slot)
                        banks[bs + gi] = ("vector", v.mark())
                    k.V(epr)
                    vtok = ("vector", k.snap("vector"))
                    ring2.release(ri1, vtok)
                    ri1 += 1
                    obch = "ob%d" % (oi % 2)
                    k.dep(["sync"], at=tmax(vtok))
                    def owr(y, mo=mo, tsl=tsl, osl=osl, obch=obch):
                        y.dma(obch, SP3[mo * 128:(mo + 1) * 128, tsl], osl)
                    k.Y(owr)
                    ob_free[oi % 2] = (obch, k.snap(obch))
                    sp3_done[(mo, tt)] = (obch, k.snap(obch))
                    oi += 1

        phaseB_T = ("tensor", k.snap("tensor"))
        obf = tmax(phaseB_T, ("ob0", k.snap("ob0")), ("ob1", k.snap("ob1")))

        gdma("wk", w8_out(Hf[0]), w8_in(WKT, 0), gate=[phaseB_T])
        gdma("wk", w8_out(Hf[1]), w8_in(WKT, 1))
        gdma("wv", w8_out(Hf[2]), w8_in(WVT, 0))
        gdma("wv", w8_out(Hf[3]), w8_in(WVT, 1))
        gdma("wr", w8_out(Hf[4]), w8_in(WRT, 0))
        gdma("wr", w8_out(Hf[5]), w8_in(WRT, 1))
        gdma("wo", w8_out(Hf[6]), w8_in(WOT, 0))
        gdma("wo", w8_out(Hf[7]), w8_in(WOT, 1))

        def wk_sl(kk, m):
            return bc16(Hf[kk // 4])[:, (kk % 4) * D + m * 128:
                                     (kk % 4) * D + (m + 1) * 128]

        def wv_sl(kk, m):
            return bc16(Hf[2 + kk // 4])[:, (kk % 4) * D + m * 128:
                                         (kk % 4) * D + (m + 1) * 128]

        def wr_sl(kk, m):
            return bc16(Hf[4 + kk // 4])[:, (kk % 4) * D + m * 128:
                                         (kk % 4) * D + (m + 1) * 128]

        def wo_sl(kk, m):
            return bc16(Hf[6 + kk // 4])[:, (kk % 4) * D + m * 128:
                                         (kk % 4) * D + (m + 1) * 128]

        qring.free = [obf, obf]
        qpf = tmax(("qp0", k.snap("qp0")), ("qp1", k.snap("qp1")),
                   ("vector", k.snap("vector")))
        ln3_rows = ln_stream_stats(
            lambda kk: SP3[kk * 128:(kk + 1) * 128, :],
            lambda kk, h2: tmax(*(sp3_done[(kk, tt)]
                                  for tt in range(h2 * (CH // FT),
                                                  (h2 + 1) * (CH // FT)))),
            EX[0], EX[1], qring,
            stage_free=(qpf, qpf))

        ALPHA = Xf[4][:, 0:TH]
        BETA = Xf[4][:, TH:2 * TH]
        EKBp = [Xf[5][:, 0:TH], Xf[5][:, TH:2 * TH]]
        EKUp = [Xf[7][:, 0:TH], Xf[7][:, TH:2 * TH]]
        EWB = SC[:, 0:TH]
        EKV = Xf[6][:, 0:TH]
        NUM = Xf[6][:, TH:2 * TH]
        DEN = SC[:, 1024:1024 + TH]
        TBF = bc16(SC)[:, 4096:4096 + TH]
        XHB = [EXf[2][:, 0:1024], EXf[2][:, 1024:2048]]

        def xk_sl(kk):
            return bc16(Xf[kk // 4])[:, (kk % 4) * TH:(kk % 4 + 1) * TH]

        def xv_sl(kk):
            return bc16(Xf[2 + kk // 4])[:, (kk % 4) * TH:(kk % 4 + 1) * TH]

        def xr_sl(kk):
            return XBFf[kk // 2][:, (kk % 2) * TH:(kk % 2 + 1) * TH]

        def rw_sl(th, kk):
            if th == 0:
                return XBFf[4 + kk // 2][:, (kk % 2) * TH:(kk % 2 + 1) * TH]
            return bc16(EXf[2 + kk // 4])[:, (kk % 4) * TH:(kk % 4 + 1) * TH]

        xhb_free = [None, None]
        kv_done = {}
        rh_done = {}
        rw_done = {}
        v1_done = {}
        rd_done = {}
        v2_done = {}
        g1_done = {}
        for th in range(2):
            hsl = slice(th * TH, (th + 1) * TH)
            for kk in range(KT):
                slot, tok = qring.load(kk % 2,
                                       SP3[kk * 128:(kk + 1) * 128, hsl],
                                       width=TH)
                xb = XHB[kk % 2][:, 0:TH]
                k.dep(["gpsimd"], at=tmax(tok, ln3_rows, xhb_free[kk % 2]))
                def bl_g(g, kk=kk, hsl=hsl, slot=slot, xb=xb):
                    g.tensor_sub(xb, slot, EX[0][:, hsl])
                    g.chain()
                    g.tensor_mul(xb, xb, EX[1][:, hsl])
                    g.mark()
                k.G(bl_g)
                at = tmax(tok, ("gpsimd", k.snap("gpsimd")),
                          phaseB_T if th == 0 else rh_done[(0, 7)])
                k.dep(["vector"], at=at)
                def bl_v(v, kk=kk, th=th, hsl=hsl, slot=slot, xb=xb):
                    v.tensor_sub(slot[:, 1:TH], xb[:, 1:TH], xb[:, 0:TH - 1])
                    if th == 0:
                        v.tensor_copy(slot[:, 0:1], xb[:, 0:1])
                    else:
                        v.tensor_sub(slot[:, 0:1], xb[:, 0:1],
                                     ABACK[:, kk:kk + 1])
                    v.chain()
                    for dst, col in ((xk_sl(kk), 64), (xv_sl(kk), 72),
                                     (xr_sl(kk), 80)):
                        v.scalar_tensor_tensor(dst[:, 1:TH], slot[:, 1:TH],
                                               SM[:, col + kk:col + kk + 1],
                                               xb[:, 0:TH - 1],
                                               OP.mult, OP.add)
                        if th == 0:
                            v.tensor_scalar(dst[:, 0:1], slot[:, 0:1],
                                            SM[:, col + kk:col + kk + 1],
                                            None, OP.mult)
                        else:
                            v.scalar_tensor_tensor(
                                dst[:, 0:1], slot[:, 0:1],
                                SM[:, col + kk:col + kk + 1],
                                ABACK[:, kk:kk + 1], OP.mult, OP.add)
                    if th == 0:
                        v.chain()
                        v.tensor_copy(ABACK[:, kk:kk + 1], xb[:, TH - 1:TH])
                    v.mark()
                k.V(bl_v)
                btok = ("vector", k.snap("vector"))
                qring.release(kk % 2, btok)
                xhb_free[kk % 2] = btok
            blends_done = ("vector", k.snap("vector"))

            for m in range(8):
                a = (m % 2) * 4
                b = 4 - a
                at = tmax(blends_done,
                          *((("gd_wk", 32), ("gd_wv", 32), ("gd_wr", 32))
                            if (th, m) == (0, 0) else ()))
                k.dep(["tensor"], at=at)
                def mkv(t, m=m, a=a):
                    bank_wait(t, [a + i for i in range(2 * NHH)])
                    for nh in range(NHH):
                        s2 = slice(nh * 512, (nh + 1) * 512)
                        for kk in range(KT):
                            t.matmul(PB[a + nh][:], wk_sl(kk, m),
                                     xk_sl(kk)[:, s2],
                                     start=(kk == 0), stop=(kk == KT - 1))
                        for kk in range(KT):
                            t.matmul(PB[a + NHH + nh][:], wv_sl(kk, m),
                                     xv_sl(kk)[:, s2],
                                     start=(kk == 0), stop=(kk == KT - 1))
                    kv_done[(th, m)] = ("tensor", t.mark())
                k.T(mkv)
                def mr(t, m=m, b=b):
                    bank_wait(t, [b + i for i in range(NHH)])
                    for nh in range(NHH):
                        s2 = slice(nh * 512, (nh + 1) * 512)
                        for kk in range(KT):
                            t.matmul(PB[b + nh][:], wr_sl(kk, m),
                                     xr_sl(kk)[:, s2],
                                     start=(kk == 0), stop=(kk == KT - 1))
                    rh_done[(th, m)] = ("tensor", t.mark())
                k.T(mr)
                p = m % 2
                mi = 2 * th + (m // 2) * 0
                def prev(dct, back):
                    mm2 = m - back
                    if mm2 >= 0:
                        return dct.get((th, mm2))
                    return dct.get((th - 1, 8 + mm2)) if th > 0 else None
                k.dep(["scalar"], at=tmax(kv_done[(th, m)], prev(v1_done, 2)))
                def ev1a(s, m=m, a=a, p=p):
                    for nh in range(NHH):
                        s2 = slice(nh * 512, (nh + 1) * 512)
                        s.activation(EKBp[p][:, s2], PB[a + nh][:], AF.Exp)
                    s.mark()
                k.S(ev1a)
                ek_tok = ("scalar", k.snap("scalar"))
                def ev1b(s, m=m, a=a, p=p):
                    for nh in range(NHH):
                        s2 = slice(nh * 512, (nh + 1) * 512)
                        s.activation(EKUp[p][:, s2], PB[a + nh][:], AF.Exp,
                                     bias=SM[:, 56 + m:57 + m])
                    tok = ("scalar", s.mark())
                    for i in range(NHH):
                        banks[a + i] = tok
                k.S(ev1b)
                eku_tok = ("scalar", k.snap("scalar"))
                k.dep(["scalar"], at=tmax(rh_done[(th, m)], prev(v1_done, 1)))
                def ev2(s, m=m, b=b):
                    for nh in range(NHH):
                        s2 = slice(nh * 512, (nh + 1) * 512)
                        s.activation(TBF[:, s2], PB[b + nh][:], AF.Exp,
                                     scale=-1.0)
                    tok = ("scalar", s.mark())
                    for i in range(NHH):
                        banks[b + i] = tok
                k.S(ev2)
                tb_tok = ("scalar", k.snap("scalar"))
                k.dep(["vector"], at=tmax(ek_tok, prev(rd_done, 1)))
                def v1a(v, m=m, a=a, p=p):
                    v.tensor_scalar(EWB[:], EX[0][:, 0:TH], 0.0,
                                    DRV[:, 32 + m:33 + m], OP.mult, OP.add)
                    for nh in range(NHH):
                        s2 = slice(nh * 512, (nh + 1) * 512)
                        v.tensor_mul(EKV[:, s2], EKBp[p][:, s2],
                                     PB[a + NHH + nh][:])
                        banks[a + NHH + nh] = ("vector", v.mark())
                k.V(v1a)
                k.dep(["vector"], at=tmax(tb_tok, prev(g1_done, 1)))
                def v1(v, m=m, th=th, a=a, p=p):
                    ia = 0.0 if th == 0 else CARA[:, m:m + 1]
                    v.tensor_tensor_scan(ALPHA, EWB, EKV, ia,
                                         OP.mult, OP.add)
                    ib = 0.0 if th == 0 else CARB[:, m:m + 1]
                    v.tensor_tensor_scan(BETA, EWB, EKBp[p], ib,
                                         OP.mult, OP.add)
                    v.chain()
                    if th == 0:
                        v.tensor_copy(CARA[:, m:m + 1], ALPHA[:, TH - 1:TH])
                        v.tensor_copy(CARB[:, m:m + 1], BETA[:, TH - 1:TH])
                    eu = DRV[:, 16 + m:17 + m]
                    v.scalar_tensor_tensor(NUM[:, 1:TH], EKV[:, 1:TH], eu,
                                           ALPHA[:, 0:TH - 1], OP.mult, OP.add)
                    if th == 0:
                        v.tensor_scalar(NUM[:, 0:1], EKV[:, 0:1], eu,
                                        None, OP.mult)
                    else:
                        v.scalar_tensor_tensor(NUM[:, 0:1], EKV[:, 0:1], eu,
                                               CARA[:, m:m + 1],
                                               OP.mult, OP.add)
                    v.wait("scalar", eku_tok[1])
                    v.tensor_add(DEN[:, 1:TH], EKUp[p][:, 1:TH],
                                 BETA[:, 0:TH - 1])
                    if th == 0:
                        v.tensor_copy(DEN[:, 0:1], EKUp[p][:, 0:1])
                    else:
                        v.tensor_add(DEN[:, 0:1], EKUp[p][:, 0:1],
                                     CARB[:, m:m + 1])
                    v.chain()
                    v.scalar_tensor_tensor(EKV[:], TBF, 1.0, DEN,
                                           OP.add, OP.mult)
                    v1_done[(th, m)] = ("vector", v.mark())
                k.V(v1)
                k.dep(["scalar"], at=tmax(v1_done[(th, m)]))
                def sdiv(s):
                    s.activation(EWB[:], EKV[:], AF.Ln)
                    s.chain()
                    s.activation(DEN[:], EWB[:], AF.Exp, scale=-1.0)
                    s.mark()
                k.S(sdiv)
                rd_tok = ("scalar", k.snap("scalar"))
                rd_done[(th, m)] = rd_tok
                at = tmax(rd_tok)
                if th == 1:
                    at = tmax(rd_tok, xhb_free[0], xhb_free[1],
                              ("at1", k.snap("at1")))
                k.dep(["vector"], at=at)
                def v2(v, th=th, m=m):
                    v.tensor_mul(rw_sl(th, m), NUM, DEN)
                    tok = ("vector", v.mark())
                    v2_done[(th, m)] = tok
                    g1_done[(th, m)] = tok
                    rw_done[(th, m)] = tok
                k.V(v2)

        mloop_v = ("vector", k.snap("vector"))
        mloop_s = ("scalar", k.snap("scalar"))
        mloop_g = ("gpsimd", k.snap("gpsimd"))

        x2ring = Ring(k, [SCR[i][:, 0:512] for i in range(4)],
                      ["rr0", "rr1", "rr2", "rr3"], issue_eng="sync",
                      free=[mloop_s, mloop_s, mloop_g, mloop_g])
        wo_oslots = [Xf[4][:, i * 512:(i + 1) * 512] for i in range(4)]
        wo_ofree = [mloop_v, mloop_v, mloop_v, mloop_v]

        ri2 = 0
        sp1_done = {}
        for th in range(2):
            for mo in range(8):
                a = (mo % 2) * 4
                at = tmax(*((("gd_wo", 32),) if (th, mo) == (0, 0) else ()),
                          *((rw_done[(th, kk)] for kk in range(8))
                            if mo == 0 else ()))
                k.dep(["tensor"], at=at or None)
                def mwo(t, th=th, mo=mo, a=a):
                    bank_wait(t, [a + i for i in range(NHH)])
                    for nh in range(NHH):
                        s2 = slice(nh * 512, (nh + 1) * 512)
                        for kk in range(KT):
                            t.matmul(PB[a + nh][:], wo_sl(kk, mo),
                                     rw_sl(th, kk)[:, s2],
                                     start=(kk == 0), stop=(kk == KT - 1))
                    t.mark()
                k.T(mwo)
                wo_tok = ("tensor", k.snap("tensor"))
                for nh in range(NHH):
                    s2a = slice(th * TH + nh * 512, th * TH + (nh + 1) * 512)
                    tts = range((th * TH + nh * 512) // FT,
                                (th * TH + (nh + 1) * 512) // FT)
                    slot, tok = x2ring.load(
                        ri2, SP3[mo * 128:(mo + 1) * 128, s2a],
                        gates=[tmax(*(sp3_done[(mo, tt)] for tt in tts))])
                    osl = wo_oslots[ri2 % 4]
                    k.dep(["vector"], at=tmax(wo_tok, tok,
                                              wo_ofree[ri2 % 4]))
                    def awo(v, a=a, nh=nh, osl=osl, slot=slot):
                        v.tensor_add(osl, PB[a + nh][:], slot)
                        banks[a + nh] = ("vector", v.mark())
                    k.V(awo)
                    vtok = ("vector", k.snap("vector"))
                    x2ring.release(ri2, vtok)
                    obch = "ob%d" % (ri2 % 4)
                    k.dep(["sync"], at=tmax(vtok))
                    def owr(y, mo=mo, s2a=s2a, osl=osl, obch=obch):
                        y.dma(obch, SP1[mo * 128:(mo + 1) * 128, s2a], osl)
                    k.Y(owr)
                    wo_ofree[ri2 % 4] = (obch, k.snap(obch))
                    sp1_done[(mo, th, nh)] = (obch, k.snap(obch))
                    ri2 += 1
            if th == 0:
                wo_th0 = ("tensor", k.snap("tensor"))

        phaseC_T = ("tensor", k.snap("tensor"))

        gdma("fw1", w8_out(Xf[0]), w8_in(FW1, 0), gate=[phaseC_T])
        gdma("fw1", w8_out(Xf[1]), w8_in(FW1, 1))
        gdma("fa1", w8_out(Xf[2]), w8_in(FA1, 0))
        gdma("fa1", w8_out(Xf[3]), w8_in(FA1, 1))
        gdma("fa2", w8_out(Xf[5]), w8_in(FA2, 0),
             gate=[mloop_v, mloop_g, ("vector", k.snap("vector"))])
        gdma("fa2", w8_out(Xf[6]), w8_in(FA2, 1))

        def fw1_sl(kk, m):
            return bc16(Xf[kk // 4])[:, (kk % 4) * D + m * 128:
                                     (kk % 4) * D + (m + 1) * 128]

        def fa1_sl(kk, m):
            return bc16(Xf[2 + kk // 4])[:, (kk % 4) * D + m * 128:
                                         (kk % 4) * D + (m + 1) * 128]

        def fa2_sl(kk, m):
            return bc16(Xf[5 + kk // 4])[:, (kk % 4) * D + m * 128:
                                         (kk % 4) * D + (m + 1) * 128]

        qring.free = [phaseC_T, phaseC_T]

        def ln4_gate(kk, h2):
            toks = []
            for j in range(CH // 512):
                pos = h2 * CH + j * 512
                toks.append(sp1_done[(kk, pos // TH, (pos % TH) // 512)])
            return tmax(*toks)

        XBF32 = _ap(XB0).bitcast(F32)
        ln4ring = Ring(k, [XBF32[:, 4096:5120], XBF32[:, 5120:6144],
                           XBF32[:, 6144:7168]],
                       ["l40", "l41", "l42"], issue_eng="sync",
                       free=[wo_th0, wo_th0, wo_th0])
        ln4_rows = ln_stream_stats(
            lambda kk: SP1[kk * 128:(kk + 1) * 128, :],
            ln4_gate, EX[0], EX[1], ln4ring, stage="sg",
            stage_free=(wo_th0, wo_th0),
            tiles=(XBF32[:, 7168:7680], XBF32[:, 7680:8192]))

        xh4_tok = {}
        li4 = 0
        for h2 in range(NCH):
            for kk in range(KT):
                slot, tok = qring.load(li4, SP1[kk * 128:(kk + 1) * 128,
                                               h2 * CH:(h2 + 1) * CH])
                eng = "vector" if li4 % 4 != 3 else "gpsimd"
                k.dep([eng], at=tmax(tok, ln4_rows))
                def xh4(e, kk=kk, h2=h2, slot=slot):
                    hs = slice(h2 * CH, (h2 + 1) * CH)
                    e.tensor_sub(slot, slot, EX[0][:, hs])
                    e.chain()
                    e.tensor_mul(XBF[kk][:, hs], slot, EX[1][:, hs])
                    e.mark()
                k.body(eng, xh4)
                tok2 = (eng, k.snap(eng))
                qring.release(li4, tok2)
                xh4_tok[(kk, h2)] = tok2
                li4 += 1

        def h1_sl(kk):
            return bc16(Hf[kk // 2])[:, (kk % 2) * T:(kk % 2) * T + T]

        def h2_sl(kk):
            return bc16(Hf[4 + kk // 2])[:, (kk % 2) * T:(kk % 2) * T + T]

        NJ = CH // 512
        em1_tok = {}
        for h2 in range(NCH):
            for m in range(8):
                bs = (m % 2) * 4 + (h2 % 2) * 2
                at = None
                if m == 0:
                    at = tmax(*(xh4_tok[(kk, h2)] for kk in range(KT)),
                              *((("gd_fw1", 32),) if h2 == 0 else ()))
                k.dep(["tensor"], at=at)
                def mm1(t, m=m, bs=bs, h2=h2):
                    bank_wait(t, [bs + j for j in range(NJ)])
                    for j in range(NJ):
                        n = h2 * NJ + j
                        sl = slice(n * NSZ, (n + 1) * NSZ)
                        for kk in range(KT):
                            t.matmul(PB[bs + j][:], fw1_sl(kk, m),
                                     XBF[kk][:, sl], start=(kk == 0),
                                     stop=(kk == KT - 1))
                    t.mark()
                k.T(mm1)
                mtok = ("tensor", k.snap("tensor"))
                k.dep(["vector"], at=tmax(mtok))
                def em1(v, m=m, bs=bs, h2=h2):
                    for j in range(NJ):
                        n = h2 * NJ + j
                        sl = slice(n * NSZ, (n + 1) * NSZ)
                        v.tensor_copy(h1_sl(m)[:, sl], PB[bs + j][:])
                    tok = ("vector", v.mark())
                    for j in range(NJ):
                        banks[bs + j] = tok
                    em1_tok[(h2, m)] = tok
                k.V(em1)

        em2_tok = {}
        for h2 in range(NCH):
            for m in range(8):
                bs = (m % 2) * 4 + (h2 % 2) * 2
                at = None
                if m == 0:
                    at = tmax(*(em1_tok[(h2, mm)] for mm in range(8)),
                              *((("gd_fa1", 32),) if h2 == 0 else ()))
                k.dep(["tensor"], at=at)
                def mm2(t, m=m, bs=bs, h2=h2):
                    bank_wait(t, [bs + j for j in range(NJ)])
                    for j in range(NJ):
                        n = h2 * NJ + j
                        sl = slice(n * NSZ, (n + 1) * NSZ)
                        for kk in range(KT):
                            t.matmul(PB[bs + j][:], fa1_sl(kk, m),
                                     h1_sl(kk)[:, sl], start=(kk == 0),
                                     stop=(kk == KT - 1))
                    t.mark()
                k.T(mm2)
                mtok = ("tensor", k.snap("tensor"))
                k.dep(["vector"], at=tmax(mtok))
                def em2(v, m=m, bs=bs, h2=h2):
                    for j in range(NJ):
                        n = h2 * NJ + j
                        sl = slice(n * NSZ, (n + 1) * NSZ)
                        v.tensor_copy(h2_sl(m)[:, sl], PB[bs + j][:])
                    tok = ("vector", v.mark())
                    for j in range(NJ):
                        banks[bs + j] = tok
                    em2_tok[(h2, m)] = tok
                k.V(em2)

        x3ring = Ring(k, [EXf[2][:, i * 512:(i + 1) * 512] for i in range(4)],
                      ["rr0", "rr1", "rr2", "rr3"], issue_eng="sync",
                      free=[phaseC_T] * 4)
        out_slots = [Xf[4][:, i * 512:(i + 1) * 512] for i in range(4)]
        out_free = [("ob%d" % i, k.snap("ob%d" % i)) for i in range(4)]
        ri3 = 0
        oi3 = 0
        for h2 in range(NCH):
            for m in range(8):
                bs = (m % 2) * 4 + (h2 % 2) * 2
                at = None
                if m == 0:
                    at = tmax(*(em2_tok[(h2, mm)] for mm in range(8)),
                              *((("gd_fa2", 32),) if h2 == 0 else ()))
                k.dep(["tensor"], at=at)
                def mm3(t, m=m, bs=bs, h2=h2):
                    bank_wait(t, [bs + j for j in range(NJ)])
                    for j in range(NJ):
                        n = h2 * NJ + j
                        sl = slice(n * NSZ, (n + 1) * NSZ)
                        for kk in range(KT):
                            t.matmul(PB[bs + j][:], fa2_sl(kk, m),
                                     h2_sl(kk)[:, sl], start=(kk == 0),
                                     stop=(kk == KT - 1))
                    t.mark()
                k.T(mm3)
                mtok = ("tensor", k.snap("tensor"))
                slots = []
                for j in range(NJ):
                    n = h2 * NJ + j
                    sl = slice(n * NSZ, (n + 1) * NSZ)
                    slots.append((ri3 + j, x3ring.load(
                        ri3 + j, SP1[m * 128:(m + 1) * 128, sl],
                        gates=[sp1_done[(m, n // NHH, n % NHH)]])))
                ri3 += NJ
                k.dep(["scalar"],
                      at=tmax(mtok, *(out_free[(oi3 + j) % 4]
                                      for j in range(NJ))))
                def em3(s, m=m, bs=bs, oi3=oi3):
                    for j in range(NJ):
                        s.activation(out_slots[(oi3 + j) % 4], PB[bs + j][:],
                                     AF.Tanh, bias=SM[:, 40 + m:41 + m],
                                     scale=1.0)
                    tok2 = ("scalar", s.mark())
                    for j in range(NJ):
                        banks[bs + j] = tok2
                k.S(em3)
                stok = ("scalar", k.snap("scalar"))
                for j in range(NJ):
                    n = h2 * NJ + j
                    sl = slice(n * NSZ, (n + 1) * NSZ)
                    ridx, (slot, tok) = slots[j]
                    osl = out_slots[(oi3 + j) % 4]
                    k.dep(["vector"], at=tmax(stok, tok))
                    def am3(v, osl=osl, slot=slot):
                        v.tensor_add(osl, osl, slot)
                        v.mark()
                    k.V(am3)
                    vtok = ("vector", k.snap("vector"))
                    x3ring.release(ridx, vtok)
                    obch = "ob%d" % ((oi3 + j) % 4)
                    k.dep(["sync"], at=tmax(vtok))
                    def owr(y, m=m, sl=sl, osl=osl, obch=obch):
                        y.dma(obch, YT[m * 128:(m + 1) * 128, sl], osl)
                    k.Y(owr)
                    out_free[(oi3 + j) % 4] = (obch, k.snap(obch))
                oi3 += NJ

        fin_vals = {name: c.val for name, c in k.ch.items()}
        def fin(y):
            for name, v in sorted(fin_vals.items()):
                if v > 0:
                    y.wait_ge(k.ch[name].sem, v)
        k.Y(fin)

    return nc


def _prep_inputs(inputs, b, T=2048):
    c = np.ascontiguousarray
    f = np.float32

    def col(v, n):
        return np.asarray(v, f).reshape(n, 128).T

    sm = np.zeros((128, 120), f)
    sm[:, 0:8] = col(inputs["L0_attn_bz"], 8)
    sm[:, 8:16] = col(inputs["L0_attn_bh"], 8)
    sm[:, 16:24] = col(inputs["L0_attn_gamma"], 8)
    sm[:, 24:32] = col(inputs["L0_attn_beta"], 8)
    sm[:, 32:40] = col(inputs["L0_ffn_b_proj"], 8)
    sm[:, 40:48] = col(inputs["L1_ffn_bias"], 8)
    sm[:, 48:56] = col(inputs["L1_attn_time_decay"], 8)
    sm[:, 56:64] = col(inputs["L1_attn_time_first"], 8)
    sm[:, 64:72] = col(np.asarray(inputs["L1_attn_time_mix_k"]).reshape(-1), 8)
    sm[:, 72:80] = col(np.asarray(inputs["L1_attn_time_mix_v"]).reshape(-1), 8)
    sm[:, 80:88] = col(np.asarray(inputs["L1_attn_time_mix_r"]).reshape(-1), 8)
    sm[:, 88:120] = col(inputs["L0_ffn_b_fc"], 32)

    m = dict(
        XT=c(np.asarray(inputs["L0_attn_x"], f)[b].T[:, 0:T]),
        WZ=c(np.asarray(inputs["L0_attn_Wz"], f)),
        WH=c(np.asarray(inputs["L0_attn_Wh"], f)),
        WFCT=c(np.asarray(inputs["L0_ffn_W_fc"], f).T),
        WPRT=c(np.asarray(inputs["L0_ffn_W_proj"], f).T),
        AFC=c(np.asarray(inputs["L0_ffn_A_fc"], f)),
        BFCT=c(np.asarray(inputs["L0_ffn_B_fc"], f).T),
        APR=c(np.asarray(inputs["L0_ffn_A_proj"], f)),
        BPRT=c(np.asarray(inputs["L0_ffn_B_proj"], f).T),
        WKT=c(np.asarray(inputs["L1_attn_Wk"], f).T),
        WVT=c(np.asarray(inputs["L1_attn_Wv"], f).T),
        WRT=c(np.asarray(inputs["L1_attn_Wr"], f).T),
        WOT=c(np.asarray(inputs["L1_attn_Wo"], f).T),
        FW1=c(np.asarray(inputs["L1_ffn_weight"], f)),
        FA1=c(np.asarray(inputs["L1_ffn_aux_weight1"], f)),
        FA2=c(np.asarray(inputs["L1_ffn_aux_weight2"], f)),
        SM=sm,
        IDNI=np.eye(128, dtype=f),
    )
    return m


def kernel(**inputs):
    nc = build(T=2048, debug=False)
    in_maps = [_prep_inputs(inputs, b) for b in range(8)]
    res = run_bass_kernel_spmd(nc, in_maps, core_ids=list(range(8)))
    out = np.stack([np.asarray(r["YT"], np.float32).T for r in res.results],
                   axis=0)
    return out
